# version 05
# baseline (speedup 1.0000x reference)
import math
from contextlib import ExitStack

import numpy as np
import ml_dtypes
import concourse.bass as bass
import concourse.mybir as mybir
from concourse.bass import IndirectOffsetOnAxis
from concourse.bass_utils import run_bass_kernel_spmd

F32 = mybir.dt.float32
BF16 = mybir.dt.bfloat16
I32 = mybir.dt.int32
U32 = mybir.dt.uint32
AF = mybir.ActivationFunctionType
AL = mybir.AluOpType

B, S, D = 8, 2048, 1024
NT = B * S
E_EXP = 8
K_SEL = NT // E_EXP
CAP = 384
F_MOE = 4096
H_FF1 = 3072
EPS = 1e-5
BISECT = 34
NH0 = 16
NH1 = 16
OOB = 1.0e9

DT_NBYTES = {F32: 4, BF16: 2, I32: 4, U32: 4}


ENGINES = ("sync", "scalar", "vector", "tensor", "gpsimd")


class Sem:
    def __init__(self, handle):
        self.h = handle
        self.val = 0


class KB:

    def __init__(self, nc, stack):
        self.nc = nc
        self.stack = stack
        self.ops = {e: [] for e in ENGINES}
        self._sems = {}
        self._bind = {}
        self.d = {}
        self.phase_stack = None

    MIXED = {"p0_lo", "p0_var", "p0_ws", "la_exp", "la_att", "lc_var", "rs_bis",
             "la_pe", "mf_yev"}

    def _dedicated(self, key):
        if key not in self._sems:
            h = self.stack.enter_context(self.nc.semaphore(key))
            self._sems[key] = Sem(h)
        return self._sems[key]

    def sem(self, name, producer=None):
        if name in self.MIXED:
            return self._dedicated(name)
        if name in self._bind:
            s, prod = self._bind[name]
            if producer is not None and prod is None:
                self._bind[name] = (s, producer)
            elif producer is not None and prod != producer:
                raise AssertionError(
                    f"sem {name} has two producers ({prod}, {producer}); "
                    f"add it to KB.MIXED")
            return s
        if producer is None or producer.startswith("dma:"):
            s = self._dedicated(name)
            self._bind[name] = (s, producer)
            return s
        prefix = name.split("_")[0]
        s = self._dedicated(f"&{prefix}_{producer}")
        self._bind[name] = (s, producer)
        return s

    def dma(self, q, out, in_, sem, **kw):
        s = self.sem(sem, producer="dma:" + q)
        s.val += 16
        tgt = s.val

        def fn(eng, out=out, in_=in_, h=s.h, kw=kw):
            eng.dma_start(out=out, in_=in_, **kw).then_inc(h, 16)

        self.ops[q].append(fn)
        return tgt

    def dmat(self, q, out, in_, sem):
        s = self.sem(sem, producer="dma:" + q)
        s.val += 16
        tgt = s.val

        def fn(eng, out=out, in_=in_, h=s.h):
            eng.dma_start_transpose(out, in_)
        self.ops[q].append(lambda eng, out=out, in_=in_, h=s.h:
                           eng.dma_start_transpose(out, in_).then_inc(h, 16))
        return tgt

    def idma(self, out, out_off, in_, in_off, sem, **kw):
        s = self.sem(sem, producer="dma:gpsimd")
        s.val += 16
        tgt = s.val

        def fn(eng, out=out, oo=out_off, in_=in_, io=in_off, h=s.h, kw=kw):
            eng.indirect_dma_start(out, oo, in_, io, oob_is_err=False, **kw).then_inc(h, 16)

        self.ops["gpsimd"].append(fn)
        return tgt

    def op(self, engq, fn, sem=None, inc=1):
        dr = engq in ("scalar", "vector", "gpsimd")
        if sem is not None:
            s = self.sem(sem, producer=engq)
            s.val += inc
            tgt = s.val

            def wrapped(eng, fn=fn, h=s.h, inc=inc, dr=dr):
                if dr:
                    eng.drain()
                ins = fn(eng)
                ins.then_inc(h, inc)

            self.ops[engq].append(wrapped)
            return tgt

        def plain(eng, fn=fn, dr=dr):
            if dr:
                eng.drain()
            fn(eng)

        self.ops[engq].append(plain)
        return None

    def wait(self, engq, sem, val=None):
        s = self.sem(sem)
        v = s.val if val is None else val
        self.ops[engq].append(lambda eng, h=s.h, v=v: eng.wait_ge(h, v))
        return v

    def barrier(self, name, engines=ENGINES):
        s = self.sem(name + "_bar")
        n = len(engines)
        base = s.val
        for e in engines:
            def fn(eng, h=s.h):
                eng.nop().then_inc(h, 1)
            self.ops[e].append(fn)
        s.val += n
        for e in engines:
            self.ops[e].append(lambda eng, h=s.h, v=base + n: eng.wait_ge(h, v))

    def dram(self, name, shape, dtype, kind="Internal"):
        t = self.nc.dram_tensor(name, list(shape), dtype, kind=kind)
        self.d[name] = t
        return t

    def phase_begin(self):
        assert self.phase_stack is None
        self.phase_stack = ExitStack()

    def phase_end(self):
        self.phase_stack.close()
        self.phase_stack = None

    def sbuf(self, name, shape, dtype):
        st = self.phase_stack if self.phase_stack is not None else self.stack
        return st.enter_context(self.nc.sbuf_tensor(name, list(shape), dtype))

    def psum(self, name, shape, dtype=F32):
        st = self.phase_stack if self.phase_stack is not None else self.stack
        return st.enter_context(self.nc.psum_tensor(name, list(shape), dtype))

    def finalize_block(self):
        nc = self.nc
        ops = self.ops
        with nc.Block() as block:
            @block.sync
            def _(eng):
                for f in ops["sync"]:
                    f(eng)

            @block.scalar
            def _(eng):
                for f in ops["scalar"]:
                    f(eng)

            @block.vector
            def _(eng):
                for f in ops["vector"]:
                    f(eng)

            @block.tensor
            def _(eng):
                for f in ops["tensor"]:
                    f(eng)

            @block.gpsimd
            def _(eng):
                for i, f in enumerate(ops["gpsimd"]):
                    try:
                        f(eng)
                    except Exception:
                        import sys
                        cv = getattr(f, "__defaults__", None) or ()
                        print(f"gpsimd op {i}/{len(ops['gpsimd'])} failed: "
                              f"{getattr(f, '__qualname__', f)} defaults={cv}",
                              file=sys.stderr)
                        raise

    def fence(self, *sems):
        for s in sems:
            for eng in ("sync", "scalar", "vector", "gpsimd"):
                self.wait(eng, s)

    def sbuf_global(self, name, shape, dtype):
        t = self.stack.enter_context(self.nc.sbuf_tensor(name, list(shape), dtype))
        if not hasattr(self, "shared"):
            self.shared = {}
        self.shared[name] = t
        return t


def mm_accum(kb, ps_ap, pairs, sem=None, inc=1, tile_position=None):
    n = len(pairs)
    ret = None
    for i, (lh, rh) in enumerate(pairs):
        last = i == n - 1

        def fn(eng, ps=ps_ap, lh=lh, rh=rh, st=(i == 0), sp=last, tp=tile_position):
            return eng.matmul(ps, lh, rh, start=st, stop=sp, tile_position=tp)

        if last and sem is not None:
            ret = kb.op("tensor", fn, sem=sem, inc=inc)
        else:
            kb.op("tensor", fn)
    return ret


def p0_prep(kb):
    x = kb.d["x"].ap()
    xhi = kb.d["xhi"].ap()
    xlo = kb.d["xlo"].ap()
    mu_d = kb.d["mu"].ap()
    rstd_d = kb.d["rstd"].ap()
    xnb = kb.d["xnb"].ap()
    xnf = kb.d["xnf"].ap()
    ws = kb.d["ws"].ap()

    wsx = [kb.sbuf_global(f"wsx{c}", [128, 16], BF16) for c in range(8)]
    ones128f = kb.sbuf_global("ones128f", [128, 1], F32)
    epst = kb.sbuf_global("epst", [128, 1], F32)
    kb.op("vector", lambda e, o=ones128f: e.memset(o[:, :], 1.0))
    kb.op("vector", lambda e, o=epst: e.memset(o[:, :], EPS), sem="g_ones", inc=2)

    kb.phase_begin()
    NB = 4
    xt = [kb.sbuf(f"p0_x{i}", [128, 1024], F32) for i in range(NB)]
    hib = [kb.sbuf(f"p0_h{i}", [128, 1024], BF16) for i in range(NB)]
    lob = [kb.sbuf(f"p0_l{i}", [128, 1024], BF16) for i in range(NB)]
    xnbt = [kb.sbuf(f"p0_xnb{i}", [128, 1024], BF16) for i in range(NB)]
    xnft = [kb.sbuf(f"p0_xnf{i}", [128, 1024], F32) for i in range(NB)]
    junk_v = kb.sbuf("p0_jv", [128, 1024], BF16)
    junk_s = kb.sbuf("p0_js", [128, 1024], F32)
    s1 = [kb.sbuf(f"p0_s1{i}", [128, 1], F32) for i in range(NB)]
    s2 = [kb.sbuf(f"p0_s2{i}", [128, 1], F32) for i in range(NB)]
    mu = [kb.sbuf(f"p0_mu{i}", [128, 1], F32) for i in range(NB)]
    rst = [kb.sbuf(f"p0_rs{i}", [128, 1], F32) for i in range(NB)]
    tv = [kb.sbuf(f"p0_tv{i}", [128, 2], F32) for i in range(NB)]
    wsf = kb.sbuf("p0_wsf", [128, 8], F32)
    wsh = kb.sbuf("p0_wsh", [128, 8], F32)

    for c in range(8):
        v = kb.dma("sync", wsf[:, :], ws[c * 128:(c + 1) * 128, :], "p0_wsld")
        kb.wait("scalar", "p0_wsld", v)
        kb.op("scalar", lambda e, o=wsx[c], i=wsf: e.copy(o[:, 0:8], i[:, :]))
        kb.op("scalar", lambda e, o=wsh, i=wsx[c]: e.copy(o[:, :], i[:, 0:8]),
              sem="p0_ws")
        wv = kb.sem("p0_ws").val
        kb.wait("vector", "p0_ws", wv)
        kb.op("vector", lambda e, o=wsx[c], i0=wsf, i1=wsh:
              e.tensor_tensor(o[:, 8:16], i0[:, :], i1[:, :], op=AL.subtract),
              sem="p0_ws")
        kb.wait("sync", "p0_ws")

    ld_v = {}
    hi_v = {}
    lo_v = {}
    var_v = {}
    rstd_v = {}
    st_v = {}

    def stage_load(T):
        b = T % NB
        if T >= NB:
            kb.wait("sync", "p0_st", st_v.get(T - NB, 0))
            kb.wait("sync", "p0_st2", st2_v.get(T - NB, 0))
            kb.wait("sync", "p0_sq", sq_v.get(T - NB, 0))
        ld_v[T] = kb.dma("sync", xt[b][:, :], x[T * 128:(T + 1) * 128, :], "p0_ld")

    st2_v = {}

    sq_v = {}

    def stage_main(T):
        b = T % NB
        if T >= NB:
            kb.wait("scalar", "p0_st2", st2_v.get(T - NB, 0))
            kb.wait("scalar", "p0_var", var_v.get(T - NB, 0))
            kb.wait("gpsimd", "p0_st", st_v.get(T - NB, 0))
            kb.wait("vector", "p0_st", st_v.get(T - NB, 0))
            kb.wait("vector", "p0_st2", st2_v.get(T - NB, 0))
        kb.wait("vector", "p0_ld", ld_v[T])
        kb.op("vector", lambda e, o=hib[b], i=xt[b], a=s1[b]:
              e.tensor_scalar(o[:, :], i[:, :], 1.0, 0.0, op0=AL.mult, op1=AL.add,
                              accum_out=a[:, :]), sem="p0_hi")
        hi_v[T] = kb.sem("p0_hi").val
        kb.wait("scalar", "p0_ld", ld_v[T])
        if T == 0:
            kb.wait("scalar", "g_ones", 2)
        kb.op("scalar", lambda e, i=xt[b], a=s2[b], j=junk_s:
              e.activation(j[:, :], i[:, :], AF.Square, accum_out=a[:, :]),
              sem="p0_sq")
        sq_v[T] = kb.sem("p0_sq").val
        kb.wait("gpsimd", "p0_ld", ld_v[T])
        kb.wait("gpsimd", "p0_hi", hi_v[T])
        kb.op("gpsimd", lambda e, o=lob[b], i0=xt[b], i1=hib[b]:
              e.tensor_tensor(o[:, :], i0[:, :], i1[:, :], op=AL.subtract),
              sem="p0_lo")
        lo_v[T] = kb.sem("p0_lo").val
        kb.wait("vector", "p0_sq", sq_v[T])
        kb.op("vector", lambda e, s=s1[b], m=mu[b]:
              e.tensor_scalar(m[:, :], s[:, :], 1.0 / D, None, op0=AL.mult))
        kb.op("vector", lambda e, s=s2[b], t=tv[b]:
              e.tensor_scalar(t[:, 0:1], s[:, :], 1.0 / D, None, op0=AL.mult))
        kb.op("vector", lambda e, m=mu[b], t=tv[b]:
              e.tensor_tensor(t[:, 1:2], m[:, :], m[:, :], op=AL.mult))
        kb.op("vector", lambda e, t=tv[b]:
              e.tensor_tensor(t[:, 0:1], t[:, 0:1], t[:, 1:2], op=AL.subtract),
              sem="p0_var")
        var_v[T] = kb.sem("p0_var").val

    def stage_rstd(T):
        b = T % NB
        kb.wait("scalar", "p0_var", var_v[T])
        kb.op("scalar", lambda e, t=tv[b], ep=epst:
              e.activation(t[:, 1:2], t[:, 0:1], AF.Ln, bias=ep[:, :]))
        kb.op("scalar", lambda e, t=tv[b], r=rst[b]:
              e.activation(r[:, :], t[:, 1:2], AF.Exp, scale=-0.5), sem="p0_rst")
        rstd_v[T] = kb.sem("p0_rst").val

    def stage_out(T):
        b = T % NB
        mine = T < 16
        if mine:
            kb.wait("vector", "p0_rst", rstd_v[T])
            kb.op("vector", lambda e, i=xt[b], m=mu[b], r=rst[b], o=xnbt[b]:
                  e.tensor_scalar(o[:, :], i[:, :], m[:, :], r[:, :],
                                  op0=AL.subtract, op1=AL.mult))
            kb.op("vector", lambda e, i=xt[b], m=mu[b], r=rst[b], o=xnft[b]:
                  e.tensor_scalar(o[:, :], i[:, :], m[:, :], r[:, :],
                                  op0=AL.subtract, op1=AL.mult), sem="p0_xn")
        kb.wait("gpsimd", "p0_hi", hi_v[T])
        kb.dma("gpsimd", xhi[T * 128:(T + 1) * 128, :], hib[b][:, :], "p0_st")
        kb.dma("gpsimd", xlo[T * 128:(T + 1) * 128, :], lob[b][:, :], "p0_st")
        st_v[T] = kb.sem("p0_st").val
        kb.wait("sync", "p0_rst", rstd_v[T])
        kb.dma("sync", mu_d[T * 128:(T + 1) * 128, :], mu[b][:, :], "p0_st2")
        kb.dma("sync", rstd_d[T * 128:(T + 1) * 128, :], rst[b][:, :], "p0_st2")
        if mine:
            kb.wait("sync", "p0_xn")
            kb.dma("sync", xnb[T * 128:(T + 1) * 128, :], xnbt[b][:, :], "p0_st2")
            kb.dma("sync", xnf[T * 128:(T + 1) * 128, :], xnft[b][:, :], "p0_st2")
        st2_v[T] = kb.sem("p0_st2").val

    stage_load(0)
    stage_load(1)
    stage_main(0)
    for T in range(1, 128):
        if T + 1 < 128:
            stage_load(T + 1)
        stage_main(T)
        stage_rstd(T - 1)
        stage_out(T - 1)
    stage_rstd(127)
    stage_out(127)

    kb.wait("vector", "p0_sq", sq_v[127])
    kb.op("vector", lambda e_, o=junk_v: e_.memset(o[:, :], 0.0), sem="p0_z")
    kb.wait("gpsimd", "p0_z")
    kb.dma("gpsimd", xnf[S:S + 1, :], junk_v[0:1, :], "p0_zst")
    kb.phase_end()


def routing_logits(kb):
    kb.sem("rt_log", producer="tensor")
    kb.fence("p0_st", "p0_st2", "p0_zst")
    xhi = kb.d["xhi"].ap()
    xlo = kb.d["xlo"].ap()
    mu_d = kb.d["mu"].ap()
    rstd_d = kb.d["rstd"].ap()
    logits_d = kb.d["logits"].ap()
    ws = kb.d["ws"].ap()
    comb17_d = kb.d["comb17"].ap()
    bs_d = kb.d["bs"].ap()

    kb.phase_begin()
    for eng in ("sync", "scalar"):
        kb.wait(eng, "p0_st")
        kb.wait(eng, "p0_st2")
    hiT = [kb.sbuf(f"rt_hiT{c}", [128, 2048], BF16) for c in range(8)]
    loT = [kb.sbuf(f"rt_loT{c}", [128, 2048], BF16) for c in range(8)]
    stack17 = kb.sbuf("rt_stack", [17, 2048], F32)
    comb = kb.sbuf("rt_comb", [17, 8], F32)
    csneg = kb.sbuf("rt_csneg", [1, 8], F32)
    rrow = kb.sbuf("rt_rrow", [1, 2048], F32)
    bc_sb = kb.sbuf("rt_bc", [8, 2048], F32)
    tmul = kb.sbuf("rt_tmul", [8, 2048], F32)
    logt = kb.sbuf("rt_logt", [8, 2048], F32)
    bs_sb = kb.sbuf("rt_bs", [8, 1], F32)
    ones1 = kb.sbuf("rt_ones1", [1, 8], F32)
    wsf32 = kb.sbuf("rt_wsf", [128, 8], F32)
    rawp = kb.psum("rt_rawp", [16, 512])
    logp = kb.psum("rt_logp", [8, 512])
    bcp = kb.psum("rt_bcp", [8, 512])
    csp = kb.psum("rt_csp", [1, 8])
    wsx = [kb.shared[f"wsx{c}"] for c in range(8)]
    ones128f = kb.shared["ones128f"]

    kb.dma("sync", comb[0:17, :], comb17_d[:, :], "rt_ld")
    kb.dma("sync", bs_sb[:, :], bs_d[:, :], "rt_ld")
    ldv = kb.sem("rt_ld").val
    kb.op("vector", lambda e, o=ones1: e.memset(o[:, :], 1.0), sem="rt_ones")
    onev = kb.sem("rt_ones").val

    for c in range(8):
        vv = kb.dma("sync", wsf32[:, :], ws[c * 128:(c + 1) * 128, :], "rt_wsld")
        kb.wait("tensor", "rt_wsld", vv)
        if c == 0:
            kb.wait("tensor", "g_ones", 1)
        kb.op("tensor", lambda e, ps=csp, lh=ones128f, rh=wsf32, st=(c == 0), sp=(c == 7):
              e.matmul(ps[:, :], lh[:, 0:1], rh[:, :], start=st, stop=sp),
              sem="rt_cs" if c == 7 else "rt_wsmm")
        if c < 7:
            kb.wait("sync", "rt_wsmm")
    csv = kb.sem("rt_cs").val
    kb.wait("vector", "rt_cs", csv)
    kb.op("vector", lambda e, o=csneg, i=csp:
          e.tensor_scalar(o[:, :], i[:, :], -1.0, None, op0=AL.mult), sem="rt_csn")
    csnv = kb.sem("rt_csn").val
    kb.wait("sync", "rt_csn", csnv)
    kb.wait("sync", "rt_ld", ldv)
    kb.dma("sync", comb[16:17, :], csneg[:, :], "rt_combst")
    combv = kb.sem("rt_combst").val

    for r in range(8):
        tv = None
        for c in range(8):
            kb.dmat("scalar", hiT[c][:, :],
                    xhi[r * 2048:(r + 1) * 2048, c * 128:(c + 1) * 128], "rt_tp")
            tv = kb.dmat("scalar", loT[c][:, :],
                         xlo[r * 2048:(r + 1) * 2048, c * 128:(c + 1) * 128], "rt_tp")
        kb.wait("sync", "rt_log")
        mv = kb.dma("sync", stack17[16:17, :],
                    mu_d[r * 2048:(r + 1) * 2048, :].rearrange("n one -> one n"),
                    "rt_murow")
        rv = kb.dma("sync", rrow[:, :],
                    rstd_d[r * 2048:(r + 1) * 2048, :].rearrange("n one -> one n"),
                    "rt_murow")
        kb.wait("tensor", "rt_tp", tv)
        if r == 0:
            kb.wait("tensor", "rt_combst", combv)
            kb.wait("tensor", "g_ones", 1)
            kb.wait("tensor", "rt_ones", onev)
        kb.wait("tensor", "rt_murow", rv)
        for n in range(4):
            prs = [(wsx[c][:, :], hiT[c][:, n * 512:(n + 1) * 512]) for c in range(8)]
            prs += [(wsx[c][:, :], loT[c][:, n * 512:(n + 1) * 512]) for c in range(8)]
            rawv = mm_accum(kb, rawp[:, :], prs, sem="rt_raw")
            kb.wait("vector", "rt_raw", rawv)
            kb.op("vector", lambda e, o=stack17, i=rawp, n=n:
                  e.tensor_copy(o[0:16, n * 512:(n + 1) * 512], i[:, :]), sem="rt_evac")
            ev = kb.sem("rt_evac").val
            kb.wait("tensor", "rt_evac", ev)
            kb.op("tensor", lambda e, ps=logp, lh=comb, rh=stack17, n=n:
                  e.matmul(ps[:, :], lh[:, :], rh[:, n * 512:(n + 1) * 512],
                           start=True, stop=True))
            kb.op("tensor", lambda e, ps=bcp, lh=ones1, rh=rrow, n=n:
                  e.matmul(ps[:, :], lh[:, :], rh[:, n * 512:(n + 1) * 512],
                           start=True, stop=True), sem="rt_log")
            lgv = kb.sem("rt_log").val
            kb.wait("vector", "rt_log", lgv)
            kb.op("vector", lambda e, o=bc_sb, i=bcp, n=n:
                  e.tensor_copy(o[:, n * 512:(n + 1) * 512], i[:, :]))
            kb.op("vector", lambda e, o=tmul, i0=logp, i1=bc_sb, n=n:
                  e.tensor_tensor(o[:, n * 512:(n + 1) * 512], i0[:, :],
                                  i1[:, n * 512:(n + 1) * 512], op=AL.mult))
            kb.op("vector", lambda e, o=logt, i=tmul, s=bs_sb, n=n:
                  e.tensor_scalar(o[:, n * 512:(n + 1) * 512],
                                  i[:, n * 512:(n + 1) * 512], s[:, :], None,
                                  op0=AL.add), sem="rt_fix")
        fv = kb.sem("rt_fix").val
        kb.wait("sync", "rt_fix", fv)
        stv = kb.dma("sync", logits_d[r, :, :], logt[:, :], "rt_lst")
        kb.wait("scalar", "rt_raw")
        kb.wait("vector", "rt_lst", stv)

    kb.phase_end()


def routing_select(kb):
    logits_d = kb.d["logits"].ap()
    probs_d = kb.d["probs"].ap()
    pairs_d = kb.d["pairs"].ap()
    s16_d = kb.d["s16"].ap()
    r16_d = kb.d["r16"].ap()
    rep16_d = kb.d["rep16"].ap()
    grp16_d = kb.d["grp16"].ap()
    sel8_d = kb.d["sel8"].ap()
    lt128_d = kb.d["lt128"].ap()
    jj16_d = kb.d["jj16"].ap()
    iota16_d = kb.d["iota16"].ap()
    iota1_d = kb.d["iota1"].ap()

    kb.fence("rt_lst")
    kb.phase_begin()
    logL = kb.sbuf("rs_logL", [128, 1024], F32)
    expL = kb.sbuf("rs_expL", [128, 1024], F32)
    probL = kb.sbuf("rs_probL", [128, 1024], F32)
    sigr = kb.sbuf("rs_sigr", [16, 1024], F32)
    s16 = kb.sbuf("rs_s16", [128, 16], F32)
    r16 = kb.sbuf("rs_r16", [16, 128], F32)
    rep16 = kb.sbuf("rs_rep16", [8, 128], F32)
    grp16 = kb.sbuf("rs_grp16", [128, 8], F32)
    sel8 = kb.sbuf("rs_sel8", [8, 1024], F32)
    lt128 = kb.sbuf("rs_lt", [128, 128], F32)
    jj16 = kb.sbuf("rs_jj", [128, 16], F32)
    iota16 = kb.sbuf("rs_iota", [128, 1], F32)
    iota1 = kb.sbuf("rs_iota1", [128, 1], F32)
    fillt = kb.sbuf("rs_fill", [128, 48], F32)
    dumpc = kb.sbuf("rs_dump", [128, 1], F32)
    ojj = kb.sbuf("rs_ojj", [128, 16], F32)
    okv = kb.sbuf("rs_okv", [128, 16], F32)
    okvi = kb.sbuf("rs_okvi", [128, 16], U32)
    lohi = kb.sbuf("rs_lohi", [8, 2], F32)
    mid = kb.sbuf("rs_mid", [8, 1], F32)
    pred = kb.sbuf("rs_pred", [8, 1], U32)
    cnt8s = kb.sbuf("rs_cnt8", [8, 1], F32)
    trep = kb.sbuf("rs_trep", [128, 1], F32)
    cntp_s = kb.sbuf("rs_cntp", [128, 1], F32)
    junk = kb.sbuf("rs_junk", [128, 1024], F32)
    sigp = kb.psum("rs_sigp", [16, 1024])
    repp = kb.psum("rs_repp", [128, 1024])
    trp = kb.psum("rs_trp", [128, 1])
    c8p = kb.psum("rs_c8p", [8, 1])
    mine_l = [kb.sbuf(f"rs_mine{i}", [128, 16], F32) for i in range(E_EXP)]
    work_l = [kb.sbuf(f"rs_work{i}", [128, 16], F32) for i in range(E_EXP)]
    vals8 = kb.sbuf("rs_vals", [128, 8], F32)
    idx8 = kb.sbuf("rs_idx", [128, 8], U32)
    idxf = kb.sbuf("rs_idxf", [128, 8], F32)
    stg_l = [kb.sbuf(f"rs_stg{i}", [128, 32], F32) for i in range(E_EXP)]
    mask16 = kb.sbuf("rs_m16", [128, 16], F32)
    cpart = kb.sbuf("rs_cp", [128, 1], F32)
    offs = kb.sbuf("rs_offs", [128, 1], F32)
    tse = kb.sbuf("rs_tse", [128, 1], F32)
    posf = kb.sbuf("rs_posf", [128, 16], F32)
    posi_l = [kb.sbuf(f"rs_posi{i}", [128, 16], I32) for i in range(E_EXP)]
    offp = kb.psum("rs_offp", [128, 1])
    tsep = kb.psum("rs_tsep", [128, 1])

    for dst, src in ((s16, s16_d), (r16, r16_d), (rep16, rep16_d),
                     (grp16, grp16_d), (sel8, sel8_d),
                     (lt128, lt128_d), (jj16, jj16_d), (iota16, iota16_d),
                     (iota1, iota1_d)):
        kb.dma("sync", dst[:, :], src[:, :], "rs_cld")
    cldv = kb.sem("rs_cld").val
    kb.wait("vector", "rs_cld", cldv)
    kb.wait("tensor", "rs_cld", cldv)
    kb.op("vector", lambda e, o=dumpc, i=iota1:
          e.tensor_scalar(o[:, :], i[:, :], float(E_EXP * CAP), None, op0=AL.add))
    kb.op("vector", lambda e, o=fillt: e.memset(o[:, :], 0.0))
    kb.op("vector", lambda e, o=fillt: e.memset(o[:, 0:48:2], 2048.0), sem="rs_z")
    kb.wait("sync", "rs_z")
    kb.dma("sync", pairs_d[0:E_EXP * CAP, :].rearrange("(p i) two -> p (i two)", p=128),
           fillt[:, :], "rs_pf")
    pfv = kb.sem("rs_pf").val

    kb.wait("sync", "rt_lst")
    lsrc = logits_d.rearrange("r e (j2 i) -> e r j2 i", j2=2)
    for ee in range(8):
        for rr in range(8):
            lv = kb.dma("sync", logL[ee * 16 + rr * 2: ee * 16 + rr * 2 + 2, :],
                        lsrc[ee, rr], "rs_lld")
    kb.wait("scalar", "rs_lld", lv)
    kb.op("scalar", lambda e, o=expL, i=logL:
          e.activation(o[:, :], i[:, :], AF.Exp), sem="rs_exp")
    ev = kb.sem("rs_exp").val
    kb.wait("tensor", "rs_exp", ev)
    kb.op("tensor", lambda e, ps=sigp, lh=s16, rh=expL:
          e.matmul(ps[:, 0:512], lh[:, :], rh[:, 0:512], start=True, stop=True))
    kb.op("tensor", lambda e, ps=sigp, lh=s16, rh=expL:
          e.matmul(ps[:, 512:1024], lh[:, :], rh[:, 512:1024], start=True, stop=True),
          sem="rs_sig")
    sv = kb.sem("rs_sig").val
    kb.wait("vector", "rs_sig", sv)
    kb.op("vector", lambda e, o=sigr, i=sigp: e.reciprocal(o[:, :], i[:, :]),
          sem="rs_rec")
    rv = kb.sem("rs_rec").val
    kb.wait("tensor", "rs_rec", rv)
    kb.op("tensor", lambda e, ps=repp, lh=r16, rh=sigr:
          e.matmul(ps[:, 0:512], lh[:, :], rh[:, 0:512], start=True, stop=True))
    kb.op("tensor", lambda e, ps=repp, lh=r16, rh=sigr:
          e.matmul(ps[:, 512:1024], lh[:, :], rh[:, 512:1024], start=True, stop=True),
          sem="rs_rep")
    rpv = kb.sem("rs_rep").val
    kb.wait("vector", "rs_rep", rpv)
    kb.op("vector", lambda e, o=probL, i0=expL, i1=repp:
          e.tensor_tensor(o[:, :], i0[:, :], i1[:, :], op=AL.mult), sem="rs_prob")
    pv = kb.sem("rs_prob").val
    kb.wait("sync", "rs_prob", pv)
    pdst = probs_d.rearrange("e r (j2 i) -> e r j2 i", j2=2)
    for ee in range(8):
        for rr in range(8):
            kb.dma("sync", pdst[ee, rr],
                   probL[ee * 16 + rr * 2: ee * 16 + rr * 2 + 2, :], "rs_pst")
    pstv = kb.sem("rs_pst").val

    kb.op("vector", lambda e, o=lohi: e.memset(o[:, 0:1], 0.0))
    kb.op("vector", lambda e, o=lohi: e.memset(o[:, 1:2], 1.0), sem="rs_bis")
    for it in range(BISECT):
        kb.op("vector", lambda e, o=mid, l=lohi:
              e.tensor_tensor(o[:, :], l[:, 0:1], l[:, 1:2], op=AL.add))
        kb.op("vector", lambda e, o=mid:
              e.tensor_scalar(o[:, :], o[:, :], 0.5, None, op0=AL.mult), sem="rs_bis")
        bv = kb.sem("rs_bis").val
        kb.wait("tensor", "rs_bis", bv)
        kb.op("tensor", lambda e, ps=trp, lh=rep16, rh=mid:
              e.matmul(ps[:, :], lh[:, :], rh[:, :], start=True, stop=True),
              sem="rs_trm")
        tmv = kb.sem("rs_trm").val
        kb.wait("vector", "rs_trm", tmv)
        kb.op("vector", lambda e, o=trep, i=trp: e.tensor_copy(o[:, :], i[:, :]),
              sem="rs_trc")
        tcv = kb.sem("rs_trc").val
        kb.op("vector", lambda e, o=junk, i=probL, t=trep, a=cntp_s:
              e.tensor_scalar(o[:, :], i[:, :], t[:, :], 0.0, op0=AL.is_ge,
                              op1=AL.add, accum_out=a[:, :]), sem="rs_cnt")
        cv = kb.sem("rs_cnt").val
        kb.wait("tensor", "rs_cnt", cv)
        kb.op("tensor", lambda e, ps=c8p, lh=grp16, rh=cntp_s:
              e.matmul(ps[:, :], lh[:, :], rh[:, :], start=True, stop=True),
              sem="rs_c8")
        c8v = kb.sem("rs_c8").val
        kb.wait("vector", "rs_c8", c8v)
        kb.op("vector", lambda e, o=pred, i=c8p:
              e.tensor_scalar(o[:, :], i[:, :], float(K_SEL), None, op0=AL.is_ge))
        kb.op("vector", lambda e, o=lohi, p=pred, m=mid:
              e.select(o[:, 0:1], p[:, :], m[:, :], o[:, 0:1]))
        kb.op("vector", lambda e, o=lohi, p=pred, m=mid:
              e.select(o[:, 1:2], p[:, :], o[:, 1:2], m[:, :]), sem="rs_bis")

    bisv = kb.sem("rs_bis").val

    kb.wait("sync", "rs_pst", pstv)
    for ex in range(E_EXP):
        mine = mine_l[ex]; work = work_l[ex]; stg = stg_l[ex]; posi = posi_l[ex]
        mv = kb.dma("sync", mine[:, :],
                    probs_d[ex, 0, :].rearrange("(p i) -> p i", p=128), "rs_mld")
        kb.wait("tensor", "rs_bis", bisv)
        kb.op("tensor", lambda e, ps=tsep, lh=sel8, rh=lohi, ex=ex:
              e.matmul(ps[:, :], lh[:, ex * 128:(ex + 1) * 128], rh[:, 0:1],
                       start=True, stop=True), sem="rs_tse")
        tsv = kb.sem("rs_tse").val
        kb.wait("vector", "rs_tse", tsv)
        kb.op("vector", lambda e, o=tse, i=tsep: e.tensor_copy(o[:, :], i[:, :]),
              sem="rs_tsec")
        kb.wait("vector", "rs_mld", mv)
        kb.op("vector", lambda e, o=work, i=mine: e.tensor_copy(o[:, :], i[:, :]))
        for rd in range(2):
            kb.op("vector", lambda e, vm=vals8, w=work:
                  e.max(out=vm[:, :], in_=w[:, :]))
            kb.op("vector", lambda e, vm=vals8, vi=idx8, w=work:
                  e.max_index(out=vi[:, :], in_max=vm[:, :], in_values=w[:, :]))
            kb.op("vector", lambda e, w=work, vm=vals8:
                  e.match_replace(out=w[:, :], in_to_replace=vm[:, :],
                                  in_values=w[:, :], imm_value=-1.0))
            kb.op("vector", lambda e, o=idxf, i=idx8: e.tensor_copy(o[:, :], i[:, :]))
            kb.op("vector", lambda e, o=idxf, i=iota16:
                  e.tensor_scalar(o[:, :], o[:, :], i[:, :], None, op0=AL.add))
            kb.op("vector", lambda e, o=stg, i=idxf, rd=rd:
                  e.tensor_copy(o[:, rd * 16 + 0:rd * 16 + 16:2], i[:, :]))
            kb.op("vector", lambda e, o=stg, i=vals8, rd=rd:
                  e.tensor_copy(o[:, rd * 16 + 1:rd * 16 + 16:2], i[:, :]), sem="rs_stg")
        kb.op("vector", lambda e, o=mask16, i=stg, t=tse, a=cpart:
              e.tensor_scalar(o[:, :], i[:, 1:32:2], t[:, :], 0.0, op0=AL.is_ge,
                              op1=AL.add, accum_out=a[:, :]), sem="rs_cp")
        cpv = kb.sem("rs_cp").val
        kb.wait("tensor", "rs_cp", cpv)
        kb.op("tensor", lambda e, ps=offp, lh=lt128, rh=cpart:
              e.matmul(ps[:, :], lh[:, :], rh[:, :], start=True, stop=True),
              sem="rs_off")
        ofv = kb.sem("rs_off").val
        kb.wait("vector", "rs_off", ofv)
        kb.op("vector", lambda e, o=offs, i=offp: e.tensor_copy(o[:, :], i[:, :]))
        kb.op("vector", lambda e, o=ojj, j=jj16, s=offs:
              e.tensor_scalar(o[:, :], j[:, :], s[:, :], None, op0=AL.add))
        kb.op("vector", lambda e, o=okv, i=ojj:
              e.tensor_scalar(o[:, :], i[:, :], float(CAP), None, op0=AL.is_lt))
        kb.op("vector", lambda e, o=okv, m=mask16:
              e.tensor_tensor(o[:, :], o[:, :], m[:, :], op=AL.mult))
        kb.op("vector", lambda e, o=okvi, i=okv: e.tensor_copy(o[:, :], i[:, :]))
        kb.op("vector", lambda e, o=posf, i=ojj, ex=ex:
              e.tensor_scalar(o[:, :], i[:, :], float(ex * CAP), None, op0=AL.add))
        kb.op("vector", lambda e, o=posf, m=okvi, d=dumpc:
              e.select(o[:, :], m[:, :], o[:, :],
                       d[:, 0:1].to_broadcast([128, 16])))
        kb.op("vector", lambda e, o=posi, i=posf: e.tensor_copy(o[:, :], i[:, :]),
              sem="rs_pos")
        psv = kb.sem("rs_pos").val
        kb.wait("gpsimd", "rs_pf", pfv)
        kb.wait("gpsimd", "rs_pos", psv)
        for j in range(16):
            kb.idma(pairs_d, IndirectOffsetOnAxis(ap=posi[:, j:j + 1], axis=0),
                    stg[:, 2 * j:2 * j + 2], None, "rs_psc")
        kb.wait("vector", "rs_psc")

    sent_d = kb.d["sentinel"].ap()
    kb.op("vector", lambda e_, o=posi_l[0]: e_.memset(o[0:128, 0:1], 0), sem="rs_zi")
    kb.wait("gpsimd", "rs_zi")
    sv = kb.idma(fillt[:, 0:1], None, sent_d,
                 IndirectOffsetOnAxis(ap=posi_l[0][:, 0:1], axis=0), "rs_sent")
    for eng in ENGINES:
        kb.wait(eng, "rs_sent", sv)
    kb.phase_end()


def host_constants():
    c = {}
    comb17 = np.zeros((17, 8), np.float32)
    for e in range(8):
        comb17[e, e] = 1.0
        comb17[e + 8, e] = 1.0
    c["comb17"] = comb17
    p = np.arange(128)
    c["s16"] = (p[:, None] % 16 == np.arange(16)[None, :]).astype(np.float32)
    c["r16"] = (np.arange(16)[:, None] == p[None, :] % 16).astype(np.float32)
    c["rep16"] = (np.arange(8)[:, None] == p[None, :] // 16).astype(np.float32)
    c["grp16"] = (p[:, None] // 16 == np.arange(8)[None, :]).astype(np.float32)
    sel8 = np.zeros((8, 1024), np.float32)
    for e in range(8):
        sel8[e, e * 128:(e + 1) * 128] = 1.0
    c["sel8"] = sel8
    c["lt128"] = (p[:, None] < p[None, :]).astype(np.float32)
    c["jj16"] = np.broadcast_to(np.arange(16, dtype=np.float32), (128, 16)).copy()
    c["iota16"] = (16.0 * p[:, None]).astype(np.float32)
    c["iota1"] = p[:, None].astype(np.float32)
    pad32 = np.zeros((128, 4), np.float32)
    for hh in range(4):
        pad32[32 * hh:32 * hh + 16, hh] = 1.0
    c["pad32"] = pad32.astype(ml_dtypes.bfloat16)
    selh4 = np.zeros((4, 512), np.float32)
    for hh in range(4):
        selh4[hh, 128 * hh:128 * (hh + 1)] = 1.0
    c["selh4"] = selh4
    c["ident"] = np.eye(128, dtype=ml_dtypes.bfloat16)
    dh = 64
    inv = 1.0 / (10000.0 ** (np.arange(0, dh, 2, dtype=np.float64) / dh))
    pos = np.arange(S, dtype=np.float64)
    dd = p % 64
    jj = np.where(dd < 32, dd, dd - 32)
    ang = pos[None, :] * inv[jj][:, None]
    c["ropecos"] = np.cos(ang).astype(np.float32)
    c["ropesin"] = np.sin(ang).astype(np.float32)
    rr = np.zeros((128, 128), np.float32)
    for blk in range(2):
        o = blk * 64
        for d_ in range(32):
            rr[o + d_ + 32, o + d_] = -1.0
            rr[o + d_, o + d_ + 32] = 1.0
    c["rrot2"] = rr.astype(ml_dtypes.bfloat16)
    cm = np.zeros((4, 128, 512), np.float32)
    for rel in range(4):
        mm = 128 * rel + np.arange(128)[:, None]
        nn = np.arange(512)[None, :]
        cm[rel] = (mm <= nn).astype(np.float32)
    c["cmask"] = cm.astype(ml_dtypes.bfloat16)
    p64 = np.zeros((128, 2), np.float32)
    p64[0:64, 0] = 1.0
    p64[64:128, 1] = 1.0
    c["pad64"] = p64.astype(ml_dtypes.bfloat16)
    s2 = np.zeros((2, 256), np.float32)
    s2[0, 0:128] = 1.0
    s2[1, 128:256] = 1.0
    c["selh2"] = s2
    return c


CONST_SPECS = {
    "comb17": ([17, 8], F32), "s16": ([128, 16], F32), "r16": ([16, 128], F32),
    "rep16": ([8, 128], F32), "grp16": ([128, 8], F32), "sel8": ([8, 1024], F32),
    "lt128": ([128, 128], F32), "jj16": ([128, 16], F32), "iota16": ([128, 1], F32),
    "iota1": ([128, 1], F32), "ident": ([128, 128], BF16),
    "pad32": ([128, 4], BF16), "selh4": ([4, 512], F32),
    "ropecos": ([128, S], F32), "ropesin": ([128, S], F32),
    "rrot2": ([128, 128], BF16), "cmask": ([4, 128, 512], BF16),
    "pad64": ([128, 2], BF16), "selh2": ([2, 256], F32),
}

SCRATCH_SPECS = {
    "xhi": ([NT, 1024], BF16), "xlo": ([NT, 1024], BF16),
    "mu": ([NT, 1], F32), "rstd": ([NT, 1], F32),
    "xnb": ([S, 1024], BF16), "xnf": ([S + 1, 1024], F32),
    "logits": ([8, 8, S], F32), "probs": ([8, 8, S], F32),
    "pairs": ([E_EXP * CAP + 128, 2], F32),
    "e2hi": ([D, S], BF16), "e2lo": ([D, S], BF16),
    "sentinel": ([128, 1], F32),
    "xin": ([E_EXP, CAP, D], BF16),
    "yhi": ([E_EXP, D, CAP], BF16), "ylo": ([E_EXP, D, CAP], BF16),
    "moeacc2": ([S + 1, E_EXP * D], F32),
    "x1": ([S, D], F32), "xn1": ([S, D], BF16),
    "s1hi": ([D, S], BF16), "s1lo": ([D, S], BF16),
    "f1hi": ([D, S], BF16), "f1lo": ([D, S], BF16),
}

INPUT_SPECS = {
    "x": ([NT, 1024], F32),
    "ws": ([1024, 8], F32), "bs": ([8, 1], F32),
    "l0_wqT": ([D, 512], F32), "l0_wkT": ([D, 512], F32),
    "l0_wvT": ([D, D], F32), "l0_wpT": ([D, D], F32),
    "l0_w1T": ([D, 64], F32), "l0_w2T": ([64, D], F32),
    "l0_b1": ([64, 1], F32), "l0_b2": ([D, 1], F32),
    "moe_w1": ([E_EXP, D, F_MOE], F32), "moe_w2": ([E_EXP, F_MOE, D], F32),
    "moe_b1": ([E_EXP, F_MOE], F32), "moe_b2": ([E_EXP, D], F32),
    "l1_wqT": ([D, D], F32), "l1_wkT": ([D, D], F32),
    "l1_wvT": ([D, D], F32), "l1_woT": ([D, D], F32),
    "l1_wgT": ([D, H_FF1], I32), "l1_wuT": ([D, H_FF1], I32),
    "l1_wdT": ([H_FF1, D], I32),
}


def build_program(phases, dump=(), extra_inputs=(), extra_scratch=()):
    nc = bass.Bass("TRN2", target_bir_lowering=False, debug=False)
    stack = ExitStack()
    kb = KB(nc, stack)
    with stack:
        for name, (shape, dt) in {**INPUT_SPECS, **CONST_SPECS, **dict(extra_inputs)}.items():
            kb.dram(name, shape, dt, kind="ExternalInput")
        for name, (shape, dt) in {**SCRATCH_SPECS, **dict(extra_scratch)}.items():
            kind = "ExternalOutput" if name in dump else "Internal"
            kb.dram(name, shape, dt, kind=kind)
        kb.dram("out", [S, D], F32, kind="ExternalOutput")
        for ph in phases:
            ph(kb)
        for name, s in list(kb._sems.items()):
            if s.val > 0:
                kb.ops["sync"].append(lambda eng, h=s.h, v=s.val: eng.wait_ge(h, v))
        kb.finalize_block()
    return kb


def prep_host_inputs(inputs, core):
    x = inputs["L0_attn_x"]
    xr = np.roll(x, -core, axis=0).reshape(NT, 1024)
    wq_pad = np.zeros((1024, 512), np.float32)
    wk_pad = np.zeros((1024, 512), np.float32)
    for hc in range(16):
        wq_pad[:, 32 * hc:32 * hc + 16] = inputs["L0_attn_Wq"][16 * hc:16 * hc + 16, :].T
        wk_pad[:, 32 * hc:32 * hc + 16] = inputs["L0_attn_Wk"][16 * hc:16 * hc + 16, :].T
    m = {
        "x": np.ascontiguousarray(xr),
        "ws": np.ascontiguousarray(inputs["L0_ffn_Ws"]),
        "bs": np.ascontiguousarray(inputs["L0_ffn_bs"].reshape(8, 1)),
        "l0_wqT": wq_pad, "l0_wkT": wk_pad,
        "l0_wvT": np.ascontiguousarray(inputs["L0_attn_Wv"].T),
        "l0_wpT": np.ascontiguousarray(inputs["L0_attn_Wp"].T),
        "l0_w1T": np.ascontiguousarray(inputs["L0_attn_W1"].T),
        "l0_w2T": np.ascontiguousarray(inputs["L0_attn_W2"].T),
        "l0_b1": np.ascontiguousarray(inputs["L0_attn_b1"].reshape(64, 1)),
        "l0_b2": np.ascontiguousarray(inputs["L0_attn_b2"].reshape(1024, 1)),
        "moe_w1": inputs["L0_ffn_W1"], "moe_w2": inputs["L0_ffn_W2"],
        "moe_b1": inputs["L0_ffn_b1"], "moe_b2": inputs["L0_ffn_b2"],
        "l1_wqT": np.ascontiguousarray(inputs["L1_attn_Wq"].T),
        "l1_wkT": np.ascontiguousarray(inputs["L1_attn_Wk"].T),
        "l1_wvT": np.ascontiguousarray(inputs["L1_attn_Wv"].T),
        "l1_woT": np.ascontiguousarray(inputs["L1_attn_Wo"].T),
        "l1_wgT": np.ascontiguousarray(inputs["L1_ffn_w_gate"].T),
        "l1_wuT": np.ascontiguousarray(inputs["L1_ffn_w_up"].T),
        "l1_wdT": np.ascontiguousarray(inputs["L1_ffn_w_down"].T),
    }
    m.update(host_constants())
    return m





def l0_attn(kb):
    kb.sem("la_sc", producer="tensor")
    kb.sem("la_u", producer="tensor")
    kb.sem("la_pj", producer="tensor")
    kb.sem("la_nmm", producer="tensor")
    kb.fence("rs_pst", "rs_mld", "rs_pf")
    xnb = kb.d["xnb"].ap()
    e2hi_d = kb.d["e2hi"].ap()
    e2lo_d = kb.d["e2lo"].ap()
    wq_d = kb.d["l0_wqT"].ap()
    wk_d = kb.d["l0_wkT"].ap()
    wv_d = kb.d["l0_wvT"].ap()
    wp_d = kb.d["l0_wpT"].ap()
    w1_d = kb.d["l0_w1T"].ap()
    w2_d = kb.d["l0_w2T"].ap()
    b1_d = kb.d["l0_b1"].ap()
    b2_d = kb.d["l0_b2"].ap()
    pad32_d = kb.d["pad32"].ap()
    selh4_d = kb.d["selh4"].ap()
    SC = 0.25

    kb.phase_begin()
    e = [kb.sbuf(f"la_e{j}", [128, 2048], F32) for j in range(8)]
    ebf = [kb.sbuf(f"la_ebf{j}", [128, 2048], BF16) for j in range(8)]
    kp = [kb.sbuf(f"la_kp{g}", [128, 2048], BF16) for g in range(4)]
    qp = kb.sbuf("la_qp", [128, 512], BF16)
    vaug = [kb.sbuf(f"la_va{m}", [128, 1040], BF16) for m in range(16)]
    expb = [kb.sbuf(f"la_ex{i}", [128, 1024], BF16) for i in range(2)]
    attnT = kb.sbuf("la_attnT", [128, 4096], BF16)
    wq = [kb.sbuf(f"la_wq{c}", [128, 512], BF16) for c in range(8)]
    wkg = [kb.sbuf(f"la_wkg{c}", [128, 128], BF16) for c in range(8)]
    wv = [kb.sbuf(f"la_wv{c}", [128, 512], BF16) for c in range(8)]
    wp = [kb.sbuf(f"la_wp{c}", [128, 1024], BF16) for c in range(8)]
    w1 = [kb.sbuf(f"la_w1{c}", [128, 64], BF16) for c in range(8)]
    w2 = kb.sbuf("la_w2", [64, 1024], BF16)
    b1 = kb.sbuf("la_b1", [64, 1], F32)
    b2 = [kb.sbuf(f"la_b2{c}", [128, 1], F32) for c in range(8)]
    h2 = [kb.sbuf(f"la_h{i}", [64, 512], BF16) for i in range(2)]
    sig = kb.sbuf("la_sig", [1, 512], F32)
    bcs = kb.sbuf("la_bcs", [64, 512], F32)
    onesf64 = kb.sbuf("la_ones64", [1, 64], F32)
    pad32 = kb.sbuf("la_pad32", [128, 4], BF16)
    selh4 = kb.sbuf("la_selh4", [4, 512], F32)
    sqt = kb.sbuf("la_sqt", [128, 512], BF16)
    kmx = [kb.sbuf(f"la_kmx{g}", [4, 1], F32) for g in range(4)]
    qmx = [kb.sbuf(f"la_qmx{g}", [4, 1], F32) for g in range(4)]
    t4 = kb.sbuf("la_t4", [4, 2], F32)
    negm4 = [kb.sbuf(f"la_nm4{g}", [4, 1], F32) for g in range(4)]
    nmax = [kb.sbuf(f"la_nmax{i}", [128, 1], F32) for i in range(4)]
    scp = [kb.psum(f"la_scp{i}", [128, 1024]) for i in range(2)]
    up = [kb.psum(f"la_up{i}", [65, 512]) for i in range(2)]
    bcp = kb.psum("la_bcp", [64, 512])
    nmsh = kb.psum("la_nmsh", [128, 512])
    np4 = nmsh[0:4, 0:512]
    nmp = nmsh[0:128, 0:1]

    for c in range(8):
        kb.dma("gpsimd", wq[c][:, :], wq_d[c * 128:(c + 1) * 128, :], "la_wld")
        kb.dma("gpsimd", wp[c][:, :], wp_d[c * 128:(c + 1) * 128, :], "la_wld")
        kb.dma("gpsimd", w1[c][:, :], w1_d[c * 128:(c + 1) * 128, :], "la_wld")
        kb.dma("gpsimd", b2[c][:, :], b2_d[c * 128:(c + 1) * 128, :], "la_wld")
    kb.dma("gpsimd", w2[:, :], w2_d[:, :], "la_wld")
    kb.dma("gpsimd", b1[:, :], b1_d[:, :], "la_wld")
    wldv = kb.sem("la_wld").val
    kb.dma("sync", pad32[:, :], pad32_d[:, :], "la_wldc")
    kb.dma("sync", selh4[:, :], selh4_d[:, :], "la_wldc")
    wldcv = kb.sem("la_wldc").val
    kb.op("vector", lambda e_, o=onesf64: e_.memset(o[:, :], 1.0), sem="la_init")
    initv = kb.sem("la_init").val

    kb.wait("scalar", "p0_st")
    for j in range(8):
        kb.dmat("scalar", ebf[j][:, :], xnb[:, j * 128:(j + 1) * 128], "la_tp")
    tpv = kb.sem("la_tp").val
    kb.wait("vector", "la_tp", tpv)
    for j in range(8):
        kb.op("vector", lambda e_, o=e[j], i=ebf[j]:
              e_.tensor_copy(o[:, :], i[:, :]), sem="la_e0")
    kb.wait("tensor", "la_wld", wldv)
    kb.wait("tensor", "la_wldc", wldcv)
    kb.wait("tensor", "la_init", initv)
    kb.wait("tensor", "la_tp", tpv)
    kb.wait("scalar", "la_wld", wldv)
    kb.wait("vector", "la_wld", wldv)

    scp_g = [0, 0]
    up_g = [0, 0]
    exb_g = [0, 0]

    def halves(i):
        return scp[i // 2][:, (i % 2) * 512:(i % 2) * 512 + 512]

    for it in range(2):
        if it == 1:
            kb.wait("vector", "la_pe")
            for j in range(8):
                kb.op("vector", lambda e_, o=ebf[j], i=e[j]:
                      e_.tensor_copy(o[:, :], i[:, :]), sem="la_ebf")
            kb.wait("tensor", "la_ebf")
            kb.wait("scalar", "la_ebf")

        for g in range(4):
            kb.wait("gpsimd", "la_sc")
            for c in range(8):
                kb.dma("gpsimd", wkg[c][:, :],
                       wk_d[c * 128:(c + 1) * 128, g * 128:(g + 1) * 128], "la_wkld")
            wkv = kb.sem("la_wkld").val
            kb.wait("tensor", "la_wkld", wkv)
            for n in range(4):
                hb = (g * 4 + n) % 4
                kb.wait("tensor", "la_exp", scp_g[hb // 2])
                prs = [(wkg[c][:, :], ebf[c][:, n * 512:(n + 1) * 512])
                       for c in range(8)]
                mm_accum(kb, halves(hb), prs, sem="la_sc")
                scv = kb.sem("la_sc").val
                kb.wait("vector", "la_sc", scv)
                kb.op("vector", lambda e_, o=kp[g], i=halves(hb), n=n:
                      e_.tensor_copy(o[:, n * 512:(n + 1) * 512], i),
                      sem="la_exp")
                scp_g[hb // 2] = kb.sem("la_exp").val
        for half in range(2):
            kb.wait("gpsimd", "la_sc")
            for c in range(8):
                kb.dma("gpsimd", wv[c][:, :],
                       wv_d[c * 128:(c + 1) * 128, half * 512:(half + 1) * 512],
                       "la_wvld")
            wvv = kb.sem("la_wvld").val
            kb.wait("tensor", "la_wvld", wvv)
            for mt in range(16):
                hb = mt % 4
                kb.wait("tensor", "la_exp", scp_g[hb // 2])
                prs = [(ebf[c][:, mt * 128:(mt + 1) * 128], wv[c][:, :])
                       for c in range(8)]
                mm_accum(kb, halves(hb), prs, sem="la_sc")
                vv = kb.sem("la_sc").val
                kb.wait("vector", "la_sc", vv)
                dst = vaug[mt].rearrange("p (hh cc) -> p hh cc", cc=65)
                kb.op("vector", lambda e_, d=dst, i=halves(hb), half=half:
                      e_.tensor_copy(
                          d[:, half * 8:(half + 1) * 8, 0:64],
                          i.rearrange("p (hh cc) -> p hh cc", cc=64)),
                      sem="la_exp")
                scp_g[hb // 2] = kb.sem("la_exp").val
        for mt in range(16):
            kb.op("vector", lambda e_, o=vaug[mt]:
                  e_.memset(o[:, 64:1040:65], 1.0), sem="la_exp")
        scp_g[0] = scp_g[1] = kb.sem("la_exp").val
        kp_done = kb.sem("la_exp").val
        kb.wait("scalar", "la_exp", kp_done)
        for g in range(4):
            for n in range(4):
                kb.wait("scalar", "la_nmm")
                kb.op("scalar", lambda e_, o=sqt, i=kp[g], n=n:
                      e_.activation(o[:, :], i[:, n * 512:(n + 1) * 512], AF.Square),
                      sem="la_ksq")
                kv = kb.sem("la_ksq").val
                kb.wait("tensor", "la_ksq", kv)
                kb.op("tensor", lambda e_, ps=np4, lh=pad32, rh=sqt:
                      e_.matmul(ps, lh[:, :], rh[:, :], start=True, stop=True),
                      sem="la_nmm")
                nv = kb.sem("la_nmm").val
                kb.wait("vector", "la_nmm", nv)
                kb.op("vector", lambda e_, o=t4, i=np4:
                      e_.tensor_reduce(o[0:4, 0:1], i, axis=mybir.AxisListType.X,
                                       op=AL.max))
                if n == 0:
                    kb.op("vector", lambda e_, o=kmx[g], i=t4:
                          e_.tensor_copy(o[:, :], i[0:4, 0:1]), sem="la_nred")
                else:
                    kb.op("vector", lambda e_, o=kmx[g], i=t4:
                          e_.tensor_tensor(o[:, :], o[:, :], i[0:4, 0:1], op=AL.max),
                          sem="la_nred")
                kb.wait("scalar", "la_nred")

        for nt in range(4):
            kb.wait("vector", "la_pj")
            for g in range(4):
                hb = g % 4
                kb.wait("tensor", "la_exp", scp_g[hb // 2])
                prs = [(wq[c][:, g * 128:(g + 1) * 128],
                        ebf[c][:, nt * 512:(nt + 1) * 512]) for c in range(8)]
                mm_accum(kb, halves(hb), prs, sem="la_q")
                qv = kb.sem("la_q").val
                kb.wait("vector", "la_q", qv)
                kb.wait("vector", "la_sc")
                kb.op("vector", lambda e_, o=qp, i=halves(hb):
                      e_.tensor_copy(o[:, :], i), sem="la_exp")
                scp_g[hb // 2] = kb.sem("la_exp").val
                qev = scp_g[hb // 2]
                kb.wait("scalar", "la_exp", qev)
                kb.wait("scalar", "la_nmm")
                kb.op("scalar", lambda e_, o=sqt, i=qp:
                      e_.activation(o[:, :], i[:, :], AF.Square), sem="la_qsq")
                qsv = kb.sem("la_qsq").val
                kb.wait("tensor", "la_qsq", qsv)
                kb.op("tensor", lambda e_, ps=np4, lh=pad32, rh=sqt:
                      e_.matmul(ps, lh[:, :], rh[:, :], start=True, stop=True),
                      sem="la_nmm")
                nv = kb.sem("la_nmm").val
                kb.wait("vector", "la_nmm", nv)
                kb.op("vector", lambda e_, o=qmx[g], i=np4:
                      e_.tensor_reduce(o[:, :], i, axis=mybir.AxisListType.X,
                                       op=AL.max))
                kb.op("vector", lambda e_, o=t4, a=kmx[g], b=qmx[g]:
                      e_.tensor_tensor(o[0:4, 1:2], a[:, :], b[:, :], op=AL.mult),
                      sem="la_nm1")
                v1 = kb.sem("la_nm1").val
                kb.wait("scalar", "la_nm1", v1)
                kb.op("scalar", lambda e_, o=t4:
                      e_.activation(o[0:4, 0:1], o[0:4, 1:2], AF.Sqrt), sem="la_nm2")
                v2 = kb.sem("la_nm2").val
                kb.wait("vector", "la_nm2", v2)
                kb.op("vector", lambda e_, o=negm4[g], i=t4:
                      e_.tensor_scalar(o[:, :], i[0:4, 0:1], -SC, None, op0=AL.mult),
                      sem="la_nm3")
                kb.wait("scalar", "la_nm3")
                kb.wait("tensor", "la_nm3")
                for p in (2 * g, 2 * g + 1):
                    c0 = 2 * p
                    c1 = 2 * p + 1
                    rb0 = 32 * (c0 % 4)
                    rb1 = 32 * (c1 % 4)
                    for ci, cc in enumerate((c0, c1)):
                        kb.op("tensor", lambda e_, ps=nmp, lh=selh4, rh=negm4[g],
                              hh=cc % 4:
                              e_.matmul(ps, lh[:, hh * 128:(hh + 1) * 128],
                                        rh[:, :], start=True, stop=True), sem="la_nbc")
                        nbv = kb.sem("la_nbc").val
                        kb.wait("vector", "la_nbc", nbv)
                        kb.op("vector", lambda e_, o=nmax[cc % 4], i=nmp:
                              e_.tensor_copy(o[:, :], i), sem="la_nbe")
                        nbe = kb.sem("la_nbe").val
                        kb.wait("scalar", "la_nbe", nbe)
                        kb.wait("tensor", "la_nbe", nbe)
                    kb.wait("tensor", "la_att", max(up_g))
                    pend = None
                    for mp in range(8):
                        for ci, (cc, rb) in enumerate(((c0, rb0), (c1, rb1))):
                            kb.wait("tensor", "la_exp", scp_g[ci])
                            for sub in range(2):
                                mt = 2 * mp + sub
                                kb.op("tensor", lambda e_, ps=scp[ci], g=g, rb=rb,
                                      mt=mt, sub=sub:
                                      e_.matmul(ps[:, sub * 512:(sub + 1) * 512],
                                                kp[g][rb:rb + 16,
                                                      mt * 128:(mt + 1) * 128],
                                                qp[rb:rb + 16, :],
                                                start=True, stop=True,
                                                tile_position=(rb, 0)),
                                      sem="la_sc" if sub == 1 else None)
                        scv = kb.sem("la_sc").val
                        if pend is not None:
                            emit_u(kb, pend, vaug, expb, up, last=False)
                        for ci, cc in enumerate((c0, c1)):
                            kb.wait("scalar", "la_sc", scv)
                            kb.wait("scalar", "la_u", exb_g[ci])
                            kb.op("scalar", lambda e_, o=expb[ci], i=scp[ci],
                                  bi=nmax[cc % 4]:
                                  e_.activation(o[:, :], i[:, :], AF.Exp,
                                                bias=bi[:, :], scale=SC), sem="la_exp")
                            scp_g[ci] = kb.sem("la_exp").val
                        pend = (mp, c0, c1, kb.sem("la_exp").val)
                        if mp == 7:
                            emit_u(kb, pend, vaug, expb, up, last=True)
                    exb_g[0] = exb_g[1] = kb.sem("la_u").val
                    uv = kb.sem("la_u").val
                    for ci, cc in enumerate((c0, c1)):
                        ub = ci
                        kb.wait("vector", "la_u", uv)
                        kb.op("vector", lambda e_, o=sig, i=up[ub]:
                              e_.reciprocal(o[:, :], i[64:65, :]), sem="la_rc")
                        rcv = kb.sem("la_rc").val
                        kb.wait("tensor", "la_rc", rcv)
                        kb.op("tensor", lambda e_, ps=bcp, lh=onesf64, rh=sig:
                              e_.matmul(ps[:, :], lh[:, :], rh[:, :],
                                        start=True, stop=True), sem="la_bc")
                        bcv = kb.sem("la_bc").val
                        kb.wait("vector", "la_bc", bcv)
                        kb.op("vector", lambda e_, o=bcs, i=bcp:
                              e_.tensor_copy(o[:, :], i[:, :]))
                        kb.op("vector", lambda e_, o=attnT, u=up[ub], b_=bcs, cc=cc:
                              e_.scalar_tensor_tensor(
                                  o[(cc % 2) * 64:(cc % 2) * 64 + 64,
                                    512 * (cc // 2):512 * (cc // 2) + 512],
                                  u[0:64, :], 1.0, b_[:, :],
                                  op0=AL.mult, op1=AL.mult), sem="la_att")
                        up_g[ub] = kb.sem("la_att").val
            attv = kb.sem("la_att").val
            kb.wait("tensor", "la_att", attv)
            for dc in range(8):
                hb = dc % 4
                kb.wait("tensor", "la_exp", scp_g[hb // 2])
                prs = [(wp[c][:, dc * 128:(dc + 1) * 128],
                        attnT[:, 512 * c:512 * c + 512]) for c in range(8)]
                mm_accum(kb, halves(hb), prs, sem="la_pj")
                pjv = kb.sem("la_pj").val
                kb.wait("vector", "la_pj", pjv)
                kb.op("vector", lambda e_, o=e[dc], i=halves(hb), nt=nt:
                      e_.tensor_tensor(o[:, nt * 512:(nt + 1) * 512], i,
                                       o[:, nt * 512:(nt + 1) * 512], op=AL.add),
                      sem="la_exp")
                scp_g[hb // 2] = kb.sem("la_exp").val

        pev = kb.sem("la_exp").val
        kb.wait("vector", "la_exp", pev)
        for j in range(8):
            kb.op("vector", lambda e_, o=ebf[j], i=e[j]:
                  e_.tensor_copy(o[:, :], i[:, :]), sem="la_rbf")
        rbv = kb.sem("la_rbf").val
        kb.wait("tensor", "la_rbf", rbv)
        h_guard = [0, 0]
        for nt in range(4):
            ub = nt % 2
            kb.wait("tensor", "la_att", up_g[ub])
            prs = [(w1[c][:, :], ebf[c][:, nt * 512:(nt + 1) * 512])
                   for c in range(8)]
            mm_accum(kb, up[ub][0:64, :], prs, sem="la_hm")
            hv = kb.sem("la_hm").val
            kb.wait("scalar", "la_hm", hv)
            kb.wait("scalar", "la_pj", h_guard[ub])
            kb.op("scalar", lambda e_, o=h2[ub], i=up[ub], b_=b1:
                  e_.activation(o[:, :], i[0:64, :], AF.Relu, bias=b_[:, :]),
                  sem="la_att")
            up_g[ub] = kb.sem("la_att").val
            hrv = kb.sem("la_att").val
            kb.wait("tensor", "la_att", hrv)
            for dc in range(8):
                hb = dc % 4
                kb.wait("tensor", "la_exp", scp_g[hb // 2])
                kb.op("tensor", lambda e_, ps=halves(hb), dc=dc, ub=ub:
                      e_.matmul(ps, w2[:, dc * 128:(dc + 1) * 128], h2[ub][:, :],
                                start=True, stop=True), sem="la_pj")
                pjv = kb.sem("la_pj").val
                kb.wait("vector", "la_pj", pjv)
                kb.op("vector", lambda e_, o=e[dc], i=halves(hb), b_=b2[dc], nt=nt:
                      e_.scalar_tensor_tensor(o[:, nt * 512:(nt + 1) * 512],
                                              i, b_[:, :],
                                              o[:, nt * 512:(nt + 1) * 512],
                                              op0=AL.add, op1=AL.add), sem="la_exp")
                scp_g[hb // 2] = kb.sem("la_exp").val
            h_guard[ub] = kb.sem("la_pj").val
        kb.op("vector", lambda e_: e_.engine_nop() if False else e_.nop(),
              sem="la_pe")

    pev = kb.sem("la_exp").val
    kb.wait("vector", "la_exp", pev)
    kb.wait("scalar", "la_exp", pev)
    st_rec = {}
    for dc in range(8):
        scr = kp[dc % 4]
        if dc >= 4:
            kb.wait("vector", "la_st", st_rec[dc - 4])
            kb.wait("scalar", "la_st", st_rec[dc - 4])
        kb.op("vector", lambda e_, o=ebf[dc], i=e[dc]:
              e_.tensor_copy(o[:, :], i[:, :]), sem="la_hi")
        hv = kb.sem("la_hi").val
        kb.wait("scalar", "la_hi", hv)
        for n in range(4):
            kb.op("scalar", lambda e_, o=sqt, i=ebf[dc], n=n:
                  e_.copy(o[:, :], i[:, n * 512:(n + 1) * 512]), sem="la_hf")
            hfv = kb.sem("la_hf").val
            kb.wait("vector", "la_hf", hfv)
            kb.op("vector", lambda e_, o=scr, i=e[dc], hf=sqt, n=n:
                  e_.tensor_tensor(o[:, n * 512:(n + 1) * 512],
                                   i[:, n * 512:(n + 1) * 512], hf[:, :],
                                   op=AL.subtract), sem="la_lo")
            lv = kb.sem("la_lo").val
            kb.wait("scalar", "la_lo", lv)
        lov = kb.sem("la_lo").val
        kb.wait("sync", "la_hi", hv)
        kb.dma("sync", e2hi_d[dc * 128:(dc + 1) * 128, :], ebf[dc][:, :], "la_st")
        kb.wait("sync", "la_lo", lov)
        kb.dma("sync", e2lo_d[dc * 128:(dc + 1) * 128, :], scr[:, :], "la_st")
        st_rec[dc] = kb.sem("la_st").val
    kb.phase_end()


def emit_u(kb, pend, vaug, expb, up, last):
    mp, c0, c1, expv = pend
    kb.wait("tensor", "la_exp", expv)
    for ci, cc in enumerate((c0, c1)):
        for sub in range(2):
            mt = 2 * mp + sub
            kb.op("tensor", lambda e_, ps=up[ci], mt=mt, cc=cc, ci=ci, sub=sub,
                  st=(mp == 0 and sub == 0), sp=(last and sub == 1):
                  e_.matmul(ps[:, :], vaug[mt][:, 65 * cc:65 * cc + 65],
                            expb[ci][:, sub * 512:(sub + 1) * 512],
                            start=st, stop=sp),
                  sem="la_u")


def moe_ffn(kb):
    kb.sem("mf_hmm", producer="tensor")
    kb.sem("mf_ymm", producer="tensor")
    kb.fence("la_st")
    pairs_d = kb.d["pairs"].ap()
    xnf_d = kb.d["xnf"].ap()
    xin_d = kb.d["xin"].ap()
    yhi_d = kb.d["yhi"].ap()
    ylo_d = kb.d["ylo"].ap()
    acc2_d = kb.d["moeacc2"].ap()
    w1_d = kb.d["moe_w1"].ap()
    w2_d = kb.d["moe_w2"].ap()
    b1_d = kb.d["moe_b1"].ap()
    b2_d = kb.d["moe_b2"].ap()
    NG = CAP // 128

    kb.phase_begin()
    ids_i = [[kb.sbuf(f"mf_id{e}_{g}", [128, 1], I32) for g in range(NG)]
             for e in range(E_EXP)]
    gts = [[kb.sbuf(f"mf_gt{e}_{g}", [128, 1], F32) for g in range(NG)]
           for e in range(E_EXP)]
    prt = kb.sbuf("mf_prt", [128, 2], F32)
    xg = [kb.sbuf(f"mf_xg{i}", [128, 1024], F32) for i in range(2)]
    xgb = [kb.sbuf(f"mf_xgb{i}", [128, 1024], BF16) for i in range(2)]
    xinT = [kb.sbuf(f"mf_xt{c}", [128, CAP], BF16) for c in range(8)]
    w1s = [kb.sbuf(f"mf_w1{c}", [128, F_MOE], BF16) for c in range(8)]
    w2s = [kb.sbuf(f"mf_w2{c}", [128, 1024], BF16) for c in range(32)]
    hT = [kb.sbuf(f"mf_h{fc}", [128, CAP], BF16) for fc in range(32)]
    b1t = kb.sbuf("mf_b1", [128, 1], F32)
    b2t = [kb.sbuf(f"mf_b2{dc}", [128, 1], F32) for dc in range(8)]
    yf = kb.sbuf("mf_yf", [128, CAP], F32)
    zidx = kb.sbuf("mf_zidx", [128, 1], I32)
    sjunk = kb.sbuf("mf_sjunk", [128, 1], F32)
    yhib = kb.sbuf("mf_yhi", [128, CAP], BF16)
    yhif = kb.sbuf("mf_yhf", [128, CAP], F32)
    ylob = kb.sbuf("mf_ylo", [128, CAP], BF16)
    yth = kb.sbuf("mf_yth", [128, 1024], BF16)
    ytl = kb.sbuf("mf_ytl", [128, 1024], BF16)
    ytf = [kb.sbuf(f"mf_ytf{g}", [128, 1024], F32) for g in range(3)]
    zt = kb.sbuf("mf_zt", [128, 1024], F32)
    hp = [kb.psum(f"mf_hp{i}", [128, 512]) for i in range(2)]

    kb.op("vector", lambda e_, o=zidx: e_.memset(o[:, :], 0), sem="mf_zi")
    kb.wait("gpsimd", "mf_zi")
    kb.op("vector", lambda e_, o=zt: e_.memset(o[:, :], 0.0), sem="mf_z")
    zv = kb.sem("mf_z").val
    kb.wait("sync", "mf_z", zv)
    acc2_flat = acc2_d[0:S, :].rearrange("s w -> (s w)").rearrange(
        "(a p i) -> a p i", p=128, i=1024)
    for t in range(128):
        kb.dma("sync", acc2_flat[t], zt[:, :], "mf_zst")
    zstv = kb.sem("mf_zst").val

    kb.wait("gpsimd", "p0_st")
    kb.wait("gpsimd", "p0_zst")
    for ex in range(E_EXP):
        for g in range(NG):
            pv = kb.dma("sync", prt[:, :],
                        pairs_d[ex * CAP + g * 128: ex * CAP + (g + 1) * 128, :],
                        "mf_pld")
            kb.wait("vector", "mf_pld", pv)
            kb.op("vector", lambda e_, o=ids_i[ex][g], i=prt:
                  e_.tensor_copy(o[:, :], i[:, 0:1]))
            kb.op("vector", lambda e_, o=gts[ex][g], i=prt:
                  e_.tensor_copy(o[:, :], i[:, 1:2]), sem="mf_prc")
            kb.wait("sync", "mf_prc")
        prcv = kb.sem("mf_prc").val
        kb.wait("gpsimd", "mf_prc", prcv)
        for g in range(NG):
            bj = g % 2
            if g >= 2:
                kb.wait("gpsimd", "mf_xst")
                kb.wait("vector", "mf_xst")
            gv = kb.idma(xg[bj][:, :], None, xnf_d,
                         IndirectOffsetOnAxis(ap=ids_i[ex][g][:, :], axis=0),
                         "mf_gat")
            kb.wait("vector", "mf_gat", gv)
            kb.op("vector", lambda e_, o=xgb[bj], i=xg[bj]:
                  e_.tensor_copy(o[:, :], i[:, :]), sem="mf_cst")
            cv = kb.sem("mf_cst").val
            kb.wait("scalar", "mf_cst", cv)
            kb.dma("scalar", xin_d[ex, g * 128:(g + 1) * 128, :], xgb[bj][:, :],
                   "mf_xst")
        xsv = kb.sem("mf_xst").val
        kb.wait("sync", "mf_xst", xsv)
        kb.wait("sync", "mf_hmm")
        for c in range(8):
            kb.dmat("sync", xinT[c][:, :],
                    xin_d[ex].rearrange("s d -> s d")[:, c * 128:(c + 1) * 128],
                    "mf_tp")
        tpv = kb.sem("mf_tp").val
        kb.wait("gpsimd", "mf_hmm")
        for c in range(8):
            kb.dma("gpsimd", w1s[c][:, :], w1_d[ex, c * 128:(c + 1) * 128, :],
                   "mf_w1l")
        w1v = kb.sem("mf_w1l").val
        kb.wait("gpsimd", "mf_ymm")
        for fc in range(32):
            kb.dma("gpsimd", w2s[fc][:, :], w2_d[ex, fc * 128:(fc + 1) * 128, :],
                   "mf_w2l")
        w2v = kb.sem("mf_w2l").val
        kb.wait("tensor", "mf_w1l", w1v)
        kb.wait("tensor", "mf_tp", tpv)
        hl_guard = kb.sem("mf_ymm").val
        kb.wait("scalar", "mf_ymm", hl_guard)
        hp_guard = [kb.sem("mf_hev").val] * 2
        for fc in range(32):
            bnk = fc % 2
            kb.wait("tensor", "mf_hev", hp_guard[bnk])
            prs = [(w1s[c][:, fc * 128:(fc + 1) * 128], xinT[c][:, :])
                   for c in range(8)]
            mm_accum(kb, hp[bnk][:, 0:CAP], prs, sem="mf_hmm")
            hv = kb.sem("mf_hmm").val
            kb.wait("sync", "mf_hev")
            bv = kb.dma("sync", b1t[:, :],
                        b1_d[ex, fc * 128:(fc + 1) * 128].rearrange(
                            "(p one) -> p one", one=1), "mf_b1l")
            kb.wait("scalar", "mf_hmm", hv)
            kb.wait("scalar", "mf_b1l", bv)
            kb.op("scalar", lambda e_, o=hT[fc], i=hp[bnk], b_=b1t:
                  e_.activation(o[:, :], i[:, 0:CAP], AF.Gelu_apprx_tanh,
                                bias=b_[:, :]), sem="mf_hev")
            hp_guard[bnk] = kb.sem("mf_hev").val
        hevv = kb.sem("mf_hev").val
        kb.wait("tensor", "mf_w2l", w2v)
        kb.wait("tensor", "mf_hev", hevv)
        yp_guard = [kb.sem("mf_yev").val] * 2
        for dc in range(8):
            bnk = dc % 2
            kb.wait("tensor", "mf_yev", yp_guard[bnk])
            bv = kb.dma("sync", b2t[dc][:, :],
                        b2_d[ex, dc * 128:(dc + 1) * 128].rearrange(
                            "(p one) -> p one", one=1), "mf_b2l")
            prs = [(w2s[fc][:, dc * 128:(dc + 1) * 128], hT[fc][:, :])
                   for fc in range(32)]
            mm_accum(kb, hp[bnk][:, 0:CAP], prs, sem="mf_ymm")
            yv = kb.sem("mf_ymm").val
            kb.wait("vector", "mf_ymm", yv)
            kb.wait("vector", "mf_b2l", bv)
            kb.op("vector", lambda e_, o=yf, i=hp[bnk], b_=b2t[dc]:
                  e_.tensor_scalar(o[:, :], i[:, 0:CAP], b_[:, :], None,
                                   op0=AL.add), sem="mf_yev")
            kb.op("vector", lambda e_, o=yhib, i=yf:
                  e_.tensor_copy(o[:, :], i[:, :]), sem="mf_yev")
            yev = kb.sem("mf_yev").val
            yp_guard[bnk] = yev
            kb.wait("scalar", "mf_yev", yev)
            kb.op("scalar", lambda e_, o=yhif, i=yhib:
                  e_.copy(o[:, :], i[:, :]), sem="mf_yhf")
            hfv = kb.sem("mf_yhf").val
            kb.wait("vector", "mf_yhf", hfv)
            kb.op("vector", lambda e_, o=ylob, i0=yf, i1=yhif:
                  e_.tensor_tensor(o[:, :], i0[:, :], i1[:, :], op=AL.subtract),
                  sem="mf_ylo")
            lv = kb.sem("mf_ylo").val
            kb.dma("scalar", yhi_d[ex, dc * 128:(dc + 1) * 128, :], yhib[:, :],
                   "mf_yst")
            kb.wait("scalar", "mf_ylo", lv)
            kb.dma("scalar", ylo_d[ex, dc * 128:(dc + 1) * 128, :], ylob[:, :],
                   "mf_yst")
            ystv = kb.sem("mf_yst").val
            kb.wait("vector", "mf_yst", ystv)
        kb.wait("sync", "mf_yst")
        for g in range(NG):
            kb.wait("sync", "mf_yg")
            kb.dmat("sync", yth[:, :], yhi_d[ex][:, g * 128:(g + 1) * 128], "mf_yt")
            tv1 = kb.dmat("sync", ytl[:, :], ylo_d[ex][:, g * 128:(g + 1) * 128],
                          "mf_yt")
            kb.wait("vector", "mf_yt", tv1)
            kb.op("vector", lambda e_, o=ytf[g], i0=yth, i1=ytl:
                  e_.tensor_tensor(o[:, :], i0[:, :], i1[:, :], op=AL.add))
            kb.op("vector", lambda e_, o=ytf[g], s=gts[ex][g]:
                  e_.tensor_scalar(o[:, :], o[:, :], s[:, :], None, op0=AL.mult),
                  sem="mf_yg")
            ygv = kb.sem("mf_yg").val
            kb.wait("gpsimd", "mf_yg", ygv)
            if ex == 0 and g == 0:
                kb.wait("gpsimd", "mf_zst", zstv)
            kb.idma(acc2_d, IndirectOffsetOnAxis(ap=ids_i[ex][g][:, :], axis=0),
                    ytf[g][:, :], None, "mf_sca",
                    element_offset=ex * 1024)
        scav = kb.sem("mf_sca").val
        kb.wait("gpsimd", "mf_sca", scav)
        sent_d0 = kb.d["sentinel"].ap()
        sv0 = kb.idma(sjunk[:, :], None, sent_d0,
                      IndirectOffsetOnAxis(ap=zidx[:, :], axis=0), "mf_esent")
        kb.wait("gpsimd", "mf_esent", sv0)
        kb.wait("vector", "mf_sca", scav)
        kb.wait("vector", "mf_esent", sv0)
    sent_d = kb.d["sentinel"].ap()
    sv = kb.idma(sjunk[:, :], None, sent_d,
                 IndirectOffsetOnAxis(ap=zidx[:, :], axis=0), "mf_sent")
    for eng in ENGINES:
        kb.wait(eng, "mf_sent", sv)
    kb.phase_end()


def l0_combine(kb):
    x = kb.d["x"].ap()
    e2hi_d = kb.d["e2hi"].ap()
    e2lo_d = kb.d["e2lo"].ap()
    acc2_d = kb.d["moeacc2"].ap()
    x1_d = kb.d["x1"].ap()
    xn1_d = kb.d["xn1"].ap()
    epst = kb.shared["epst"]

    kb.fence("mf_yst", "mf_xst")
    kb.phase_begin()
    NB = 3
    xt = [kb.sbuf(f"lc_x{i}", [128, 1024], F32) for i in range(NB)]
    ac = [kb.sbuf(f"lc_a{i}", [128, 8192], F32) for i in range(NB)]
    th = [kb.sbuf(f"lc_th{i}", [128, 1024], BF16) for i in range(NB)]
    tl = [kb.sbuf(f"lc_tl{i}", [128, 1024], BF16) for i in range(NB)]
    asum = [kb.sbuf(f"lc_as{i}", [128, 1024], F32) for i in range(NB)]
    x1t = [kb.sbuf(f"lc_x1{i}", [128, 1024], F32) for i in range(NB)]
    xn1t = [kb.sbuf(f"lc_xn{i}", [128, 1024], BF16) for i in range(NB)]
    jk = kb.sbuf("lc_jk", [128, 1024], F32)
    s1 = [kb.sbuf(f"lc_s1{i}", [128, 1], F32) for i in range(NB)]
    s2 = [kb.sbuf(f"lc_s2{i}", [128, 1], F32) for i in range(NB)]
    mu = [kb.sbuf(f"lc_mu{i}", [128, 1], F32) for i in range(NB)]
    rst = [kb.sbuf(f"lc_rs{i}", [128, 1], F32) for i in range(NB)]
    tv = [kb.sbuf(f"lc_tv{i}", [128, 2], F32) for i in range(NB)]

    ld_v = {}
    x1_v = {}
    var_v = {}
    rstd_v = {}
    st_v = {}

    acs_v = {}

    def stage_load(T):
        b = T % NB
        if T >= NB:
            kb.wait("sync", "lc_st", st_v.get(T - NB, 0))
            kb.wait("scalar", "lc_st", st_v.get(T - NB, 0))
            kb.wait("sync", "lc_acs", acs_v.get(T - NB, 0))
        kb.dma("sync", xt[b][:, :], x[T * 128:(T + 1) * 128, :], "lc_ld")
        kb.dma("sync", ac[b][:, :], acc2_d[T * 128:(T + 1) * 128, :], "lc_ld")
        kb.dmat("scalar", th[b][:, :], e2hi_d[:, T * 128:(T + 1) * 128], "lc_tp")
        kb.dmat("scalar", tl[b][:, :], e2lo_d[:, T * 128:(T + 1) * 128], "lc_tp")
        ld_v[T] = (kb.sem("lc_ld").val, kb.sem("lc_tp").val)

    def stage_main(T):
        b = T % NB
        if T >= NB:
            kb.wait("vector", "lc_st", st_v.get(T - NB, 0))
            kb.wait("gpsimd", "lc_st", st_v.get(T - NB, 0))
        kb.wait("gpsimd", "lc_ld", ld_v[T][0])
        kb.op("gpsimd", lambda e, i=ac[b]:
              e.tensor_tensor(i[:, 0:4096], i[:, 0:4096], i[:, 4096:8192],
                              op=AL.add))
        kb.op("gpsimd", lambda e, i=ac[b]:
              e.tensor_tensor(i[:, 0:2048], i[:, 0:2048], i[:, 2048:4096],
                              op=AL.add))
        kb.op("gpsimd", lambda e, i=ac[b], o=asum[b]:
              e.tensor_tensor(o[:, :], i[:, 0:1024], i[:, 1024:2048],
                              op=AL.add), sem="lc_acs")
        acsv = kb.sem("lc_acs").val
        acs_v[T] = acsv
        kb.wait("vector", "lc_ld", ld_v[T][0])
        kb.wait("vector", "lc_tp", ld_v[T][1])
        kb.op("vector", lambda e, o=x1t[b], i0=th[b], i1=tl[b]:
              e.tensor_tensor(o[:, :], i0[:, :], i1[:, :], op=AL.add))
        kb.op("vector", lambda e, o=x1t[b], i=xt[b]:
              e.tensor_tensor(o[:, :], o[:, :], i[:, :], op=AL.add))
        kb.wait("vector", "lc_acs", acsv)
        kb.op("vector", lambda e, o=x1t[b], i=asum[b]:
              e.tensor_tensor(o[:, :], o[:, :], i[:, :], op=AL.add), sem="lc_x1")
        x1_v[T] = kb.sem("lc_x1").val
        kb.op("vector", lambda e, i=x1t[b], a=s1[b]:
              e.tensor_reduce(a[:, :], i[:, :], axis=mybir.AxisListType.X,
                              op=AL.add))
        kb.op("vector", lambda e, i=x1t[b], a=s2[b], j=jk:
              e.scalar_tensor_tensor(j[:, :], i[:, :], 1.0, i[:, :],
                                     op0=AL.mult, op1=AL.mult, accum_out=a[:, :]))
        kb.op("vector", lambda e, s=s1[b], m=mu[b]:
              e.tensor_scalar(m[:, :], s[:, :], 1.0 / D, None, op0=AL.mult))
        kb.op("vector", lambda e, s=s2[b], t=tv[b]:
              e.tensor_scalar(t[:, 0:1], s[:, :], 1.0 / D, None, op0=AL.mult))
        kb.op("vector", lambda e, m=mu[b], t=tv[b]:
              e.tensor_tensor(t[:, 1:2], m[:, :], m[:, :], op=AL.mult))
        kb.op("vector", lambda e, t=tv[b]:
              e.tensor_tensor(t[:, 0:1], t[:, 0:1], t[:, 1:2], op=AL.subtract),
              sem="lc_var")
        var_v[T] = kb.sem("lc_var").val
        kb.wait("scalar", "lc_var", var_v[T])
        kb.op("scalar", lambda e, t=tv[b], ep=epst:
              e.activation(t[:, 1:2], t[:, 0:1], AF.Ln, bias=ep[:, :]))
        kb.op("scalar", lambda e, t=tv[b], r=rst[b]:
              e.activation(r[:, :], t[:, 1:2], AF.Exp, scale=-0.5), sem="lc_rst")
        rstd_v[T] = kb.sem("lc_rst").val

    def stage_out(T):
        b = T % NB
        kb.wait("vector", "lc_rst", rstd_v[T])
        kb.op("vector", lambda e, i=x1t[b], m=mu[b], r=rst[b], o=xn1t[b]:
              e.tensor_scalar(o[:, :], i[:, :], m[:, :], r[:, :],
                              op0=AL.subtract, op1=AL.mult), sem="lc_xn")
        xnv = kb.sem("lc_xn").val
        kb.wait("sync", "lc_x1", x1_v[T])
        kb.dma("sync", x1_d[T * 128:(T + 1) * 128, :], x1t[b][:, :], "lc_st")
        kb.wait("sync", "lc_xn", xnv)
        kb.dma("sync", xn1_d[T * 128:(T + 1) * 128, :], xn1t[b][:, :], "lc_st")
        st_v[T] = kb.sem("lc_st").val

    stage_load(0)
    stage_load(1)
    for T in range(16):
        stage_main(T)
        if T >= 1:
            stage_out(T - 1)
        if T + 2 < 16:
            stage_load(T + 2)
    stage_out(15)
    kb.phase_end()


def l1_attn(kb):
    kb.sem("lb_gmm", producer="tensor")
    kb.sem("lb_rot", producer="tensor")
    kb.sem("lb_sq", producer="scalar")
    kb.fence("lc_st")
    xn1_d = kb.d["xn1"].ap()
    s1hi_d = kb.d["s1hi"].ap()
    s1lo_d = kb.d["s1lo"].ap()
    wq_d = kb.d["l1_wqT"].ap()
    wk_d = kb.d["l1_wkT"].ap()
    wv_d = kb.d["l1_wvT"].ap()
    wo_d = kb.d["l1_woT"].ap()
    cos_d = kb.d["ropecos"].ap()
    sin_d = kb.d["ropesin"].ap()
    rrot2_d = kb.d["rrot2"].ap()
    cmask_d = kb.d["cmask"].ap()
    pad64_d = kb.d["pad64"].ap()
    selh2_d = kb.d["selh2"].ap()
    SC = 0.125

    kb.phase_begin()
    xn1T = [kb.sbuf(f"lb_xt{j}", [128, 2048], BF16) for j in range(8)]
    qro = [kb.sbuf(f"lb_q{j}", [128, 2048], BF16) for j in range(8)]
    kro = [kb.sbuf(f"lb_k{j}", [128, 2048], BF16) for j in range(8)]
    vaug = [kb.sbuf(f"lb_va{m}", [128, 1040], BF16) for m in range(16)]
    expb = [kb.sbuf(f"lb_ex{i}", [128, 512], BF16) for i in range(2)]
    attnT = kb.sbuf("lb_attnT", [128, 4096], BF16)
    wst = [kb.sbuf(f"lb_w{i}", [128, 1024], BF16) for i in range(2)]
    wvs = [kb.sbuf(f"lb_wv{c}", [128, 512], BF16) for c in range(8)]
    wo = [kb.sbuf(f"lb_wo{c}", [128, 1024], BF16) for c in range(8)]
    cosb = kb.sbuf("lb_cos", [128, 2048], F32)
    sinb = kb.sbuf("lb_sin", [128, 2048], F32)
    rrot2 = kb.sbuf("lb_rr", [128, 128], BF16)
    cmask = [kb.sbuf(f"lb_cm{i}", [128, 512], BF16) for i in range(4)]
    pad64 = kb.sbuf("lb_p64", [128, 2], BF16)
    selh2 = kb.sbuf("lb_sh2", [2, 256], F32)
    qtmp = kb.sbuf("lb_qt", [128, 512], BF16)
    r1 = kb.sbuf("lb_r1", [128, 512], F32)
    r2 = kb.sbuf("lb_r2", [128, 512], F32)
    sqt = kb.sbuf("lb_sqt", [128, 512], BF16)
    qmx = [kb.sbuf(f"lb_qmx{j}", [2, 1], F32) for j in range(8)]
    kmx = [kb.sbuf(f"lb_kmx{j}", [2, 1], F32) for j in range(8)]
    t2c = kb.sbuf("lb_t2", [2, 2], F32)
    negm2 = [kb.sbuf(f"lb_nm{j}", [2, 1], F32) for j in range(8)]
    nmax = [kb.sbuf(f"lb_nmax{i}", [128, 1], F32) for i in range(2)]
    sig = kb.sbuf("lb_sig", [1, 512], F32)
    bcs = kb.sbuf("lb_bcs", [64, 512], F32)
    onesf64 = kb.sbuf("lb_ones64", [1, 64], F32)
    sh = kb.sbuf("lb_sh", [128, 512], BF16)
    sl = kb.sbuf("lb_sl", [128, 512], BF16)
    shf = kb.sbuf("lb_shf", [128, 512], F32)
    scp = [kb.psum(f"lb_scp{i}", [128, 512]) for i in range(2)]
    up = [kb.psum(f"lb_up{i}", [65, 512]) for i in range(2)]
    bcp = kb.psum("lb_bcp", [64, 512])
    gp = [kb.psum(f"lb_gp{i}", [128, 512]) for i in range(2)]
    npsh = kb.psum("lb_npsh", [128, 512])
    np2 = npsh[0:2, 0:512]
    nmp = npsh[0:128, 0:1]

    for dst, srcd in ((cosb, cos_d), (sinb, sin_d), (rrot2, rrot2_d),
                      (pad64, pad64_d), (selh2, selh2_d)):
        kb.dma("sync", dst[:, :], srcd[:, :], "lb_cld")
    for i in range(4):
        kb.dma("sync", cmask[i][:, :], cmask_d[i], "lb_cld")
    cldv = kb.sem("lb_cld").val
    for c in range(8):
        kb.dma("gpsimd", wo[c][:, :], wo_d[c * 128:(c + 1) * 128, :], "lb_wld")
    wldv = kb.sem("lb_wld").val
    kb.op("vector", lambda e_, o=onesf64: e_.memset(o[:, :], 1.0), sem="lb_init")
    initv = kb.sem("lb_init").val
    kb.wait("scalar", "lc_st")
    for j in range(8):
        kb.dmat("scalar", xn1T[j][:, :], xn1_d[:, j * 128:(j + 1) * 128], "lb_tp")
    tpv = kb.sem("lb_tp").val
    for eng in ("tensor",):
        kb.wait(eng, "lb_cld", cldv)
        kb.wait(eng, "lb_wld", wldv)
        kb.wait(eng, "lb_init", initv)
        kb.wait(eng, "lb_tp", tpv)
    kb.wait("vector", "lb_cld", cldv)
    kb.wait("scalar", "lb_cld", cldv)

    gp_guard = [0, 0]
    scp_guard = [0, 0]
    up_guard = [0, 0]
    qt_rot = 0
    qt_sq = 0

    for qk in range(2):
        w_d = wq_d if qk == 0 else wk_d
        dstt = qro if qk == 0 else kro
        mxt = qmx if qk == 0 else kmx
        for j in range(8):
            wb = wst[j % 2]
            kb.wait("gpsimd", "lb_gmm")
            for c in range(8):
                kb.dma("gpsimd", wb[:, c * 128:(c + 1) * 128],
                       w_d[c * 128:(c + 1) * 128, j * 128:(j + 1) * 128], "lb_wql")
            wqlv = kb.sem("lb_wql").val
            kb.wait("tensor", "lb_wql", wqlv)
            for nt in range(4):
                bnk = nt % 2
                kb.wait("tensor", "lb_pe", gp_guard[bnk])
                prs = [(wb[:, c * 128:(c + 1) * 128],
                        xn1T[c][:, nt * 512:(nt + 1) * 512]) for c in range(8)]
                mm_accum(kb, gp[bnk][:, :], prs, sem="lb_gmm")
                gv = kb.sem("lb_gmm").val
                kb.wait("vector", "lb_gmm", gv)
                kb.wait("vector", "lb_rot", qt_rot)
                kb.wait("vector", "lb_sq", qt_sq)
                kb.op("vector", lambda e_, o=qtmp, i=gp[bnk]:
                      e_.tensor_copy(o[:, :], i[:, :]), sem="lb_pe")
                gp_guard[bnk] = kb.sem("lb_pe").val
                qev = gp_guard[bnk]
                kb.wait("tensor", "lb_pe", qev)
                kb.op("tensor", lambda e_, ps=scp[0], lh=rrot2, rh=qtmp:
                      e_.matmul(ps[:, :], lh[:, :], rh[:, :], start=True, stop=True),
                      sem="lb_rot")
                rv = kb.sem("lb_rot").val
                qt_rot = rv
                kb.wait("scalar", "lb_pe", qev)
                kb.op("scalar", lambda e_, o=sqt, i=qtmp:
                      e_.activation(o[:, :], i[:, :], AF.Square), sem="lb_sq")
                sqv = kb.sem("lb_sq").val
                qt_sq = sqv
                kb.wait("tensor", "lb_sq", sqv)
                kb.op("tensor", lambda e_, ps=np2, lh=pad64, rh=sqt:
                      e_.matmul(ps, lh[:, :], rh[:, :], start=True, stop=True),
                      sem="lb_nmm")
                nv = kb.sem("lb_nmm").val
                kb.wait("vector", "lb_nmm", nv)
                kb.op("vector", lambda e_, o=t2c, i=np2:
                      e_.tensor_reduce(o[0:2, 0:1], i, axis=mybir.AxisListType.X,
                                       op=AL.max))
                if nt == 0:
                    kb.op("vector", lambda e_, o=mxt[j], i=t2c:
                          e_.tensor_copy(o[:, :], i[0:2, 0:1]), sem="lb_nred")
                else:
                    kb.op("vector", lambda e_, o=mxt[j], i=t2c:
                          e_.tensor_tensor(o[:, :], o[:, :], i[0:2, 0:1], op=AL.max),
                          sem="lb_nred")
                kb.wait("scalar", "lb_nred")
                kb.wait("vector", "lb_rot", rv)
                kb.op("vector", lambda e_, o=r1, i0=qtmp, i1=cosb, nt=nt:
                      e_.tensor_tensor(o[:, :], i0[:, :],
                                       i1[:, nt * 512:(nt + 1) * 512], op=AL.mult))
                kb.op("vector", lambda e_, o=r2, i0=scp[0], i1=sinb, nt=nt:
                      e_.tensor_tensor(o[:, :], i0[:, :],
                                       i1[:, nt * 512:(nt + 1) * 512], op=AL.mult))
                kb.op("vector", lambda e_, o=dstt[j], i0=r1, i1=r2, nt=nt:
                      e_.tensor_tensor(o[:, nt * 512:(nt + 1) * 512], i0[:, :],
                                       i1[:, :], op=AL.add), sem="lb_rope")
                kb.wait("tensor", "lb_rope")
    ropev = kb.sem("lb_rope").val
    kb.wait("tensor", "lb_rope", ropev)

    for half in range(2):
        kb.wait("gpsimd", "lb_vmm")
        for c in range(8):
            kb.dma("gpsimd", wvs[c][:, :],
                   wv_d[c * 128:(c + 1) * 128, half * 512:(half + 1) * 512],
                   "lb_wvl")
        wvv = kb.sem("lb_wvl").val
        kb.wait("tensor", "lb_wvl", wvv)
        for mt in range(16):
            bnk = mt % 2
            kb.wait("tensor", "lb_vev", scp_guard[bnk])
            prs = [(xn1T[c][:, mt * 128:(mt + 1) * 128], wvs[c][:, :])
                   for c in range(8)]
            mm_accum(kb, scp[bnk][:, :], prs, sem="lb_vmm")
            vv = kb.sem("lb_vmm").val
            kb.wait("vector", "lb_vmm", vv)
            dst = vaug[mt].rearrange("p (hh cc) -> p hh cc", cc=65)
            kb.op("vector", lambda e_, d=dst, i=scp[bnk], half=half:
                  e_.tensor_copy(
                      d[:, half * 8:(half + 1) * 8, 0:64],
                      i[:, :].rearrange("p (hh cc) -> p hh cc", cc=64)),
                  sem="lb_vev")
            scp_guard[bnk] = kb.sem("lb_vev").val
    for mt in range(16):
        kb.op("vector", lambda e_, o=vaug[mt]:
              e_.memset(o[:, 64:1040:65], 1.0), sem="lb_vev")
    vev_final = kb.sem("lb_vev").val
    kb.wait("tensor", "lb_vev", vev_final)
    scp_guard = [0, 0]

    kb.wait("vector", "lb_vev")
    for j in range(8):
        kb.op("vector", lambda e_, o=t2c, a=kmx[j], b=qmx[j]:
              e_.tensor_tensor(o[0:2, 1:2], a[:, :], b[:, :], op=AL.mult),
              sem="lb_nm1")
        v1 = kb.sem("lb_nm1").val
        kb.wait("scalar", "lb_nm1", v1)
        kb.op("scalar", lambda e_, o=t2c:
              e_.activation(o[0:2, 0:1], o[0:2, 1:2], AF.Sqrt), sem="lb_nm2")
        v2 = kb.sem("lb_nm2").val
        kb.wait("vector", "lb_nm2", v2)
        kb.op("vector", lambda e_, o=negm2[j], i=t2c:
              e_.tensor_scalar(o[:, :], i[0:2, 0:1], -SC, None, op0=AL.mult),
              sem="lb_nm3")
        kb.wait("scalar", "lb_nm3")

    kb.wait("tensor", "lb_rope", ropev)
    kb.wait("tensor", "lb_nm3")
    for nt in range(4):
        kb.wait("vector", "lb_pj")
        for h in range(16):
            j = h // 2
            h2 = h % 2
            rb = 64 * h2
            kb.op("tensor", lambda e_, ps=nmp, lh=selh2, rh=negm2[j], h2=h2:
                  e_.matmul(ps, lh[:, h2 * 128:(h2 + 1) * 128], rh[:, :],
                            start=True, stop=True), sem="lb_nbc")
            nbv = kb.sem("lb_nbc").val
            kb.wait("vector", "lb_nbc", nbv)
            kb.op("vector", lambda e_, o=nmax[h % 2], i=nmp:
                  e_.tensor_copy(o[:, :], i), sem="lb_nbe")
            nbe = kb.sem("lb_nbe").val
            kb.wait("scalar", "lb_nbe", nbe)
            kb.wait("tensor", "lb_nbe", nbe)
            ub = h % 2
            kb.wait("tensor", "lb_att", up_guard[ub])
            mts = list(range(4 * nt + 4))
            exp_slots = {}
            for mi, mt in enumerate(mts):
                bnk = mt % 2
                kb.wait("tensor", "lb_exp", scp_guard[bnk])
                kb.op("tensor", lambda e_, ps=scp[bnk], j=j, rb=rb, mt=mt, nt=nt:
                      e_.matmul(ps[:, :],
                                kro[j][rb:rb + 64, mt * 128:(mt + 1) * 128],
                                qro[j][rb:rb + 64, nt * 512:(nt + 1) * 512],
                                start=True, stop=True,
                                tile_position=(rb, 0)), sem="lb_sc")
                scv = kb.sem("lb_sc").val
                slot = mt % 2
                kb.wait("scalar", "lb_sc", scv)
                if slot in exp_slots:
                    kb.wait("scalar", "lb_u", exp_slots[slot][0])
                    if exp_slots[slot][1] is not None:
                        kb.wait("scalar", "lb_mk", exp_slots[slot][1])
                kb.op("scalar", lambda e_, o=expb[slot], i=scp[bnk],
                      bi=nmax[h % 2]:
                      e_.activation(o[:, :], i[:, :], AF.Exp,
                                    bias=bi[:, :], scale=SC), sem="lb_exp")
                scp_guard[bnk] = kb.sem("lb_exp").val
                ev = scp_guard[bnk]
                diag = mt >= 4 * nt
                mkv = None
                if diag:
                    kb.wait("vector", "lb_exp", ev)
                    kb.op("vector", lambda e_, o=expb[slot], m=cmask[mt - 4 * nt]:
                          e_.tensor_tensor(o[:, :], o[:, :], m[:, :], op=AL.mult),
                          sem="lb_mk")
                    mkv = kb.sem("lb_mk").val
                    kb.wait("tensor", "lb_mk", mkv)
                else:
                    kb.wait("tensor", "lb_exp", ev)
                kb.op("tensor", lambda e_, ps=up[ub], mt=mt, h=h,
                      st=(mi == 0), sp=(mi == len(mts) - 1):
                      e_.matmul(ps[:, :], vaug[mt][:, 65 * h:65 * h + 65],
                                expb[mt % 2][:, :], start=st, stop=sp),
                      sem="lb_u")
                exp_slots[slot] = (kb.sem("lb_u").val, mkv)
            uv = kb.sem("lb_u").val
            kb.wait("vector", "lb_u", uv)
            kb.op("vector", lambda e_, o=sig, i=up[ub]:
                  e_.reciprocal(o[:, :], i[64:65, :]), sem="lb_rc")
            rcv = kb.sem("lb_rc").val
            kb.wait("tensor", "lb_rc", rcv)
            kb.op("tensor", lambda e_, ps=bcp, lh=onesf64, rh=sig:
                  e_.matmul(ps[:, :], lh[:, :], rh[:, :], start=True, stop=True),
                  sem="lb_bc")
            bcv = kb.sem("lb_bc").val
            kb.wait("vector", "lb_bc", bcv)
            kb.op("vector", lambda e_, o=bcs, i=bcp:
                  e_.tensor_copy(o[:, :], i[:, :]))
            kb.op("vector", lambda e_, o=attnT, u=up[ub], b_=bcs, h=h:
                  e_.scalar_tensor_tensor(
                      o[(h % 2) * 64:(h % 2) * 64 + 64,
                        512 * (h // 2):512 * (h // 2) + 512],
                      u[0:64, :], 1.0, b_[:, :],
                      op0=AL.mult, op1=AL.mult), sem="lb_att")
            up_guard[ub] = kb.sem("lb_att").val
        attv = kb.sem("lb_att").val
        kb.wait("tensor", "lb_att", attv)
        for dc in range(8):
            bnk = dc % 2
            kb.wait("tensor", "lb_pe", gp_guard[bnk])
            prs = [(wo[c][:, dc * 128:(dc + 1) * 128],
                    attnT[:, 512 * c:512 * c + 512]) for c in range(8)]
            mm_accum(kb, gp[bnk][:, :], prs, sem="lb_pj")
            pjv = kb.sem("lb_pj").val
            kb.wait("vector", "lb_pj", pjv)
            kb.wait("vector", "lb_sst")
            kb.op("vector", lambda e_, o=sh, i=gp[bnk]:
                  e_.tensor_copy(o[:, :], i[:, :]), sem="lb_pe")
            hv = kb.sem("lb_pe").val
            kb.wait("scalar", "lb_pe", hv)
            kb.op("scalar", lambda e_, o=shf, i=sh:
                  e_.copy(o[:, :], i[:, :]), sem="lb_shf")
            hfv = kb.sem("lb_shf").val
            kb.wait("vector", "lb_shf", hfv)
            kb.op("vector", lambda e_, o=sl, i0=gp[bnk], i1=shf:
                  e_.tensor_tensor(o[:, :], i0[:, :], i1[:, :], op=AL.subtract),
                  sem="lb_pe")
            gp_guard[bnk] = kb.sem("lb_pe").val
            lv = gp_guard[bnk]
            kb.wait("sync", "lb_pe", lv)
            kb.dma("sync", s1hi_d[dc * 128:(dc + 1) * 128,
                                  nt * 512:(nt + 1) * 512], sh[:, :], "lb_sst")
            kb.dma("sync", s1lo_d[dc * 128:(dc + 1) * 128,
                                  nt * 512:(nt + 1) * 512], sl[:, :], "lb_sst")
    kb.phase_end()


def l1_ffn(kb):
    xn1_d = kb.d["xn1"].ap()
    f1hi_d = kb.d["f1hi"].ap()
    f1lo_d = kb.d["f1lo"].ap()
    wg_d = kb.d["l1_wgT"].ap()
    wu_d = kb.d["l1_wuT"].ap()
    wd_d = kb.d["l1_wdT"].ap()
    KDQ = 1.0 / (127.0 * 127.0)

    kb.fence("lb_sst")
    kb.phase_begin()
    xn1T = [kb.sbuf(f"lf_xt{j}", [128, 2048], BF16) for j in range(8)]
    wg = [kb.sbuf(f"lf_wg{c}", [128, 1024], BF16) for c in range(8)]
    wu = [kb.sbuf(f"lf_wu{c}", [128, 1024], BF16) for c in range(8)]
    wd = [kb.sbuf(f"lf_wd{c}", [128, 1024], BF16) for c in range(8)]
    h8 = [kb.sbuf(f"lf_h{i}", [128, 512], BF16) for i in range(8)]
    gs = kb.sbuf("lf_gs", [128, 512], BF16)
    fsb = [kb.sbuf(f"lf_f{dc}", [128, 2048], F32) for dc in range(8)]
    sh = kb.sbuf("lf_sh", [128, 2048], BF16)
    shf = kb.sbuf("lf_shf", [128, 512], F32)
    sl = kb.sbuf("lf_sl", [128, 2048], BF16)
    gp = [kb.psum(f"lf_gp{i}", [128, 512]) for i in range(2)]
    yp = [kb.psum(f"lf_yp{i}", [128, 512]) for i in range(2)]

    kb.wait("scalar", "lc_st")
    for j in range(8):
        kb.dmat("scalar", xn1T[j][:, :], xn1_d[:, j * 128:(j + 1) * 128], "lf_tp")
    tpv = kb.sem("lf_tp").val
    kb.wait("tensor", "lf_tp", tpv)
    for dc in range(8):
        kb.op("vector", lambda e_, o=fsb[dc]: e_.memset(o[:, :], 0.0), sem="lf_fz")
    fzv = kb.sem("lf_fz").val
    kb.wait("vector", "lf_fz", fzv)

    gp_guard = [0, 0]
    yph_guard = [0, 0]
    ypf_guard = [0, 0]
    h8_guard = 0
    for hb in range(3):
        kb.wait("gpsimd", "lf_gmm")
        kb.wait("gpsimd", "lf_ymm")
        for c in range(8):
            kb.dma("gpsimd", wg[c][:, :],
                   wg_d[c * 128:(c + 1) * 128, hb * 1024:(hb + 1) * 1024], "lf_wl")
            kb.dma("gpsimd", wu[c][:, :],
                   wu_d[c * 128:(c + 1) * 128, hb * 1024:(hb + 1) * 1024], "lf_wl")
            kb.dma("gpsimd", wd[c][:, :],
                   wd_d[hb * 1024 + c * 128: hb * 1024 + (c + 1) * 128, :], "lf_wl")
        wlv = kb.sem("lf_wl").val
        kb.wait("tensor", "lf_wl", wlv)
        for nt in range(4):
            for hc in range(8):
                bnk = hc % 2
                kb.wait("tensor", "lf_hev", gp_guard[bnk])
                prs = [(wg[c][:, hc * 128:(hc + 1) * 128],
                        xn1T[c][:, nt * 512:(nt + 1) * 512]) for c in range(8)]
                mm_accum(kb, gp[bnk][:, :], prs, sem="lf_gmm")
                kb.wait("tensor", "lf_hev", yph_guard[bnk])
                kb.wait("tensor", "lf_fadd", ypf_guard[bnk])
                prs = [(wu[c][:, hc * 128:(hc + 1) * 128],
                        xn1T[c][:, nt * 512:(nt + 1) * 512]) for c in range(8)]
                mm_accum(kb, yp[bnk][:, :], prs, sem="lf_gmm")
                gv = kb.sem("lf_gmm").val
                kb.wait("scalar", "lf_gmm", gv)
                kb.op("scalar", lambda e_, o=gs, i=gp[bnk]:
                      e_.activation(o[:, :], i[:, :], AF.Silu, scale=1.0 / 127.0),
                      sem="lf_sil")
                sv = kb.sem("lf_sil").val
                kb.wait("vector", "lf_sil", sv)
                kb.wait("vector", "lf_ymm", h8_guard)
                kb.op("vector", lambda e_, o=h8[hc], i0=yp[bnk], g_=gs:
                      e_.scalar_tensor_tensor(o[:, :], i0[:, :], KDQ, g_[:, :],
                                              op0=AL.mult, op1=AL.mult),
                      sem="lf_hev")
                gp_guard[bnk] = kb.sem("lf_hev").val
                yph_guard[bnk] = gp_guard[bnk]
            hev = kb.sem("lf_hev").val
            kb.wait("tensor", "lf_hev", hev)
            for dc in range(8):
                bnk = dc % 2
                kb.wait("tensor", "lf_fadd", ypf_guard[bnk])
                prs = [(wd[hc][:, dc * 128:(dc + 1) * 128], h8[hc][:, :])
                       for hc in range(8)]
                mm_accum(kb, yp[bnk][:, :], prs, sem="lf_ymm")
                yv = kb.sem("lf_ymm").val
                kb.wait("vector", "lf_ymm", yv)
                kb.op("vector", lambda e_, o=fsb[dc], i=yp[bnk], nt=nt:
                      e_.tensor_tensor(o[:, nt * 512:(nt + 1) * 512], i[:, :],
                                       o[:, nt * 512:(nt + 1) * 512], op=AL.add),
                      sem="lf_fadd")
                ypf_guard[bnk] = kb.sem("lf_fadd").val
            h8_guard = kb.sem("lf_ymm").val
    fav = kb.sem("lf_fadd").val
    kb.wait("vector", "lf_fadd", fav)
    kb.wait("scalar", "lf_fadd", fav)
    st_rec = {}
    for dc in range(8):
        kb.op("vector", lambda e_, o=sh, i=fsb[dc]:
              e_.tensor_copy(o[:, :], i[:, :]), sem="lf_hi")
        hv = kb.sem("lf_hi").val
        kb.wait("scalar", "lf_hi", hv)
        for n in range(4):
            kb.op("scalar", lambda e_, o=shf, i=sh, n=n:
                  e_.copy(o[:, :], i[:, n * 512:(n + 1) * 512]), sem="lf_hf")
            hfv = kb.sem("lf_hf").val
            kb.wait("vector", "lf_hf", hfv)
            kb.op("vector", lambda e_, o=sl, i=fsb[dc], hf=shf, n=n:
                  e_.tensor_tensor(o[:, n * 512:(n + 1) * 512],
                                   i[:, n * 512:(n + 1) * 512], hf[:, :],
                                   op=AL.subtract), sem="lf_lo")
            lv = kb.sem("lf_lo").val
            kb.wait("scalar", "lf_lo", lv)
        lov = kb.sem("lf_lo").val
        kb.wait("sync", "lf_hi", hv)
        kb.dma("sync", f1hi_d[dc * 128:(dc + 1) * 128, :], sh[:, :], "lf_st")
        kb.wait("sync", "lf_lo", lov)
        kb.dma("sync", f1lo_d[dc * 128:(dc + 1) * 128, :], sl[:, :], "lf_st")
        stv = kb.sem("lf_st").val
        kb.wait("vector", "lf_st", stv)
    kb.phase_end()


def final_out(kb):
    x1_d = kb.d["x1"].ap()
    s1hi_d = kb.d["s1hi"].ap()
    s1lo_d = kb.d["s1lo"].ap()
    f1hi_d = kb.d["f1hi"].ap()
    f1lo_d = kb.d["f1lo"].ap()
    out_d = kb.d["out"].ap()

    kb.fence("lf_st")
    kb.phase_begin()
    NB = 2
    xt = [kb.sbuf(f"fo_x{i}", [128, 1024], F32) for i in range(NB)]
    th = [kb.sbuf(f"fo_th{i}", [128, 1024], BF16) for i in range(NB)]
    tl = [kb.sbuf(f"fo_tl{i}", [128, 1024], BF16) for i in range(NB)]
    fh = [kb.sbuf(f"fo_fh{i}", [128, 1024], BF16) for i in range(NB)]
    fl = [kb.sbuf(f"fo_fl{i}", [128, 1024], BF16) for i in range(NB)]
    ot = [kb.sbuf(f"fo_o{i}", [128, 1024], F32) for i in range(NB)]
    kb.wait("sync", "lc_st")
    kb.wait("scalar", "lb_sst")
    kb.wait("scalar", "lf_st")
    st_after = []
    for t in range(16):
        b = t % NB
        if t >= NB:
            for eng in ("sync", "scalar", "vector"):
                kb.wait(eng, "fo_st", st_after[t - NB])
        ldv0 = kb.dma("sync", xt[b][:, :], x1_d[t * 128:(t + 1) * 128, :], "fo_ld")
        kb.dmat("scalar", th[b][:, :], s1hi_d[:, t * 128:(t + 1) * 128], "fo_tp")
        kb.dmat("scalar", tl[b][:, :], s1lo_d[:, t * 128:(t + 1) * 128], "fo_tp")
        kb.dmat("scalar", fh[b][:, :], f1hi_d[:, t * 128:(t + 1) * 128], "fo_tp")
        ldv = kb.dmat("scalar", fl[b][:, :], f1lo_d[:, t * 128:(t + 1) * 128],
                      "fo_tp")
        kb.wait("vector", "fo_ld", ldv0)
        kb.wait("vector", "fo_tp", ldv)
        kb.op("vector", lambda e_, o=ot[b], i0=th[b], i1=tl[b]:
              e_.tensor_tensor(o[:, :], i0[:, :], i1[:, :], op=AL.add))
        kb.op("vector", lambda e_, o=ot[b], i=xt[b]:
              e_.tensor_tensor(o[:, :], o[:, :], i[:, :], op=AL.add))
        kb.op("vector", lambda e_, o=ot[b], i0=fh[b], i1=fl[b], x_=xt[b]:
              e_.tensor_tensor(x_[:, :], i0[:, :], i1[:, :], op=AL.add))
        kb.op("vector", lambda e_, o=ot[b], x_=xt[b]:
              e_.tensor_tensor(o[:, :], o[:, :], x_[:, :], op=AL.add), sem="fo_s")
        sv = kb.sem("fo_s").val
        kb.wait("sync", "fo_s", sv)
        kb.dma("sync", out_d[t * 128:(t + 1) * 128, :], ot[b][:, :], "fo_st")
        st_after.append(kb.sem("fo_st").val)
    kb.phase_end()


ALL_PHASES = [p0_prep, routing_logits, routing_select, l0_attn, moe_ffn,
              l0_combine, l1_attn, l1_ffn, final_out]


def kernel(**inputs):
    kb = build_program(ALL_PHASES)
    in_maps = [prep_host_inputs(inputs, c) for c in range(8)]
    res = run_bass_kernel_spmd(kb.nc, in_maps, core_ids=list(range(8)))
    out = np.stack([np.asarray(res.results[c]["out"]) for c in range(8)], axis=0)
    return out.astype(np.float32)


# version 06
# speedup vs baseline: 1.0336x; 1.0336x over previous
import math
from contextlib import ExitStack

import numpy as np
import ml_dtypes
import concourse.bass as bass
import concourse.mybir as mybir
from concourse.bass import IndirectOffsetOnAxis
from concourse.bass_utils import run_bass_kernel_spmd

F32 = mybir.dt.float32
BF16 = mybir.dt.bfloat16
I32 = mybir.dt.int32
U32 = mybir.dt.uint32
AF = mybir.ActivationFunctionType
AL = mybir.AluOpType

B, S, D = 8, 2048, 1024
NT = B * S
E_EXP = 8
K_SEL = NT // E_EXP
CAP = 384
F_MOE = 4096
H_FF1 = 3072
EPS = 1e-5
BISECT = 34
NH0 = 16
NH1 = 16
OOB = 1.0e9

DT_NBYTES = {F32: 4, BF16: 2, I32: 4, U32: 4}


ENGINES = ("sync", "scalar", "vector", "tensor", "gpsimd")


class Sem:
    def __init__(self, handle):
        self.h = handle
        self.val = 0


class KB:

    def __init__(self, nc, stack):
        self.nc = nc
        self.stack = stack
        self.ops = {e: [] for e in ENGINES}
        self._sems = {}
        self._bind = {}
        self.d = {}
        self.phase_stack = None

    MIXED = {"p0_lo", "p0_var", "p0_ws", "la_exp", "la_att", "lc_var", "rs_bis",
             "la_pe", "mf_yev"}

    def _dedicated(self, key):
        if key not in self._sems:
            h = self.stack.enter_context(self.nc.semaphore(key))
            self._sems[key] = Sem(h)
        return self._sems[key]

    def sem(self, name, producer=None):
        if name in self.MIXED:
            return self._dedicated(name)
        if name in self._bind:
            s, prod = self._bind[name]
            if producer is not None and prod is None:
                self._bind[name] = (s, producer)
            elif producer is not None and prod != producer:
                raise AssertionError(
                    f"sem {name} has two producers ({prod}, {producer}); "
                    f"add it to KB.MIXED")
            return s
        if producer is None or producer.startswith("dma:"):
            s = self._dedicated(name)
            self._bind[name] = (s, producer)
            return s
        prefix = name.split("_")[0]
        s = self._dedicated(f"&{prefix}_{producer}")
        self._bind[name] = (s, producer)
        return s

    def dma(self, q, out, in_, sem, **kw):
        s = self.sem(sem, producer="dma:" + q)
        s.val += 16
        tgt = s.val

        def fn(eng, out=out, in_=in_, h=s.h, kw=kw):
            eng.dma_start(out=out, in_=in_, **kw).then_inc(h, 16)

        self.ops[q].append(fn)
        return tgt

    def dmat(self, q, out, in_, sem):
        s = self.sem(sem, producer="dma:" + q)
        s.val += 16
        tgt = s.val

        def fn(eng, out=out, in_=in_, h=s.h):
            eng.dma_start_transpose(out, in_)
        self.ops[q].append(lambda eng, out=out, in_=in_, h=s.h:
                           eng.dma_start_transpose(out, in_).then_inc(h, 16))
        return tgt

    def idma(self, out, out_off, in_, in_off, sem, **kw):
        s = self.sem(sem, producer="dma:gpsimd")
        s.val += 16
        tgt = s.val

        def fn(eng, out=out, oo=out_off, in_=in_, io=in_off, h=s.h, kw=kw):
            eng.indirect_dma_start(out, oo, in_, io, oob_is_err=False, **kw).then_inc(h, 16)

        self.ops["gpsimd"].append(fn)
        return tgt

    def op(self, engq, fn, sem=None, inc=1, nodrain=False):
        dr = (engq in ("scalar", "vector", "gpsimd")) and not nodrain
        if sem is not None:
            s = self.sem(sem, producer=engq)
            s.val += inc
            tgt = s.val

            def wrapped(eng, fn=fn, h=s.h, inc=inc, dr=dr):
                if dr:
                    eng.drain()
                ins = fn(eng)
                ins.then_inc(h, inc)

            self.ops[engq].append(wrapped)
            return tgt

        def plain(eng, fn=fn, dr=dr):
            if dr:
                eng.drain()
            fn(eng)

        self.ops[engq].append(plain)
        return None

    def wait(self, engq, sem, val=None):
        s = self.sem(sem)
        v = s.val if val is None else val
        self.ops[engq].append(lambda eng, h=s.h, v=v: eng.wait_ge(h, v))
        return v

    def barrier(self, name, engines=ENGINES):
        s = self.sem(name + "_bar")
        n = len(engines)
        base = s.val
        for e in engines:
            def fn(eng, h=s.h):
                eng.nop().then_inc(h, 1)
            self.ops[e].append(fn)
        s.val += n
        for e in engines:
            self.ops[e].append(lambda eng, h=s.h, v=base + n: eng.wait_ge(h, v))

    def dram(self, name, shape, dtype, kind="Internal"):
        t = self.nc.dram_tensor(name, list(shape), dtype, kind=kind)
        self.d[name] = t
        return t

    def phase_begin(self):
        assert self.phase_stack is None
        self.phase_stack = ExitStack()

    def phase_end(self):
        self.phase_stack.close()
        self.phase_stack = None

    def sbuf(self, name, shape, dtype):
        st = self.phase_stack if self.phase_stack is not None else self.stack
        return st.enter_context(self.nc.sbuf_tensor(name, list(shape), dtype))

    def psum(self, name, shape, dtype=F32):
        st = self.phase_stack if self.phase_stack is not None else self.stack
        return st.enter_context(self.nc.psum_tensor(name, list(shape), dtype))

    def finalize_block(self):
        nc = self.nc
        ops = self.ops
        with nc.Block() as block:
            @block.sync
            def _(eng):
                for f in ops["sync"]:
                    f(eng)

            @block.scalar
            def _(eng):
                for f in ops["scalar"]:
                    f(eng)

            @block.vector
            def _(eng):
                for f in ops["vector"]:
                    f(eng)

            @block.tensor
            def _(eng):
                for f in ops["tensor"]:
                    f(eng)

            @block.gpsimd
            def _(eng):
                for i, f in enumerate(ops["gpsimd"]):
                    try:
                        f(eng)
                    except Exception:
                        import sys
                        cv = getattr(f, "__defaults__", None) or ()
                        print(f"gpsimd op {i}/{len(ops['gpsimd'])} failed: "
                              f"{getattr(f, '__qualname__', f)} defaults={cv}",
                              file=sys.stderr)
                        raise

    def fence(self, *sems):
        for s in sems:
            for eng in ("sync", "scalar", "vector", "gpsimd"):
                self.wait(eng, s)

    def sbuf_global(self, name, shape, dtype):
        t = self.stack.enter_context(self.nc.sbuf_tensor(name, list(shape), dtype))
        if not hasattr(self, "shared"):
            self.shared = {}
        self.shared[name] = t
        return t


def mm_accum(kb, ps_ap, pairs, sem=None, inc=1, tile_position=None):
    n = len(pairs)
    ret = None
    for i, (lh, rh) in enumerate(pairs):
        last = i == n - 1

        def fn(eng, ps=ps_ap, lh=lh, rh=rh, st=(i == 0), sp=last, tp=tile_position):
            return eng.matmul(ps, lh, rh, start=st, stop=sp, tile_position=tp)

        if last and sem is not None:
            ret = kb.op("tensor", fn, sem=sem, inc=inc)
        else:
            kb.op("tensor", fn)
    return ret


def p0_prep(kb):
    x = kb.d["x"].ap()
    xhi = kb.d["xhi"].ap()
    xlo = kb.d["xlo"].ap()
    mu_d = kb.d["mu"].ap()
    rstd_d = kb.d["rstd"].ap()
    xnb = kb.d["xnb"].ap()
    xnf = kb.d["xnf"].ap()
    ws = kb.d["ws"].ap()

    wsx = [kb.sbuf_global(f"wsx{c}", [128, 16], BF16) for c in range(8)]
    ones128f = kb.sbuf_global("ones128f", [128, 1], F32)
    epst = kb.sbuf_global("epst", [128, 1], F32)
    kb.op("vector", lambda e, o=ones128f: e.memset(o[:, :], 1.0))
    kb.op("vector", lambda e, o=epst: e.memset(o[:, :], EPS), sem="g_ones", inc=2)

    kb.phase_begin()
    NB = 4
    xt = [kb.sbuf(f"p0_x{i}", [128, 1024], F32) for i in range(NB)]
    hib = [kb.sbuf(f"p0_h{i}", [128, 1024], BF16) for i in range(NB)]
    lob = [kb.sbuf(f"p0_l{i}", [128, 1024], BF16) for i in range(NB)]
    xnbt = [kb.sbuf(f"p0_xnb{i}", [128, 1024], BF16) for i in range(NB)]
    xnft = [kb.sbuf(f"p0_xnf{i}", [128, 1024], F32) for i in range(NB)]
    junk_v = kb.sbuf("p0_jv", [128, 1024], BF16)
    junk_s = kb.sbuf("p0_js", [128, 1024], F32)
    s1 = [kb.sbuf(f"p0_s1{i}", [128, 1], F32) for i in range(NB)]
    s2 = [kb.sbuf(f"p0_s2{i}", [128, 1], F32) for i in range(NB)]
    mu = [kb.sbuf(f"p0_mu{i}", [128, 1], F32) for i in range(NB)]
    rst = [kb.sbuf(f"p0_rs{i}", [128, 1], F32) for i in range(NB)]
    tv = [kb.sbuf(f"p0_tv{i}", [128, 2], F32) for i in range(NB)]
    wsf = kb.sbuf("p0_wsf", [128, 8], F32)
    wsh = kb.sbuf("p0_wsh", [128, 8], F32)

    for c in range(8):
        v = kb.dma("sync", wsf[:, :], ws[c * 128:(c + 1) * 128, :], "p0_wsld")
        kb.wait("scalar", "p0_wsld", v)
        kb.op("scalar", lambda e, o=wsx[c], i=wsf: e.copy(o[:, 0:8], i[:, :]))
        kb.op("scalar", lambda e, o=wsh, i=wsx[c]: e.copy(o[:, :], i[:, 0:8]),
              sem="p0_ws")
        wv = kb.sem("p0_ws").val
        kb.wait("vector", "p0_ws", wv)
        kb.op("vector", lambda e, o=wsx[c], i0=wsf, i1=wsh:
              e.tensor_tensor(o[:, 8:16], i0[:, :], i1[:, :], op=AL.subtract),
              sem="p0_ws")
        kb.wait("sync", "p0_ws")

    ld_v = {}
    hi_v = {}
    lo_v = {}
    var_v = {}
    rstd_v = {}
    st_v = {}

    def stage_load(T):
        b = T % NB
        if T >= NB:
            kb.wait("sync", "p0_st", st_v.get(T - NB, 0))
            kb.wait("sync", "p0_st2", st2_v.get(T - NB, 0))
            kb.wait("sync", "p0_sq", sq_v.get(T - NB, 0))
        ld_v[T] = kb.dma("sync", xt[b][:, :], x[T * 128:(T + 1) * 128, :], "p0_ld")

    st2_v = {}

    sq_v = {}

    def stage_main(T):
        b = T % NB
        if T >= NB:
            kb.wait("scalar", "p0_st2", st2_v.get(T - NB, 0))
            kb.wait("scalar", "p0_var", var_v.get(T - NB, 0))
            kb.wait("gpsimd", "p0_st", st_v.get(T - NB, 0))
            kb.wait("vector", "p0_st", st_v.get(T - NB, 0))
            kb.wait("vector", "p0_st2", st2_v.get(T - NB, 0))
        kb.wait("vector", "p0_ld", ld_v[T])
        kb.op("vector", lambda e, o=hib[b], i=xt[b], a=s1[b]:
              e.tensor_scalar(o[:, :], i[:, :], 1.0, 0.0, op0=AL.mult, op1=AL.add,
                              accum_out=a[:, :]), sem="p0_hi")
        hi_v[T] = kb.sem("p0_hi").val
        kb.wait("scalar", "p0_ld", ld_v[T])
        if T == 0:
            kb.wait("scalar", "g_ones", 2)
        kb.op("scalar", lambda e, i=xt[b], a=s2[b], j=junk_s:
              e.activation(j[:, :], i[:, :], AF.Square, accum_out=a[:, :]),
              sem="p0_sq")
        sq_v[T] = kb.sem("p0_sq").val
        kb.wait("gpsimd", "p0_ld", ld_v[T])
        kb.wait("gpsimd", "p0_hi", hi_v[T])
        kb.op("gpsimd", lambda e, o=lob[b], i0=xt[b], i1=hib[b]:
              e.tensor_tensor(o[:, :], i0[:, :], i1[:, :], op=AL.subtract),
              sem="p0_lo")
        lo_v[T] = kb.sem("p0_lo").val
        kb.wait("vector", "p0_sq", sq_v[T])
        kb.op("vector", lambda e, s=s1[b], m=mu[b]:
              e.tensor_scalar(m[:, :], s[:, :], 1.0 / D, None, op0=AL.mult))
        kb.op("vector", lambda e, s=s2[b], t=tv[b]:
              e.tensor_scalar(t[:, 0:1], s[:, :], 1.0 / D, None, op0=AL.mult))
        kb.op("vector", lambda e, m=mu[b], t=tv[b]:
              e.tensor_tensor(t[:, 1:2], m[:, :], m[:, :], op=AL.mult))
        kb.op("vector", lambda e, t=tv[b]:
              e.tensor_tensor(t[:, 0:1], t[:, 0:1], t[:, 1:2], op=AL.subtract),
              sem="p0_var")
        var_v[T] = kb.sem("p0_var").val

    def stage_rstd(T):
        b = T % NB
        kb.wait("scalar", "p0_var", var_v[T])
        kb.op("scalar", lambda e, t=tv[b], ep=epst:
              e.activation(t[:, 1:2], t[:, 0:1], AF.Ln, bias=ep[:, :]))
        kb.op("scalar", lambda e, t=tv[b], r=rst[b]:
              e.activation(r[:, :], t[:, 1:2], AF.Exp, scale=-0.5), sem="p0_rst")
        rstd_v[T] = kb.sem("p0_rst").val

    def stage_out(T):
        b = T % NB
        mine = T < 16
        if mine:
            kb.wait("vector", "p0_rst", rstd_v[T])
            kb.op("vector", lambda e, i=xt[b], m=mu[b], r=rst[b], o=xnbt[b]:
                  e.tensor_scalar(o[:, :], i[:, :], m[:, :], r[:, :],
                                  op0=AL.subtract, op1=AL.mult))
            kb.op("vector", lambda e, i=xt[b], m=mu[b], r=rst[b], o=xnft[b]:
                  e.tensor_scalar(o[:, :], i[:, :], m[:, :], r[:, :],
                                  op0=AL.subtract, op1=AL.mult), sem="p0_xn")
        kb.wait("gpsimd", "p0_hi", hi_v[T])
        kb.dma("gpsimd", xhi[T * 128:(T + 1) * 128, :], hib[b][:, :], "p0_st")
        kb.dma("gpsimd", xlo[T * 128:(T + 1) * 128, :], lob[b][:, :], "p0_st")
        st_v[T] = kb.sem("p0_st").val
        kb.wait("sync", "p0_rst", rstd_v[T])
        kb.dma("sync", mu_d[T * 128:(T + 1) * 128, :], mu[b][:, :], "p0_st2")
        kb.dma("sync", rstd_d[T * 128:(T + 1) * 128, :], rst[b][:, :], "p0_st2")
        if mine:
            kb.wait("sync", "p0_xn")
            kb.dma("sync", xnb[T * 128:(T + 1) * 128, :], xnbt[b][:, :], "p0_st2")
            kb.dma("sync", xnf[T * 128:(T + 1) * 128, :], xnft[b][:, :], "p0_st2")
        st2_v[T] = kb.sem("p0_st2").val

    stage_load(0)
    stage_load(1)
    stage_main(0)
    for T in range(1, 128):
        if T + 1 < 128:
            stage_load(T + 1)
        stage_main(T)
        stage_rstd(T - 1)
        stage_out(T - 1)
    stage_rstd(127)
    stage_out(127)

    kb.wait("vector", "p0_sq", sq_v[127])
    kb.op("vector", lambda e_, o=junk_v: e_.memset(o[:, :], 0.0), sem="p0_z")
    kb.wait("gpsimd", "p0_z")
    kb.dma("gpsimd", xnf[S:S + 1, :], junk_v[0:1, :], "p0_zst")
    kb.phase_end()


def routing_logits(kb):
    kb.sem("rt_log", producer="tensor")
    kb.fence("p0_st", "p0_st2", "p0_zst")
    xhi = kb.d["xhi"].ap()
    xlo = kb.d["xlo"].ap()
    mu_d = kb.d["mu"].ap()
    rstd_d = kb.d["rstd"].ap()
    logits_d = kb.d["logits"].ap()
    ws = kb.d["ws"].ap()
    comb17_d = kb.d["comb17"].ap()
    bs_d = kb.d["bs"].ap()

    kb.phase_begin()
    for eng in ("sync", "scalar"):
        kb.wait(eng, "p0_st")
        kb.wait(eng, "p0_st2")
    hiT = [kb.sbuf(f"rt_hiT{c}", [128, 2048], BF16) for c in range(8)]
    loT = [kb.sbuf(f"rt_loT{c}", [128, 2048], BF16) for c in range(8)]
    stack17 = kb.sbuf("rt_stack", [17, 2048], F32)
    comb = kb.sbuf("rt_comb", [17, 8], F32)
    csneg = kb.sbuf("rt_csneg", [1, 8], F32)
    rrow = kb.sbuf("rt_rrow", [1, 2048], F32)
    bc_sb = kb.sbuf("rt_bc", [8, 2048], F32)
    tmul = kb.sbuf("rt_tmul", [8, 2048], F32)
    logt = kb.sbuf("rt_logt", [8, 2048], F32)
    bs_sb = kb.sbuf("rt_bs", [8, 1], F32)
    ones1 = kb.sbuf("rt_ones1", [1, 8], F32)
    wsf32 = kb.sbuf("rt_wsf", [128, 8], F32)
    rawp = kb.psum("rt_rawp", [16, 512])
    logp = kb.psum("rt_logp", [8, 512])
    bcp = kb.psum("rt_bcp", [8, 512])
    csp = kb.psum("rt_csp", [1, 8])
    wsx = [kb.shared[f"wsx{c}"] for c in range(8)]
    ones128f = kb.shared["ones128f"]

    kb.dma("sync", comb[0:17, :], comb17_d[:, :], "rt_ld")
    kb.dma("sync", bs_sb[:, :], bs_d[:, :], "rt_ld")
    ldv = kb.sem("rt_ld").val
    kb.op("vector", lambda e, o=ones1: e.memset(o[:, :], 1.0), sem="rt_ones")
    onev = kb.sem("rt_ones").val

    for c in range(8):
        vv = kb.dma("sync", wsf32[:, :], ws[c * 128:(c + 1) * 128, :], "rt_wsld")
        kb.wait("tensor", "rt_wsld", vv)
        if c == 0:
            kb.wait("tensor", "g_ones", 1)
        kb.op("tensor", lambda e, ps=csp, lh=ones128f, rh=wsf32, st=(c == 0), sp=(c == 7):
              e.matmul(ps[:, :], lh[:, 0:1], rh[:, :], start=st, stop=sp),
              sem="rt_cs" if c == 7 else "rt_wsmm")
        if c < 7:
            kb.wait("sync", "rt_wsmm")
    csv = kb.sem("rt_cs").val
    kb.wait("vector", "rt_cs", csv)
    kb.op("vector", lambda e, o=csneg, i=csp:
          e.tensor_scalar(o[:, :], i[:, :], -1.0, None, op0=AL.mult), sem="rt_csn")
    csnv = kb.sem("rt_csn").val
    kb.wait("sync", "rt_csn", csnv)
    kb.wait("sync", "rt_ld", ldv)
    kb.dma("sync", comb[16:17, :], csneg[:, :], "rt_combst")
    combv = kb.sem("rt_combst").val

    for r in range(8):
        tv = None
        for c in range(8):
            kb.dmat("scalar", hiT[c][:, :],
                    xhi[r * 2048:(r + 1) * 2048, c * 128:(c + 1) * 128], "rt_tp")
            tv = kb.dmat("scalar", loT[c][:, :],
                         xlo[r * 2048:(r + 1) * 2048, c * 128:(c + 1) * 128], "rt_tp")
        kb.wait("sync", "rt_log")
        mv = kb.dma("sync", stack17[16:17, :],
                    mu_d[r * 2048:(r + 1) * 2048, :].rearrange("n one -> one n"),
                    "rt_murow")
        rv = kb.dma("sync", rrow[:, :],
                    rstd_d[r * 2048:(r + 1) * 2048, :].rearrange("n one -> one n"),
                    "rt_murow")
        kb.wait("tensor", "rt_tp", tv)
        if r == 0:
            kb.wait("tensor", "rt_combst", combv)
            kb.wait("tensor", "g_ones", 1)
            kb.wait("tensor", "rt_ones", onev)
        kb.wait("tensor", "rt_murow", rv)
        for n in range(4):
            prs = [(wsx[c][:, :], hiT[c][:, n * 512:(n + 1) * 512]) for c in range(8)]
            prs += [(wsx[c][:, :], loT[c][:, n * 512:(n + 1) * 512]) for c in range(8)]
            rawv = mm_accum(kb, rawp[:, :], prs, sem="rt_raw")
            kb.wait("vector", "rt_raw", rawv)
            kb.op("vector", lambda e, o=stack17, i=rawp, n=n:
                  e.tensor_copy(o[0:16, n * 512:(n + 1) * 512], i[:, :]), sem="rt_evac")
            ev = kb.sem("rt_evac").val
            kb.wait("tensor", "rt_evac", ev)
            kb.op("tensor", lambda e, ps=logp, lh=comb, rh=stack17, n=n:
                  e.matmul(ps[:, :], lh[:, :], rh[:, n * 512:(n + 1) * 512],
                           start=True, stop=True))
            kb.op("tensor", lambda e, ps=bcp, lh=ones1, rh=rrow, n=n:
                  e.matmul(ps[:, :], lh[:, :], rh[:, n * 512:(n + 1) * 512],
                           start=True, stop=True), sem="rt_log")
            lgv = kb.sem("rt_log").val
            kb.wait("vector", "rt_log", lgv)
            kb.op("vector", lambda e, o=bc_sb, i=bcp, n=n:
                  e.tensor_copy(o[:, n * 512:(n + 1) * 512], i[:, :]))
            kb.op("vector", lambda e, o=tmul, i0=logp, i1=bc_sb, n=n:
                  e.tensor_tensor(o[:, n * 512:(n + 1) * 512], i0[:, :],
                                  i1[:, n * 512:(n + 1) * 512], op=AL.mult))
            kb.op("vector", lambda e, o=logt, i=tmul, s=bs_sb, n=n:
                  e.tensor_scalar(o[:, n * 512:(n + 1) * 512],
                                  i[:, n * 512:(n + 1) * 512], s[:, :], None,
                                  op0=AL.add), sem="rt_fix")
        fv = kb.sem("rt_fix").val
        kb.wait("sync", "rt_fix", fv)
        stv = kb.dma("sync", logits_d[r, :, :], logt[:, :], "rt_lst")
        kb.wait("scalar", "rt_raw")
        kb.wait("vector", "rt_lst", stv)

    kb.phase_end()


def routing_select(kb):
    logits_d = kb.d["logits"].ap()
    probs_d = kb.d["probs"].ap()
    pairs_d = kb.d["pairs"].ap()
    s16_d = kb.d["s16"].ap()
    r16_d = kb.d["r16"].ap()
    rep16_d = kb.d["rep16"].ap()
    grp16_d = kb.d["grp16"].ap()
    sel8_d = kb.d["sel8"].ap()
    lt128_d = kb.d["lt128"].ap()
    jj16_d = kb.d["jj16"].ap()
    iota16_d = kb.d["iota16"].ap()
    iota1_d = kb.d["iota1"].ap()

    kb.fence("rt_lst")
    kb.phase_begin()
    logL = kb.sbuf("rs_logL", [128, 1024], F32)
    expL = kb.sbuf("rs_expL", [128, 1024], F32)
    probL = kb.sbuf("rs_probL", [128, 1024], F32)
    sigr = kb.sbuf("rs_sigr", [16, 1024], F32)
    s16 = kb.sbuf("rs_s16", [128, 16], F32)
    r16 = kb.sbuf("rs_r16", [16, 128], F32)
    rep16 = kb.sbuf("rs_rep16", [8, 128], F32)
    grp16 = kb.sbuf("rs_grp16", [128, 8], F32)
    sel8 = kb.sbuf("rs_sel8", [8, 1024], F32)
    lt128 = kb.sbuf("rs_lt", [128, 128], F32)
    jj16 = kb.sbuf("rs_jj", [128, 16], F32)
    iota16 = kb.sbuf("rs_iota", [128, 1], F32)
    iota1 = kb.sbuf("rs_iota1", [128, 1], F32)
    fillt = kb.sbuf("rs_fill", [128, 48], F32)
    dumpc = kb.sbuf("rs_dump", [128, 1], F32)
    ojj = kb.sbuf("rs_ojj", [128, 16], F32)
    okv = kb.sbuf("rs_okv", [128, 16], F32)
    okvi = kb.sbuf("rs_okvi", [128, 16], U32)
    lohi = kb.sbuf("rs_lohi", [8, 2], F32)
    mid = kb.sbuf("rs_mid", [8, 1], F32)
    pred = kb.sbuf("rs_pred", [8, 1], U32)
    cnt8s = kb.sbuf("rs_cnt8", [8, 1], F32)
    trep = kb.sbuf("rs_trep", [128, 1], F32)
    cntp_s = kb.sbuf("rs_cntp", [128, 1], F32)
    junk = kb.sbuf("rs_junk", [128, 1024], F32)
    sigp = kb.psum("rs_sigp", [16, 1024])
    repp = kb.psum("rs_repp", [128, 1024])
    trp = kb.psum("rs_trp", [128, 1])
    c8p = kb.psum("rs_c8p", [8, 1])
    mine_l = [kb.sbuf(f"rs_mine{i}", [128, 16], F32) for i in range(E_EXP)]
    work_l = [kb.sbuf(f"rs_work{i}", [128, 16], F32) for i in range(E_EXP)]
    vals8 = kb.sbuf("rs_vals", [128, 8], F32)
    idx8 = kb.sbuf("rs_idx", [128, 8], U32)
    idxf = kb.sbuf("rs_idxf", [128, 8], F32)
    stg_l = [kb.sbuf(f"rs_stg{i}", [128, 32], F32) for i in range(E_EXP)]
    mask16 = kb.sbuf("rs_m16", [128, 16], F32)
    cpart = kb.sbuf("rs_cp", [128, 1], F32)
    offs = kb.sbuf("rs_offs", [128, 1], F32)
    tse = kb.sbuf("rs_tse", [128, 1], F32)
    posf = kb.sbuf("rs_posf", [128, 16], F32)
    posi_l = [kb.sbuf(f"rs_posi{i}", [128, 16], I32) for i in range(E_EXP)]
    offp = kb.psum("rs_offp", [128, 1])
    tsep = kb.psum("rs_tsep", [128, 1])

    for dst, src in ((s16, s16_d), (r16, r16_d), (rep16, rep16_d),
                     (grp16, grp16_d), (sel8, sel8_d),
                     (lt128, lt128_d), (jj16, jj16_d), (iota16, iota16_d),
                     (iota1, iota1_d)):
        kb.dma("sync", dst[:, :], src[:, :], "rs_cld")
    cldv = kb.sem("rs_cld").val
    kb.wait("vector", "rs_cld", cldv)
    kb.wait("tensor", "rs_cld", cldv)
    kb.op("vector", lambda e, o=dumpc, i=iota1:
          e.tensor_scalar(o[:, :], i[:, :], float(E_EXP * CAP), None, op0=AL.add))
    kb.op("vector", lambda e, o=fillt: e.memset(o[:, :], 0.0))
    kb.op("vector", lambda e, o=fillt: e.memset(o[:, 0:48:2], 2048.0), sem="rs_z")
    kb.wait("sync", "rs_z")
    kb.dma("sync", pairs_d[0:E_EXP * CAP, :].rearrange("(p i) two -> p (i two)", p=128),
           fillt[:, :], "rs_pf")
    pfv = kb.sem("rs_pf").val

    kb.wait("sync", "rt_lst")
    lsrc = logits_d.rearrange("r e (j2 i) -> e r j2 i", j2=2)
    for ee in range(8):
        for rr in range(8):
            lv = kb.dma("sync", logL[ee * 16 + rr * 2: ee * 16 + rr * 2 + 2, :],
                        lsrc[ee, rr], "rs_lld")
    kb.wait("scalar", "rs_lld", lv)
    kb.op("scalar", lambda e, o=expL, i=logL:
          e.activation(o[:, :], i[:, :], AF.Exp), sem="rs_exp")
    ev = kb.sem("rs_exp").val
    kb.wait("tensor", "rs_exp", ev)
    kb.op("tensor", lambda e, ps=sigp, lh=s16, rh=expL:
          e.matmul(ps[:, 0:512], lh[:, :], rh[:, 0:512], start=True, stop=True))
    kb.op("tensor", lambda e, ps=sigp, lh=s16, rh=expL:
          e.matmul(ps[:, 512:1024], lh[:, :], rh[:, 512:1024], start=True, stop=True),
          sem="rs_sig")
    sv = kb.sem("rs_sig").val
    kb.wait("vector", "rs_sig", sv)
    kb.op("vector", lambda e, o=sigr, i=sigp: e.reciprocal(o[:, :], i[:, :]),
          sem="rs_rec")
    rv = kb.sem("rs_rec").val
    kb.wait("tensor", "rs_rec", rv)
    kb.op("tensor", lambda e, ps=repp, lh=r16, rh=sigr:
          e.matmul(ps[:, 0:512], lh[:, :], rh[:, 0:512], start=True, stop=True))
    kb.op("tensor", lambda e, ps=repp, lh=r16, rh=sigr:
          e.matmul(ps[:, 512:1024], lh[:, :], rh[:, 512:1024], start=True, stop=True),
          sem="rs_rep")
    rpv = kb.sem("rs_rep").val
    kb.wait("vector", "rs_rep", rpv)
    kb.op("vector", lambda e, o=probL, i0=expL, i1=repp:
          e.tensor_tensor(o[:, :], i0[:, :], i1[:, :], op=AL.mult), sem="rs_prob")
    pv = kb.sem("rs_prob").val
    kb.wait("sync", "rs_prob", pv)
    pdst = probs_d.rearrange("e r (j2 i) -> e r j2 i", j2=2)
    for ee in range(8):
        for rr in range(8):
            kb.dma("sync", pdst[ee, rr],
                   probL[ee * 16 + rr * 2: ee * 16 + rr * 2 + 2, :], "rs_pst")
    pstv = kb.sem("rs_pst").val

    kb.op("vector", lambda e, o=lohi: e.memset(o[:, 0:1], 0.0))
    kb.op("vector", lambda e, o=lohi: e.memset(o[:, 1:2], 1.0), sem="rs_bis")
    for it in range(BISECT):
        kb.op("vector", lambda e, o=mid, l=lohi:
              e.tensor_tensor(o[:, :], l[:, 0:1], l[:, 1:2], op=AL.add))
        kb.op("vector", lambda e, o=mid:
              e.tensor_scalar(o[:, :], o[:, :], 0.5, None, op0=AL.mult), sem="rs_bis")
        bv = kb.sem("rs_bis").val
        kb.wait("tensor", "rs_bis", bv)
        kb.op("tensor", lambda e, ps=trp, lh=rep16, rh=mid:
              e.matmul(ps[:, :], lh[:, :], rh[:, :], start=True, stop=True),
              sem="rs_trm")
        tmv = kb.sem("rs_trm").val
        kb.wait("vector", "rs_trm", tmv)
        kb.op("vector", lambda e, o=trep, i=trp: e.tensor_copy(o[:, :], i[:, :]),
              sem="rs_trc")
        tcv = kb.sem("rs_trc").val
        kb.op("vector", lambda e, o=junk, i=probL, t=trep, a=cntp_s:
              e.tensor_scalar(o[:, :], i[:, :], t[:, :], 0.0, op0=AL.is_ge,
                              op1=AL.add, accum_out=a[:, :]), sem="rs_cnt")
        cv = kb.sem("rs_cnt").val
        kb.wait("tensor", "rs_cnt", cv)
        kb.op("tensor", lambda e, ps=c8p, lh=grp16, rh=cntp_s:
              e.matmul(ps[:, :], lh[:, :], rh[:, :], start=True, stop=True),
              sem="rs_c8")
        c8v = kb.sem("rs_c8").val
        kb.wait("vector", "rs_c8", c8v)
        kb.op("vector", lambda e, o=pred, i=c8p:
              e.tensor_scalar(o[:, :], i[:, :], float(K_SEL), None, op0=AL.is_ge))
        kb.op("vector", lambda e, o=lohi, p=pred, m=mid:
              e.select(o[:, 0:1], p[:, :], m[:, :], o[:, 0:1]))
        kb.op("vector", lambda e, o=lohi, p=pred, m=mid:
              e.select(o[:, 1:2], p[:, :], o[:, 1:2], m[:, :]), sem="rs_bis")

    bisv = kb.sem("rs_bis").val

    kb.wait("sync", "rs_pst", pstv)
    for ex in range(E_EXP):
        mine = mine_l[ex]; work = work_l[ex]; stg = stg_l[ex]; posi = posi_l[ex]
        mv = kb.dma("sync", mine[:, :],
                    probs_d[ex, 0, :].rearrange("(p i) -> p i", p=128), "rs_mld")
        kb.wait("tensor", "rs_bis", bisv)
        kb.op("tensor", lambda e, ps=tsep, lh=sel8, rh=lohi, ex=ex:
              e.matmul(ps[:, :], lh[:, ex * 128:(ex + 1) * 128], rh[:, 0:1],
                       start=True, stop=True), sem="rs_tse")
        tsv = kb.sem("rs_tse").val
        kb.wait("vector", "rs_tse", tsv)
        kb.op("vector", lambda e, o=tse, i=tsep: e.tensor_copy(o[:, :], i[:, :]),
              sem="rs_tsec")
        kb.wait("vector", "rs_mld", mv)
        kb.op("vector", lambda e, o=work, i=mine: e.tensor_copy(o[:, :], i[:, :]))
        for rd in range(2):
            kb.op("vector", lambda e, vm=vals8, w=work:
                  e.max(out=vm[:, :], in_=w[:, :]))
            kb.op("vector", lambda e, vm=vals8, vi=idx8, w=work:
                  e.max_index(out=vi[:, :], in_max=vm[:, :], in_values=w[:, :]))
            kb.op("vector", lambda e, w=work, vm=vals8:
                  e.match_replace(out=w[:, :], in_to_replace=vm[:, :],
                                  in_values=w[:, :], imm_value=-1.0))
            kb.op("vector", lambda e, o=idxf, i=idx8: e.tensor_copy(o[:, :], i[:, :]))
            kb.op("vector", lambda e, o=idxf, i=iota16:
                  e.tensor_scalar(o[:, :], o[:, :], i[:, :], None, op0=AL.add))
            kb.op("vector", lambda e, o=stg, i=idxf, rd=rd:
                  e.tensor_copy(o[:, rd * 16 + 0:rd * 16 + 16:2], i[:, :]))
            kb.op("vector", lambda e, o=stg, i=vals8, rd=rd:
                  e.tensor_copy(o[:, rd * 16 + 1:rd * 16 + 16:2], i[:, :]), sem="rs_stg")
        kb.op("vector", lambda e, o=mask16, i=stg, t=tse, a=cpart:
              e.tensor_scalar(o[:, :], i[:, 1:32:2], t[:, :], 0.0, op0=AL.is_ge,
                              op1=AL.add, accum_out=a[:, :]), sem="rs_cp")
        cpv = kb.sem("rs_cp").val
        kb.wait("tensor", "rs_cp", cpv)
        kb.op("tensor", lambda e, ps=offp, lh=lt128, rh=cpart:
              e.matmul(ps[:, :], lh[:, :], rh[:, :], start=True, stop=True),
              sem="rs_off")
        ofv = kb.sem("rs_off").val
        kb.wait("vector", "rs_off", ofv)
        kb.op("vector", lambda e, o=offs, i=offp: e.tensor_copy(o[:, :], i[:, :]))
        kb.op("vector", lambda e, o=ojj, j=jj16, s=offs:
              e.tensor_scalar(o[:, :], j[:, :], s[:, :], None, op0=AL.add))
        kb.op("vector", lambda e, o=okv, i=ojj:
              e.tensor_scalar(o[:, :], i[:, :], float(CAP), None, op0=AL.is_lt))
        kb.op("vector", lambda e, o=okv, m=mask16:
              e.tensor_tensor(o[:, :], o[:, :], m[:, :], op=AL.mult))
        kb.op("vector", lambda e, o=okvi, i=okv: e.tensor_copy(o[:, :], i[:, :]))
        kb.op("vector", lambda e, o=posf, i=ojj, ex=ex:
              e.tensor_scalar(o[:, :], i[:, :], float(ex * CAP), None, op0=AL.add))
        kb.op("vector", lambda e, o=posf, m=okvi, d=dumpc:
              e.select(o[:, :], m[:, :], o[:, :],
                       d[:, 0:1].to_broadcast([128, 16])))
        kb.op("vector", lambda e, o=posi, i=posf: e.tensor_copy(o[:, :], i[:, :]),
              sem="rs_pos")
        psv = kb.sem("rs_pos").val
        kb.wait("gpsimd", "rs_pf", pfv)
        kb.wait("gpsimd", "rs_pos", psv)
        for j in range(16):
            kb.idma(pairs_d, IndirectOffsetOnAxis(ap=posi[:, j:j + 1], axis=0),
                    stg[:, 2 * j:2 * j + 2], None, "rs_psc")
        kb.wait("vector", "rs_psc")

    sent_d = kb.d["sentinel"].ap()
    kb.op("vector", lambda e_, o=posi_l[0]: e_.memset(o[0:128, 0:1], 0), sem="rs_zi")
    kb.wait("gpsimd", "rs_zi")
    sv = kb.idma(fillt[:, 0:1], None, sent_d,
                 IndirectOffsetOnAxis(ap=posi_l[0][:, 0:1], axis=0), "rs_sent")
    for eng in ENGINES:
        kb.wait(eng, "rs_sent", sv)
    kb.phase_end()


def host_constants():
    c = {}
    comb17 = np.zeros((17, 8), np.float32)
    for e in range(8):
        comb17[e, e] = 1.0
        comb17[e + 8, e] = 1.0
    c["comb17"] = comb17
    p = np.arange(128)
    c["s16"] = (p[:, None] % 16 == np.arange(16)[None, :]).astype(np.float32)
    c["r16"] = (np.arange(16)[:, None] == p[None, :] % 16).astype(np.float32)
    c["rep16"] = (np.arange(8)[:, None] == p[None, :] // 16).astype(np.float32)
    c["grp16"] = (p[:, None] // 16 == np.arange(8)[None, :]).astype(np.float32)
    sel8 = np.zeros((8, 1024), np.float32)
    for e in range(8):
        sel8[e, e * 128:(e + 1) * 128] = 1.0
    c["sel8"] = sel8
    c["lt128"] = (p[:, None] < p[None, :]).astype(np.float32)
    c["jj16"] = np.broadcast_to(np.arange(16, dtype=np.float32), (128, 16)).copy()
    c["iota16"] = (16.0 * p[:, None]).astype(np.float32)
    c["iota1"] = p[:, None].astype(np.float32)
    pad32 = np.zeros((128, 4), np.float32)
    for hh in range(4):
        pad32[32 * hh:32 * hh + 16, hh] = 1.0
    c["pad32"] = pad32.astype(ml_dtypes.bfloat16)
    selh4 = np.zeros((4, 512), np.float32)
    for hh in range(4):
        selh4[hh, 128 * hh:128 * (hh + 1)] = 1.0
    c["selh4"] = selh4
    c["ident"] = np.eye(128, dtype=ml_dtypes.bfloat16)
    dh = 64
    inv = 1.0 / (10000.0 ** (np.arange(0, dh, 2, dtype=np.float64) / dh))
    pos = np.arange(S, dtype=np.float64)
    dd = p % 64
    jj = np.where(dd < 32, dd, dd - 32)
    ang = pos[None, :] * inv[jj][:, None]
    c["ropecos"] = np.cos(ang).astype(np.float32)
    c["ropesin"] = np.sin(ang).astype(np.float32)
    rr = np.zeros((128, 128), np.float32)
    for blk in range(2):
        o = blk * 64
        for d_ in range(32):
            rr[o + d_ + 32, o + d_] = -1.0
            rr[o + d_, o + d_ + 32] = 1.0
    c["rrot2"] = rr.astype(ml_dtypes.bfloat16)
    cm = np.zeros((4, 128, 512), np.float32)
    for rel in range(4):
        mm = 128 * rel + np.arange(128)[:, None]
        nn = np.arange(512)[None, :]
        cm[rel] = (mm <= nn).astype(np.float32)
    c["cmask"] = cm.astype(ml_dtypes.bfloat16)
    p64 = np.zeros((128, 2), np.float32)
    p64[0:64, 0] = 1.0
    p64[64:128, 1] = 1.0
    c["pad64"] = p64.astype(ml_dtypes.bfloat16)
    s2 = np.zeros((2, 256), np.float32)
    s2[0, 0:128] = 1.0
    s2[1, 128:256] = 1.0
    c["selh2"] = s2
    return c


CONST_SPECS = {
    "comb17": ([17, 8], F32), "s16": ([128, 16], F32), "r16": ([16, 128], F32),
    "rep16": ([8, 128], F32), "grp16": ([128, 8], F32), "sel8": ([8, 1024], F32),
    "lt128": ([128, 128], F32), "jj16": ([128, 16], F32), "iota16": ([128, 1], F32),
    "iota1": ([128, 1], F32), "ident": ([128, 128], BF16),
    "pad32": ([128, 4], BF16), "selh4": ([4, 512], F32),
    "ropecos": ([128, S], F32), "ropesin": ([128, S], F32),
    "rrot2": ([128, 128], BF16), "cmask": ([4, 128, 512], BF16),
    "pad64": ([128, 2], BF16), "selh2": ([2, 256], F32),
}

SCRATCH_SPECS = {
    "xhi": ([NT, 1024], BF16), "xlo": ([NT, 1024], BF16),
    "mu": ([NT, 1], F32), "rstd": ([NT, 1], F32),
    "xnb": ([S, 1024], BF16), "xnf": ([S + 1, 1024], F32),
    "logits": ([8, 8, S], F32), "probs": ([8, 8, S], F32),
    "pairs": ([E_EXP * CAP + 128, 2], F32),
    "e2hi": ([D, S], BF16), "e2lo": ([D, S], BF16),
    "sentinel": ([128, 1], F32),
    "xin": ([E_EXP, CAP, D], BF16),
    "yhi": ([E_EXP, D, CAP], BF16), "ylo": ([E_EXP, D, CAP], BF16),
    "moeacc2": ([S + 1, E_EXP * D], F32),
    "x1": ([S, D], F32), "xn1": ([S, D], BF16),
    "s1hi": ([D, S], BF16), "s1lo": ([D, S], BF16),
    "f1hi": ([D, S], BF16), "f1lo": ([D, S], BF16),
}

INPUT_SPECS = {
    "x": ([NT, 1024], F32),
    "ws": ([1024, 8], F32), "bs": ([8, 1], F32),
    "l0_wqT": ([D, 512], F32), "l0_wkT": ([D, 512], F32),
    "l0_wvT": ([D, D], F32), "l0_wpT": ([D, D], F32),
    "l0_w1T": ([D, 64], F32), "l0_w2T": ([64, D], F32),
    "l0_b1": ([64, 1], F32), "l0_b2": ([D, 1], F32),
    "moe_w1": ([E_EXP, D, F_MOE], F32), "moe_w2": ([E_EXP, F_MOE, D], F32),
    "moe_b1": ([E_EXP, F_MOE], F32), "moe_b2": ([E_EXP, D], F32),
    "l1_wqT": ([D, D], F32), "l1_wkT": ([D, D], F32),
    "l1_wvT": ([D, D], F32), "l1_woT": ([D, D], F32),
    "l1_wgT": ([D, H_FF1], I32), "l1_wuT": ([D, H_FF1], I32),
    "l1_wdT": ([H_FF1, D], I32),
}


def build_program(phases, dump=(), extra_inputs=(), extra_scratch=()):
    nc = bass.Bass("TRN2", target_bir_lowering=False, debug=False)
    stack = ExitStack()
    kb = KB(nc, stack)
    with stack:
        for name, (shape, dt) in {**INPUT_SPECS, **CONST_SPECS, **dict(extra_inputs)}.items():
            kb.dram(name, shape, dt, kind="ExternalInput")
        for name, (shape, dt) in {**SCRATCH_SPECS, **dict(extra_scratch)}.items():
            kind = "ExternalOutput" if name in dump else "Internal"
            kb.dram(name, shape, dt, kind=kind)
        kb.dram("out", [S, D], F32, kind="ExternalOutput")
        for ph in phases:
            ph(kb)
        for name, s in list(kb._sems.items()):
            if s.val > 0:
                kb.ops["sync"].append(lambda eng, h=s.h, v=s.val: eng.wait_ge(h, v))
        kb.finalize_block()
    return kb


def prep_host_inputs(inputs, core):
    x = inputs["L0_attn_x"]
    xr = np.roll(x, -core, axis=0).reshape(NT, 1024)
    wq_pad = np.zeros((1024, 512), np.float32)
    wk_pad = np.zeros((1024, 512), np.float32)
    for hc in range(16):
        wq_pad[:, 32 * hc:32 * hc + 16] = inputs["L0_attn_Wq"][16 * hc:16 * hc + 16, :].T
        wk_pad[:, 32 * hc:32 * hc + 16] = inputs["L0_attn_Wk"][16 * hc:16 * hc + 16, :].T
    m = {
        "x": np.ascontiguousarray(xr),
        "ws": np.ascontiguousarray(inputs["L0_ffn_Ws"]),
        "bs": np.ascontiguousarray(inputs["L0_ffn_bs"].reshape(8, 1)),
        "l0_wqT": wq_pad, "l0_wkT": wk_pad,
        "l0_wvT": np.ascontiguousarray(inputs["L0_attn_Wv"].T),
        "l0_wpT": np.ascontiguousarray(inputs["L0_attn_Wp"].T),
        "l0_w1T": np.ascontiguousarray(inputs["L0_attn_W1"].T),
        "l0_w2T": np.ascontiguousarray(inputs["L0_attn_W2"].T),
        "l0_b1": np.ascontiguousarray(inputs["L0_attn_b1"].reshape(64, 1)),
        "l0_b2": np.ascontiguousarray(inputs["L0_attn_b2"].reshape(1024, 1)),
        "moe_w1": inputs["L0_ffn_W1"], "moe_w2": inputs["L0_ffn_W2"],
        "moe_b1": inputs["L0_ffn_b1"], "moe_b2": inputs["L0_ffn_b2"],
        "l1_wqT": np.ascontiguousarray(inputs["L1_attn_Wq"].T),
        "l1_wkT": np.ascontiguousarray(inputs["L1_attn_Wk"].T),
        "l1_wvT": np.ascontiguousarray(inputs["L1_attn_Wv"].T),
        "l1_woT": np.ascontiguousarray(inputs["L1_attn_Wo"].T),
        "l1_wgT": np.ascontiguousarray(inputs["L1_ffn_w_gate"].T),
        "l1_wuT": np.ascontiguousarray(inputs["L1_ffn_w_up"].T),
        "l1_wdT": np.ascontiguousarray(inputs["L1_ffn_w_down"].T),
    }
    m.update(host_constants())
    return m





def l0_attn(kb):
    kb.sem("la_sc", producer="tensor")
    kb.sem("la_u", producer="tensor")
    kb.sem("la_pj", producer="tensor")
    kb.sem("la_nmm", producer="tensor")
    kb.fence("rs_pst", "rs_mld", "rs_pf")
    xnb = kb.d["xnb"].ap()
    e2hi_d = kb.d["e2hi"].ap()
    e2lo_d = kb.d["e2lo"].ap()
    wq_d = kb.d["l0_wqT"].ap()
    wk_d = kb.d["l0_wkT"].ap()
    wv_d = kb.d["l0_wvT"].ap()
    wp_d = kb.d["l0_wpT"].ap()
    w1_d = kb.d["l0_w1T"].ap()
    w2_d = kb.d["l0_w2T"].ap()
    b1_d = kb.d["l0_b1"].ap()
    b2_d = kb.d["l0_b2"].ap()
    pad32_d = kb.d["pad32"].ap()
    selh4_d = kb.d["selh4"].ap()
    SC = 0.25

    kb.phase_begin()
    e = [kb.sbuf(f"la_e{j}", [128, 2048], F32) for j in range(8)]
    ebf = [kb.sbuf(f"la_ebf{j}", [128, 2048], BF16) for j in range(8)]
    kp = [kb.sbuf(f"la_kp{g}", [128, 2048], BF16) for g in range(4)]
    qp = kb.sbuf("la_qp", [128, 512], BF16)
    vaug = [kb.sbuf(f"la_va{m}", [128, 1040], BF16) for m in range(16)]
    expb2 = [[kb.sbuf(f"la_ex{i}_{k}", [128, 1024], BF16) for k in range(2)]
             for i in range(2)]
    attnT = kb.sbuf("la_attnT", [128, 4096], BF16)
    wqg = [[kb.sbuf(f"la_wqg{k}_{c}", [128, 128], BF16) for c in range(8)]
           for k in range(2)]
    wkg = [kb.sbuf(f"la_wkg{c}", [128, 128], BF16) for c in range(8)]
    wv = [kb.sbuf(f"la_wv{c}", [128, 512], BF16) for c in range(8)]
    wp = [kb.sbuf(f"la_wp{c}", [128, 1024], BF16) for c in range(8)]
    w1 = [kb.sbuf(f"la_w1{c}", [128, 64], BF16) for c in range(8)]
    w2 = kb.sbuf("la_w2", [64, 1024], BF16)
    b1 = kb.sbuf("la_b1", [64, 1], F32)
    b2 = [kb.sbuf(f"la_b2{c}", [128, 1], F32) for c in range(8)]
    h2 = [kb.sbuf(f"la_h{i}", [64, 512], BF16) for i in range(2)]
    sig = kb.sbuf("la_sig", [1, 512], F32)
    bcs = kb.sbuf("la_bcs", [64, 512], F32)
    onesf64 = kb.sbuf("la_ones64", [1, 64], F32)
    pad32 = kb.sbuf("la_pad32", [128, 4], BF16)
    selh4 = kb.sbuf("la_selh4", [4, 512], F32)
    sqt = kb.sbuf("la_sqt", [128, 2048], BF16)
    kmx = [kb.sbuf(f"la_kmx{g}", [4, 1], F32) for g in range(4)]
    qmx = [kb.sbuf(f"la_qmx{g}", [4, 1], F32) for g in range(4)]
    t4 = kb.sbuf("la_t4", [4, 2], F32)
    negm4 = [kb.sbuf(f"la_nm4{g}", [4, 1], F32) for g in range(4)]
    nmax16 = kb.sbuf("la_nmx16", [128, 16], F32)
    scp = [kb.psum(f"la_scp{i}", [128, 1024]) for i in range(2)]
    up = [kb.psum(f"la_up{i}", [65, 512]) for i in range(2)]
    bcp = kb.psum("la_bcp", [64, 512])
    nmsh = kb.psum("la_nmsh", [128, 512])
    np4 = nmsh[0:4, 0:512]
    nmp = nmsh[0:128, 0:1]

    for c in range(8):
        kb.dma("gpsimd", wp[c][:, :], wp_d[c * 128:(c + 1) * 128, :], "la_wld")
        kb.dma("gpsimd", w1[c][:, :], w1_d[c * 128:(c + 1) * 128, :], "la_wld")
        kb.dma("gpsimd", b2[c][:, :], b2_d[c * 128:(c + 1) * 128, :], "la_wld")
    kb.dma("gpsimd", w2[:, :], w2_d[:, :], "la_wld")
    kb.dma("gpsimd", b1[:, :], b1_d[:, :], "la_wld")
    wldv = kb.sem("la_wld").val
    kb.dma("sync", pad32[:, :], pad32_d[:, :], "la_wldc")
    kb.dma("sync", selh4[:, :], selh4_d[:, :], "la_wldc")
    wldcv = kb.sem("la_wldc").val
    kb.op("vector", lambda e_, o=onesf64: e_.memset(o[:, :], 1.0), sem="la_init")
    initv = kb.sem("la_init").val

    kb.wait("scalar", "p0_st")
    for j in range(8):
        kb.dmat("scalar", ebf[j][:, :], xnb[:, j * 128:(j + 1) * 128], "la_tp")
    tpv = kb.sem("la_tp").val
    kb.wait("vector", "la_tp", tpv)
    for j in range(8):
        kb.op("vector", lambda e_, o=e[j], i=ebf[j]:
              e_.tensor_copy(o[:, :], i[:, :]), sem="la_e0")
    kb.wait("tensor", "la_wld", wldv)
    kb.wait("tensor", "la_wldc", wldcv)
    kb.wait("tensor", "la_init", initv)
    kb.wait("tensor", "la_tp", tpv)
    kb.wait("scalar", "la_wld", wldv)
    kb.wait("vector", "la_wld", wldv)

    scp_g = [0, 0]
    up_g = [0, 0]
    u_done = {}

    def halves(i):
        return scp[i // 2][:, (i % 2) * 512:(i % 2) * 512 + 512]

    wq_use = [0, 0]
    wq_ld = [0, 0]
    wq_k = [0]

    def qgen_mm(g, dst_ps, ncol0, sem="la_q"):
        k = wq_k[0] % 2
        wq_k[0] += 1
        kb.wait("gpsimd", "la_q", wq_use[k])
        for c in range(8):
            kb.dma("gpsimd", wqg[k][c][:, :],
                   wq_d[c * 128:(c + 1) * 128, g * 128:(g + 1) * 128], "la_wkld")
        wq_ld[k] = kb.sem("la_wkld").val
        kb.wait("tensor", "la_wkld", wq_ld[k])
        prs = [(wqg[k][c][:, :], ebf[c][:, ncol0:ncol0 + 512]) for c in range(8)]
        mm_accum(kb, dst_ps, prs, sem=sem)
        wq_use[k] = kb.sem(sem).val

    for it in range(2):
        if it == 1:
            kb.wait("vector", "la_pe")
            for j in range(8):
                kb.op("vector", lambda e_, o=ebf[j], i=e[j]:
                      e_.tensor_copy(o[:, :], i[:, :]), sem="la_ebf", nodrain=True)
            kb.wait("tensor", "la_ebf")
            kb.wait("scalar", "la_ebf")

        for g in range(4):
            kb.wait("gpsimd", "la_sc")
            for c in range(8):
                kb.dma("gpsimd", wkg[c][:, :],
                       wk_d[c * 128:(c + 1) * 128, g * 128:(g + 1) * 128], "la_wkld")
            wkv = kb.sem("la_wkld").val
            kb.wait("tensor", "la_wkld", wkv)
            for n in range(4):
                hb = (g * 4 + n) % 4
                kb.wait("tensor", "la_exp", scp_g[hb // 2])
                prs = [(wkg[c][:, :], ebf[c][:, n * 512:(n + 1) * 512])
                       for c in range(8)]
                mm_accum(kb, halves(hb), prs, sem="la_sc")
                scv = kb.sem("la_sc").val
                kb.wait("vector", "la_sc", scv)
                kb.op("vector", lambda e_, o=kp[g], i=halves(hb), n=n:
                      e_.tensor_copy(o[:, n * 512:(n + 1) * 512], i),
                      sem="la_exp", nodrain=True)
                scp_g[hb // 2] = kb.sem("la_exp").val
        for half in range(2):
            kb.wait("gpsimd", "la_sc")
            for c in range(8):
                kb.dma("gpsimd", wv[c][:, :],
                       wv_d[c * 128:(c + 1) * 128, half * 512:(half + 1) * 512],
                       "la_wvld")
            wvv = kb.sem("la_wvld").val
            kb.wait("tensor", "la_wvld", wvv)
            for mt in range(16):
                hb = mt % 4
                kb.wait("tensor", "la_exp", scp_g[hb // 2])
                prs = [(ebf[c][:, mt * 128:(mt + 1) * 128], wv[c][:, :])
                       for c in range(8)]
                mm_accum(kb, halves(hb), prs, sem="la_sc")
                vv = kb.sem("la_sc").val
                kb.wait("vector", "la_sc", vv)
                dst = vaug[mt].rearrange("p (hh cc) -> p hh cc", cc=65)
                kb.op("vector", lambda e_, d=dst, i=halves(hb), half=half:
                      e_.tensor_copy(
                          d[:, half * 8:(half + 1) * 8, 0:64],
                          i.rearrange("p (hh cc) -> p hh cc", cc=64)),
                      sem="la_exp", nodrain=True)
                scp_g[hb // 2] = kb.sem("la_exp").val
        for mt in range(16):
            kb.op("vector", lambda e_, o=vaug[mt]:
                  e_.memset(o[:, 64:1040:65], 1.0), sem="la_exp", nodrain=True)
        scp_g[0] = scp_g[1] = kb.sem("la_exp").val
        kb.wait("tensor", "la_nbe")
        for g in range(4):
            kb.wait("scalar", "la_exp", kb.sem("la_exp").val)
            kb.wait("scalar", "la_nmm")
            kb.op("scalar", lambda e_, o=sqt, i=kp[g]:
                  e_.activation(o[:, :], i[:, :], AF.Square), sem="la_ksq")
            kv = kb.sem("la_ksq").val
            kb.wait("tensor", "la_ksq", kv)
            kb.wait("tensor", "la_nred")
            for n in range(4):
                kb.op("tensor", lambda e_, ps=np4, lh=pad32, rh=sqt, n=n:
                      e_.matmul(ps, lh[:, :], rh[:, n * 512:(n + 1) * 512],
                                start=(n == 0), stop=(n == 3)),
                      sem="la_nmm" if n == 3 else None)
            nv = kb.sem("la_nmm").val
            kb.wait("vector", "la_nmm", nv)
            kb.op("vector", lambda e_, o=kmx[g], i=np4:
                  e_.tensor_reduce(o[:, :], i, axis=mybir.AxisListType.X,
                                   op=AL.max), sem="la_nred")
            for n in range(4):
                hb = (g * 4 + n) % 4
                kb.wait("tensor", "la_exp", scp_g[hb // 2])
                qgen_mm(g, halves(hb), n * 512)
                qv = kb.sem("la_q").val
                kb.wait("scalar", "la_q", qv)
                kb.op("scalar", lambda e_, o=sqt, i=halves(hb), n=n:
                      e_.activation(o[:, n * 512:(n + 1) * 512], i, AF.Square),
                      sem="la_exp", nodrain=True)
                scp_g[hb // 2] = kb.sem("la_exp").val
            qsv = kb.sem("la_exp").val
            kb.wait("tensor", "la_exp", qsv)
            kb.wait("tensor", "la_nred")
            for n in range(4):
                kb.op("tensor", lambda e_, ps=np4, lh=pad32, rh=sqt, n=n:
                      e_.matmul(ps, lh[:, :], rh[:, n * 512:(n + 1) * 512],
                                start=(n == 0), stop=(n == 3)),
                      sem="la_nmm" if n == 3 else None)
            nv = kb.sem("la_nmm").val
            kb.wait("vector", "la_nmm", nv)
            kb.op("vector", lambda e_, o=qmx[g], i=np4:
                  e_.tensor_reduce(o[:, :], i, axis=mybir.AxisListType.X,
                                   op=AL.max), sem="la_nred")
            nrv = kb.sem("la_nred").val
            kb.wait("vector", "la_nred", nrv)
            kb.op("vector", lambda e_, o=t4, a=kmx[g], b=qmx[g]:
                  e_.tensor_tensor(o[0:4, 1:2], a[:, :], b[:, :], op=AL.mult),
                  sem="la_nm1")
            v1 = kb.sem("la_nm1").val
            kb.wait("scalar", "la_nm1", v1)
            kb.op("scalar", lambda e_, o=t4:
                  e_.activation(o[0:4, 0:1], o[0:4, 1:2], AF.Sqrt), sem="la_nm2")
            v2 = kb.sem("la_nm2").val
            kb.wait("vector", "la_nm2", v2)
            kb.op("vector", lambda e_, o=negm4[g], i=t4:
                  e_.tensor_scalar(o[:, :], i[0:4, 0:1], -SC, None, op0=AL.mult),
                  sem="la_nm3")
        kb.wait("tensor", "la_nm3")
        kb.wait("tensor", "la_nred")
        for cc in range(16):
            kb.op("tensor", lambda e_, ps=nmsh, lh=selh4, rh=negm4[cc // 4],
                  hh=cc % 4, cc=cc:
                  e_.matmul(ps[0:128, cc:cc + 1], lh[:, hh * 128:(hh + 1) * 128],
                            rh[:, :], start=True, stop=True),
                  sem="la_nbc" if cc == 15 else None)
        nbv = kb.sem("la_nbc").val
        kb.wait("vector", "la_nbc", nbv)
        kb.op("vector", lambda e_, o=nmax16, i=nmsh:
              e_.tensor_copy(o[:, :], i[0:128, 0:16]), sem="la_nbe")
        kb.wait("scalar", "la_nbe")
        kb.wait("tensor", "la_nbe")

        for nt in range(4):
            kb.wait("vector", "la_pj")
            for g in range(4):
                hb = g % 4
                kb.wait("tensor", "la_exp", scp_g[hb // 2])
                qgen_mm(g, halves(hb), nt * 512)
                qv = kb.sem("la_q").val
                kb.wait("vector", "la_q", qv)
                kb.wait("vector", "la_sc")
                kb.op("vector", lambda e_, o=qp, i=halves(hb):
                      e_.tensor_copy(o[:, :], i), sem="la_exp")
                scp_g[hb // 2] = kb.sem("la_exp").val
                for p in (2 * g, 2 * g + 1):
                    c0 = 2 * p
                    c1 = 2 * p + 1
                    rb0 = 32 * (c0 % 4)
                    rb1 = 32 * (c1 % 4)
                    kb.wait("tensor", "la_att", max(up_g))
                    pend = None
                    for mp in range(8):
                        par = mp % 2
                        sc_ci = [0, 0]
                        for ci, (cc, rb) in enumerate(((c0, rb0), (c1, rb1))):
                            kb.wait("tensor", "la_exp", scp_g[ci])
                            for sub in range(2):
                                mt = 2 * mp + sub
                                kb.op("tensor", lambda e_, ps=scp[ci], g=g, rb=rb,
                                      mt=mt, sub=sub, nt=nt:
                                      e_.matmul(ps[:, sub * 512:(sub + 1) * 512],
                                                kp[g][rb:rb + 16,
                                                      mt * 128:(mt + 1) * 128],
                                                qp[rb:rb + 16, :],
                                                start=True, stop=True,
                                                tile_position=(rb, 0)),
                                      sem="la_sc" if sub == 1 else None)
                            sc_ci[ci] = kb.sem("la_sc").val
                        if pend is not None:
                            emit_u(kb, pend, vaug, expb2, up, False, u_done)
                        for ci, cc in enumerate((c0, c1)):
                            kb.wait("scalar", "la_sc", sc_ci[ci])
                            kb.wait("scalar", "la_u", u_done.get((ci, par), 0))
                            kb.op("scalar", lambda e_, o=expb2[ci][par],
                                  i=scp[ci], bi=nmax16, cc=cc:
                                  e_.activation(o[:, :], i[:, :], AF.Exp,
                                                bias=bi[:, cc:cc + 1], scale=SC),
                                  sem="la_exp", nodrain=(ci == 1))
                            scp_g[ci] = kb.sem("la_exp").val
                        pend = (mp, c0, c1, kb.sem("la_exp").val)
                        if mp == 7:
                            emit_u(kb, pend, vaug, expb2, up, True, u_done)
                    uv = kb.sem("la_u").val
                    for ci, cc in enumerate((c0, c1)):
                        ub = ci
                        kb.wait("vector", "la_u", uv)
                        kb.op("vector", lambda e_, o=sig, i=up[ub]:
                              e_.reciprocal(o[:, :], i[64:65, :]), sem="la_rc")
                        rcv = kb.sem("la_rc").val
                        kb.wait("tensor", "la_rc", rcv)
                        kb.op("tensor", lambda e_, ps=bcp, lh=onesf64, rh=sig:
                              e_.matmul(ps[:, :], lh[:, :], rh[:, :],
                                        start=True, stop=True), sem="la_bc")
                        bcv = kb.sem("la_bc").val
                        kb.wait("vector", "la_bc", bcv)
                        kb.op("vector", lambda e_, o=bcs, i=bcp:
                              e_.tensor_copy(o[:, :], i[:, :]))
                        kb.op("vector", lambda e_, o=attnT, u=up[ub], b_=bcs, cc=cc:
                              e_.scalar_tensor_tensor(
                                  o[(cc % 2) * 64:(cc % 2) * 64 + 64,
                                    512 * (cc // 2):512 * (cc // 2) + 512],
                                  u[0:64, :], 1.0, b_[:, :],
                                  op0=AL.mult, op1=AL.mult), sem="la_att")
                        up_g[ub] = kb.sem("la_att").val
            attv = kb.sem("la_att").val
            kb.wait("tensor", "la_att", attv)
            for dc in range(8):
                hb = dc % 4
                kb.wait("tensor", "la_exp", scp_g[hb // 2])
                prs = [(wp[c][:, dc * 128:(dc + 1) * 128],
                        attnT[:, 512 * c:512 * c + 512]) for c in range(8)]
                mm_accum(kb, halves(hb), prs, sem="la_pj")
                pjv = kb.sem("la_pj").val
                kb.wait("vector", "la_pj", pjv)
                kb.op("vector", lambda e_, o=e[dc], i=halves(hb), nt=nt:
                      e_.tensor_tensor(o[:, nt * 512:(nt + 1) * 512], i,
                                       o[:, nt * 512:(nt + 1) * 512], op=AL.add),
                      sem="la_exp", nodrain=True)
                scp_g[hb // 2] = kb.sem("la_exp").val

        pev = kb.sem("la_exp").val
        kb.wait("vector", "la_exp", pev)
        for j in range(8):
            kb.op("vector", lambda e_, o=ebf[j], i=e[j]:
                  e_.tensor_copy(o[:, :], i[:, :]), sem="la_rbf", nodrain=True)
        rbv = kb.sem("la_rbf").val
        kb.wait("tensor", "la_rbf", rbv)
        h_guard = [0, 0]
        for nt in range(4):
            ub = nt % 2
            kb.wait("tensor", "la_att", up_g[ub])
            prs = [(w1[c][:, :], ebf[c][:, nt * 512:(nt + 1) * 512])
                   for c in range(8)]
            mm_accum(kb, up[ub][0:64, :], prs, sem="la_hm")
            hv = kb.sem("la_hm").val
            kb.wait("scalar", "la_hm", hv)
            kb.wait("scalar", "la_pj", h_guard[ub])
            kb.op("scalar", lambda e_, o=h2[ub], i=up[ub], b_=b1:
                  e_.activation(o[:, :], i[0:64, :], AF.Relu, bias=b_[:, :]),
                  sem="la_att", nodrain=True)
            up_g[ub] = kb.sem("la_att").val
            hrv = kb.sem("la_att").val
            kb.wait("tensor", "la_att", hrv)
            for dc in range(8):
                hb = dc % 4
                kb.wait("tensor", "la_exp", scp_g[hb // 2])
                kb.op("tensor", lambda e_, ps=halves(hb), dc=dc, ub=ub:
                      e_.matmul(ps, w2[:, dc * 128:(dc + 1) * 128], h2[ub][:, :],
                                start=True, stop=True), sem="la_pj")
                pjv = kb.sem("la_pj").val
                kb.wait("vector", "la_pj", pjv)
                kb.op("vector", lambda e_, o=e[dc], i=halves(hb), b_=b2[dc], nt=nt:
                      e_.scalar_tensor_tensor(o[:, nt * 512:(nt + 1) * 512],
                                              i, b_[:, :],
                                              o[:, nt * 512:(nt + 1) * 512],
                                              op0=AL.add, op1=AL.add), sem="la_exp", nodrain=True)
                scp_g[hb // 2] = kb.sem("la_exp").val
            h_guard[ub] = kb.sem("la_pj").val
        kb.op("vector", lambda e_: e_.engine_nop() if False else e_.nop(),
              sem="la_pe")

    pev = kb.sem("la_exp").val
    kb.wait("vector", "la_exp", pev)
    kb.wait("scalar", "la_exp", pev)
    st_rec = {}
    for dc in range(8):
        scr = kp[dc % 4]
        if dc >= 4:
            kb.wait("vector", "la_st", st_rec[dc - 4])
            kb.wait("scalar", "la_st", st_rec[dc - 4])
        kb.op("vector", lambda e_, o=ebf[dc], i=e[dc]:
              e_.tensor_copy(o[:, :], i[:, :]), sem="la_hi", nodrain=True)
        hv = kb.sem("la_hi").val
        kb.wait("scalar", "la_hi", hv)
        kb.wait("scalar", "la_lo")
        kb.op("scalar", lambda e_, o=sqt, i=ebf[dc]:
              e_.copy(o[:, :], i[:, :]), sem="la_hf")
        hfv = kb.sem("la_hf").val
        kb.wait("vector", "la_hf", hfv)
        kb.op("vector", lambda e_, o=scr, i=e[dc], hf=sqt:
              e_.tensor_tensor(o[:, :], i[:, :], hf[:, :],
                               op=AL.subtract), sem="la_lo", nodrain=True)
        lov = kb.sem("la_lo").val
        kb.wait("sync", "la_hi", hv)
        kb.dma("sync", e2hi_d[dc * 128:(dc + 1) * 128, :], ebf[dc][:, :], "la_st")
        kb.wait("sync", "la_lo", lov)
        kb.dma("sync", e2lo_d[dc * 128:(dc + 1) * 128, :], scr[:, :], "la_st")
        st_rec[dc] = kb.sem("la_st").val
    kb.phase_end()


def emit_u(kb, pend, vaug, expb2, up, last, u_done):
    mp, c0, c1, expv = pend
    par = mp % 2
    kb.wait("tensor", "la_exp", expv)
    for ci, cc in enumerate((c0, c1)):
        for sub in range(2):
            mt = 2 * mp + sub
            kb.op("tensor", lambda e_, ps=up[ci], mt=mt, cc=cc, ci=ci, sub=sub,
                  par=par,
                  st=(mp == 0 and sub == 0), sp=(last and sub == 1):
                  e_.matmul(ps[:, :], vaug[mt][:, 65 * cc:65 * cc + 65],
                            expb2[ci][par][:, sub * 512:(sub + 1) * 512],
                            start=st, stop=sp),
                  sem="la_u")
        u_done[(ci, par)] = kb.sem("la_u").val


def moe_ffn(kb):
    kb.sem("mf_hmm", producer="tensor")
    kb.sem("mf_ymm", producer="tensor")
    kb.fence("la_st")
    pairs_d = kb.d["pairs"].ap()
    xnf_d = kb.d["xnf"].ap()
    xin_d = kb.d["xin"].ap()
    yhi_d = kb.d["yhi"].ap()
    ylo_d = kb.d["ylo"].ap()
    acc2_d = kb.d["moeacc2"].ap()
    w1_d = kb.d["moe_w1"].ap()
    w2_d = kb.d["moe_w2"].ap()
    b1_d = kb.d["moe_b1"].ap()
    b2_d = kb.d["moe_b2"].ap()
    NG = CAP // 128

    kb.phase_begin()
    ids_i = [[kb.sbuf(f"mf_id{e}_{g}", [128, 1], I32) for g in range(NG)]
             for e in range(E_EXP)]
    gts = [[kb.sbuf(f"mf_gt{e}_{g}", [128, 1], F32) for g in range(NG)]
           for e in range(E_EXP)]
    prt = kb.sbuf("mf_prt", [128, 2], F32)
    xg = [kb.sbuf(f"mf_xg{i}", [128, 1024], F32) for i in range(2)]
    xgb = [kb.sbuf(f"mf_xgb{i}", [128, 1024], BF16) for i in range(2)]
    xinT = [kb.sbuf(f"mf_xt{c}", [128, CAP], BF16) for c in range(8)]
    w1s = [kb.sbuf(f"mf_w1{c}", [128, F_MOE], BF16) for c in range(8)]
    w2s = [kb.sbuf(f"mf_w2{c}", [128, 1024], BF16) for c in range(32)]
    hT = [kb.sbuf(f"mf_h{fc}", [128, CAP], BF16) for fc in range(32)]
    b1t = kb.sbuf("mf_b1", [128, 1], F32)
    b2t = [kb.sbuf(f"mf_b2{dc}", [128, 1], F32) for dc in range(8)]
    yf = kb.sbuf("mf_yf", [128, CAP], F32)
    zidx = kb.sbuf("mf_zidx", [128, 1], I32)
    sjunk = kb.sbuf("mf_sjunk", [128, 1], F32)
    yhib = kb.sbuf("mf_yhi", [128, CAP], BF16)
    yhif = kb.sbuf("mf_yhf", [128, CAP], F32)
    ylob = kb.sbuf("mf_ylo", [128, CAP], BF16)
    yth = kb.sbuf("mf_yth", [128, 1024], BF16)
    ytl = kb.sbuf("mf_ytl", [128, 1024], BF16)
    ytf = [kb.sbuf(f"mf_ytf{g}", [128, 1024], F32) for g in range(3)]
    zt = kb.sbuf("mf_zt", [128, 1024], F32)
    hp = [kb.psum(f"mf_hp{i}", [128, 512]) for i in range(2)]

    kb.op("vector", lambda e_, o=zidx: e_.memset(o[:, :], 0), sem="mf_zi")
    kb.wait("gpsimd", "mf_zi")
    kb.op("vector", lambda e_, o=zt: e_.memset(o[:, :], 0.0), sem="mf_z")
    zv = kb.sem("mf_z").val
    kb.wait("sync", "mf_z", zv)
    acc2_flat = acc2_d[0:S, :].rearrange("s w -> (s w)").rearrange(
        "(a p i) -> a p i", p=128, i=1024)
    for t in range(128):
        kb.dma("sync", acc2_flat[t], zt[:, :], "mf_zst")
    zstv = kb.sem("mf_zst").val

    kb.wait("gpsimd", "p0_st")
    kb.wait("gpsimd", "p0_zst")
    for ex in range(E_EXP):
        for g in range(NG):
            pv = kb.dma("sync", prt[:, :],
                        pairs_d[ex * CAP + g * 128: ex * CAP + (g + 1) * 128, :],
                        "mf_pld")
            kb.wait("vector", "mf_pld", pv)
            kb.op("vector", lambda e_, o=ids_i[ex][g], i=prt:
                  e_.tensor_copy(o[:, :], i[:, 0:1]))
            kb.op("vector", lambda e_, o=gts[ex][g], i=prt:
                  e_.tensor_copy(o[:, :], i[:, 1:2]), sem="mf_prc")
            kb.wait("sync", "mf_prc")
        prcv = kb.sem("mf_prc").val
        kb.wait("gpsimd", "mf_prc", prcv)
        for g in range(NG):
            bj = g % 2
            if g >= 2:
                kb.wait("gpsimd", "mf_xst")
                kb.wait("vector", "mf_xst")
            gv = kb.idma(xg[bj][:, :], None, xnf_d,
                         IndirectOffsetOnAxis(ap=ids_i[ex][g][:, :], axis=0),
                         "mf_gat")
            kb.wait("vector", "mf_gat", gv)
            kb.op("vector", lambda e_, o=xgb[bj], i=xg[bj]:
                  e_.tensor_copy(o[:, :], i[:, :]), sem="mf_cst")
            cv = kb.sem("mf_cst").val
            kb.wait("scalar", "mf_cst", cv)
            kb.dma("scalar", xin_d[ex, g * 128:(g + 1) * 128, :], xgb[bj][:, :],
                   "mf_xst")
        xsv = kb.sem("mf_xst").val
        kb.wait("sync", "mf_xst", xsv)
        kb.wait("sync", "mf_hmm")
        for c in range(8):
            kb.dmat("sync", xinT[c][:, :],
                    xin_d[ex].rearrange("s d -> s d")[:, c * 128:(c + 1) * 128],
                    "mf_tp")
        tpv = kb.sem("mf_tp").val
        kb.wait("gpsimd", "mf_hmm")
        for c in range(8):
            kb.dma("gpsimd", w1s[c][:, :], w1_d[ex, c * 128:(c + 1) * 128, :],
                   "mf_w1l")
        w1v = kb.sem("mf_w1l").val
        kb.wait("gpsimd", "mf_ymm")
        for fc in range(32):
            kb.dma("gpsimd", w2s[fc][:, :], w2_d[ex, fc * 128:(fc + 1) * 128, :],
                   "mf_w2l")
        w2v = kb.sem("mf_w2l").val
        kb.wait("tensor", "mf_w1l", w1v)
        kb.wait("tensor", "mf_tp", tpv)
        hl_guard = kb.sem("mf_ymm").val
        kb.wait("scalar", "mf_ymm", hl_guard)
        hp_guard = [kb.sem("mf_hev").val] * 2
        for fc in range(32):
            bnk = fc % 2
            kb.wait("tensor", "mf_hev", hp_guard[bnk])
            prs = [(w1s[c][:, fc * 128:(fc + 1) * 128], xinT[c][:, :])
                   for c in range(8)]
            mm_accum(kb, hp[bnk][:, 0:CAP], prs, sem="mf_hmm")
            hv = kb.sem("mf_hmm").val
            kb.wait("sync", "mf_hev")
            bv = kb.dma("sync", b1t[:, :],
                        b1_d[ex, fc * 128:(fc + 1) * 128].rearrange(
                            "(p one) -> p one", one=1), "mf_b1l")
            kb.wait("scalar", "mf_hmm", hv)
            kb.wait("scalar", "mf_b1l", bv)
            kb.op("scalar", lambda e_, o=hT[fc], i=hp[bnk], b_=b1t:
                  e_.activation(o[:, :], i[:, 0:CAP], AF.Gelu_apprx_tanh,
                                bias=b_[:, :]), sem="mf_hev")
            hp_guard[bnk] = kb.sem("mf_hev").val
        hevv = kb.sem("mf_hev").val
        kb.wait("tensor", "mf_w2l", w2v)
        kb.wait("tensor", "mf_hev", hevv)
        yp_guard = [kb.sem("mf_yev").val] * 2
        for dc in range(8):
            bnk = dc % 2
            kb.wait("tensor", "mf_yev", yp_guard[bnk])
            bv = kb.dma("sync", b2t[dc][:, :],
                        b2_d[ex, dc * 128:(dc + 1) * 128].rearrange(
                            "(p one) -> p one", one=1), "mf_b2l")
            prs = [(w2s[fc][:, dc * 128:(dc + 1) * 128], hT[fc][:, :])
                   for fc in range(32)]
            mm_accum(kb, hp[bnk][:, 0:CAP], prs, sem="mf_ymm")
            yv = kb.sem("mf_ymm").val
            kb.wait("vector", "mf_ymm", yv)
            kb.wait("vector", "mf_b2l", bv)
            kb.op("vector", lambda e_, o=yf, i=hp[bnk], b_=b2t[dc]:
                  e_.tensor_scalar(o[:, :], i[:, 0:CAP], b_[:, :], None,
                                   op0=AL.add), sem="mf_yev")
            kb.op("vector", lambda e_, o=yhib, i=yf:
                  e_.tensor_copy(o[:, :], i[:, :]), sem="mf_yev")
            yev = kb.sem("mf_yev").val
            yp_guard[bnk] = yev
            kb.wait("scalar", "mf_yev", yev)
            kb.op("scalar", lambda e_, o=yhif, i=yhib:
                  e_.copy(o[:, :], i[:, :]), sem="mf_yhf")
            hfv = kb.sem("mf_yhf").val
            kb.wait("vector", "mf_yhf", hfv)
            kb.op("vector", lambda e_, o=ylob, i0=yf, i1=yhif:
                  e_.tensor_tensor(o[:, :], i0[:, :], i1[:, :], op=AL.subtract),
                  sem="mf_ylo")
            lv = kb.sem("mf_ylo").val
            kb.dma("scalar", yhi_d[ex, dc * 128:(dc + 1) * 128, :], yhib[:, :],
                   "mf_yst")
            kb.wait("scalar", "mf_ylo", lv)
            kb.dma("scalar", ylo_d[ex, dc * 128:(dc + 1) * 128, :], ylob[:, :],
                   "mf_yst")
            ystv = kb.sem("mf_yst").val
            kb.wait("vector", "mf_yst", ystv)
        kb.wait("sync", "mf_yst")
        for g in range(NG):
            kb.wait("sync", "mf_yg")
            kb.dmat("sync", yth[:, :], yhi_d[ex][:, g * 128:(g + 1) * 128], "mf_yt")
            tv1 = kb.dmat("sync", ytl[:, :], ylo_d[ex][:, g * 128:(g + 1) * 128],
                          "mf_yt")
            kb.wait("vector", "mf_yt", tv1)
            kb.op("vector", lambda e_, o=ytf[g], i0=yth, i1=ytl:
                  e_.tensor_tensor(o[:, :], i0[:, :], i1[:, :], op=AL.add))
            kb.op("vector", lambda e_, o=ytf[g], s=gts[ex][g]:
                  e_.tensor_scalar(o[:, :], o[:, :], s[:, :], None, op0=AL.mult),
                  sem="mf_yg")
            ygv = kb.sem("mf_yg").val
            kb.wait("gpsimd", "mf_yg", ygv)
            if ex == 0 and g == 0:
                kb.wait("gpsimd", "mf_zst", zstv)
            kb.idma(acc2_d, IndirectOffsetOnAxis(ap=ids_i[ex][g][:, :], axis=0),
                    ytf[g][:, :], None, "mf_sca",
                    element_offset=ex * 1024)
        scav = kb.sem("mf_sca").val
        kb.wait("gpsimd", "mf_sca", scav)
        sent_d0 = kb.d["sentinel"].ap()
        sv0 = kb.idma(sjunk[:, :], None, sent_d0,
                      IndirectOffsetOnAxis(ap=zidx[:, :], axis=0), "mf_esent")
        kb.wait("gpsimd", "mf_esent", sv0)
        kb.wait("vector", "mf_sca", scav)
        kb.wait("vector", "mf_esent", sv0)
    sent_d = kb.d["sentinel"].ap()
    sv = kb.idma(sjunk[:, :], None, sent_d,
                 IndirectOffsetOnAxis(ap=zidx[:, :], axis=0), "mf_sent")
    for eng in ENGINES:
        kb.wait(eng, "mf_sent", sv)
    kb.phase_end()


def l0_combine(kb):
    x = kb.d["x"].ap()
    e2hi_d = kb.d["e2hi"].ap()
    e2lo_d = kb.d["e2lo"].ap()
    acc2_d = kb.d["moeacc2"].ap()
    x1_d = kb.d["x1"].ap()
    xn1_d = kb.d["xn1"].ap()
    epst = kb.shared["epst"]

    kb.fence("mf_yst", "mf_xst")
    kb.phase_begin()
    NB = 3
    xt = [kb.sbuf(f"lc_x{i}", [128, 1024], F32) for i in range(NB)]
    ac = [kb.sbuf(f"lc_a{i}", [128, 8192], F32) for i in range(NB)]
    th = [kb.sbuf(f"lc_th{i}", [128, 1024], BF16) for i in range(NB)]
    tl = [kb.sbuf(f"lc_tl{i}", [128, 1024], BF16) for i in range(NB)]
    asum = [kb.sbuf(f"lc_as{i}", [128, 1024], F32) for i in range(NB)]
    x1t = [kb.sbuf(f"lc_x1{i}", [128, 1024], F32) for i in range(NB)]
    xn1t = [kb.sbuf(f"lc_xn{i}", [128, 1024], BF16) for i in range(NB)]
    jk = kb.sbuf("lc_jk", [128, 1024], F32)
    s1 = [kb.sbuf(f"lc_s1{i}", [128, 1], F32) for i in range(NB)]
    s2 = [kb.sbuf(f"lc_s2{i}", [128, 1], F32) for i in range(NB)]
    mu = [kb.sbuf(f"lc_mu{i}", [128, 1], F32) for i in range(NB)]
    rst = [kb.sbuf(f"lc_rs{i}", [128, 1], F32) for i in range(NB)]
    tv = [kb.sbuf(f"lc_tv{i}", [128, 2], F32) for i in range(NB)]

    ld_v = {}
    x1_v = {}
    var_v = {}
    rstd_v = {}
    st_v = {}

    acs_v = {}

    def stage_load(T):
        b = T % NB
        if T >= NB:
            kb.wait("sync", "lc_st", st_v.get(T - NB, 0))
            kb.wait("scalar", "lc_st", st_v.get(T - NB, 0))
            kb.wait("sync", "lc_acs", acs_v.get(T - NB, 0))
        kb.dma("sync", xt[b][:, :], x[T * 128:(T + 1) * 128, :], "lc_ld")
        kb.dma("sync", ac[b][:, :], acc2_d[T * 128:(T + 1) * 128, :], "lc_ld")
        kb.dmat("scalar", th[b][:, :], e2hi_d[:, T * 128:(T + 1) * 128], "lc_tp")
        kb.dmat("scalar", tl[b][:, :], e2lo_d[:, T * 128:(T + 1) * 128], "lc_tp")
        ld_v[T] = (kb.sem("lc_ld").val, kb.sem("lc_tp").val)

    def stage_main(T):
        b = T % NB
        if T >= NB:
            kb.wait("vector", "lc_st", st_v.get(T - NB, 0))
            kb.wait("gpsimd", "lc_st", st_v.get(T - NB, 0))
        kb.wait("gpsimd", "lc_ld", ld_v[T][0])
        kb.op("gpsimd", lambda e, i=ac[b]:
              e.tensor_tensor(i[:, 0:4096], i[:, 0:4096], i[:, 4096:8192],
                              op=AL.add))
        kb.op("gpsimd", lambda e, i=ac[b]:
              e.tensor_tensor(i[:, 0:2048], i[:, 0:2048], i[:, 2048:4096],
                              op=AL.add))
        kb.op("gpsimd", lambda e, i=ac[b], o=asum[b]:
              e.tensor_tensor(o[:, :], i[:, 0:1024], i[:, 1024:2048],
                              op=AL.add), sem="lc_acs")
        acsv = kb.sem("lc_acs").val
        acs_v[T] = acsv
        kb.wait("vector", "lc_ld", ld_v[T][0])
        kb.wait("vector", "lc_tp", ld_v[T][1])
        kb.op("vector", lambda e, o=x1t[b], i0=th[b], i1=tl[b]:
              e.tensor_tensor(o[:, :], i0[:, :], i1[:, :], op=AL.add))
        kb.op("vector", lambda e, o=x1t[b], i=xt[b]:
              e.tensor_tensor(o[:, :], o[:, :], i[:, :], op=AL.add))
        kb.wait("vector", "lc_acs", acsv)
        kb.op("vector", lambda e, o=x1t[b], i=asum[b]:
              e.tensor_tensor(o[:, :], o[:, :], i[:, :], op=AL.add), sem="lc_x1")
        x1_v[T] = kb.sem("lc_x1").val
        kb.op("vector", lambda e, i=x1t[b], a=s1[b]:
              e.tensor_reduce(a[:, :], i[:, :], axis=mybir.AxisListType.X,
                              op=AL.add))
        kb.op("vector", lambda e, i=x1t[b], a=s2[b], j=jk:
              e.scalar_tensor_tensor(j[:, :], i[:, :], 1.0, i[:, :],
                                     op0=AL.mult, op1=AL.mult, accum_out=a[:, :]))
        kb.op("vector", lambda e, s=s1[b], m=mu[b]:
              e.tensor_scalar(m[:, :], s[:, :], 1.0 / D, None, op0=AL.mult))
        kb.op("vector", lambda e, s=s2[b], t=tv[b]:
              e.tensor_scalar(t[:, 0:1], s[:, :], 1.0 / D, None, op0=AL.mult))
        kb.op("vector", lambda e, m=mu[b], t=tv[b]:
              e.tensor_tensor(t[:, 1:2], m[:, :], m[:, :], op=AL.mult))
        kb.op("vector", lambda e, t=tv[b]:
              e.tensor_tensor(t[:, 0:1], t[:, 0:1], t[:, 1:2], op=AL.subtract),
              sem="lc_var")
        var_v[T] = kb.sem("lc_var").val
        kb.wait("scalar", "lc_var", var_v[T])
        kb.op("scalar", lambda e, t=tv[b], ep=epst:
              e.activation(t[:, 1:2], t[:, 0:1], AF.Ln, bias=ep[:, :]))
        kb.op("scalar", lambda e, t=tv[b], r=rst[b]:
              e.activation(r[:, :], t[:, 1:2], AF.Exp, scale=-0.5), sem="lc_rst")
        rstd_v[T] = kb.sem("lc_rst").val

    def stage_out(T):
        b = T % NB
        kb.wait("vector", "lc_rst", rstd_v[T])
        kb.op("vector", lambda e, i=x1t[b], m=mu[b], r=rst[b], o=xn1t[b]:
              e.tensor_scalar(o[:, :], i[:, :], m[:, :], r[:, :],
                              op0=AL.subtract, op1=AL.mult), sem="lc_xn")
        xnv = kb.sem("lc_xn").val
        kb.wait("sync", "lc_x1", x1_v[T])
        kb.dma("sync", x1_d[T * 128:(T + 1) * 128, :], x1t[b][:, :], "lc_st")
        kb.wait("sync", "lc_xn", xnv)
        kb.dma("sync", xn1_d[T * 128:(T + 1) * 128, :], xn1t[b][:, :], "lc_st")
        st_v[T] = kb.sem("lc_st").val

    stage_load(0)
    stage_load(1)
    for T in range(16):
        stage_main(T)
        if T >= 1:
            stage_out(T - 1)
        if T + 2 < 16:
            stage_load(T + 2)
    stage_out(15)
    kb.phase_end()


def l1_attn(kb):
    kb.sem("lb_gmm", producer="tensor")
    kb.sem("lb_rot", producer="tensor")
    kb.sem("lb_sq", producer="scalar")
    kb.fence("lc_st")
    xn1_d = kb.d["xn1"].ap()
    s1hi_d = kb.d["s1hi"].ap()
    s1lo_d = kb.d["s1lo"].ap()
    wq_d = kb.d["l1_wqT"].ap()
    wk_d = kb.d["l1_wkT"].ap()
    wv_d = kb.d["l1_wvT"].ap()
    wo_d = kb.d["l1_woT"].ap()
    cos_d = kb.d["ropecos"].ap()
    sin_d = kb.d["ropesin"].ap()
    rrot2_d = kb.d["rrot2"].ap()
    cmask_d = kb.d["cmask"].ap()
    pad64_d = kb.d["pad64"].ap()
    selh2_d = kb.d["selh2"].ap()
    SC = 0.125

    kb.phase_begin()
    xn1T = [kb.sbuf(f"lb_xt{j}", [128, 2048], BF16) for j in range(8)]
    qro = [kb.sbuf(f"lb_q{j}", [128, 2048], BF16) for j in range(8)]
    kro = [kb.sbuf(f"lb_k{j}", [128, 2048], BF16) for j in range(8)]
    vaug = [kb.sbuf(f"lb_va{m}", [128, 1040], BF16) for m in range(16)]
    expb = [kb.sbuf(f"lb_ex{i}", [128, 512], BF16) for i in range(2)]
    attnT = kb.sbuf("lb_attnT", [128, 4096], BF16)
    wst = [kb.sbuf(f"lb_w{i}", [128, 1024], BF16) for i in range(2)]
    wvs = [kb.sbuf(f"lb_wv{c}", [128, 512], BF16) for c in range(8)]
    wo = [kb.sbuf(f"lb_wo{c}", [128, 1024], BF16) for c in range(8)]
    cosb = kb.sbuf("lb_cos", [128, 2048], F32)
    sinb = kb.sbuf("lb_sin", [128, 2048], F32)
    rrot2 = kb.sbuf("lb_rr", [128, 128], BF16)
    cmask = [kb.sbuf(f"lb_cm{i}", [128, 512], BF16) for i in range(4)]
    pad64 = kb.sbuf("lb_p64", [128, 2], BF16)
    selh2 = kb.sbuf("lb_sh2", [2, 256], F32)
    qtmp = kb.sbuf("lb_qt", [128, 512], BF16)
    r1 = kb.sbuf("lb_r1", [128, 512], F32)
    r2 = kb.sbuf("lb_r2", [128, 512], F32)
    sqt = kb.sbuf("lb_sqt", [128, 512], BF16)
    qmx = [kb.sbuf(f"lb_qmx{j}", [2, 1], F32) for j in range(8)]
    kmx = [kb.sbuf(f"lb_kmx{j}", [2, 1], F32) for j in range(8)]
    t2c = kb.sbuf("lb_t2", [2, 2], F32)
    negm2 = [kb.sbuf(f"lb_nm{j}", [2, 1], F32) for j in range(8)]
    nmax = [kb.sbuf(f"lb_nmax{i}", [128, 1], F32) for i in range(2)]
    sig = kb.sbuf("lb_sig", [1, 512], F32)
    bcs = kb.sbuf("lb_bcs", [64, 512], F32)
    onesf64 = kb.sbuf("lb_ones64", [1, 64], F32)
    sh = kb.sbuf("lb_sh", [128, 512], BF16)
    sl = kb.sbuf("lb_sl", [128, 512], BF16)
    shf = kb.sbuf("lb_shf", [128, 512], F32)
    scp = [kb.psum(f"lb_scp{i}", [128, 512]) for i in range(2)]
    up = [kb.psum(f"lb_up{i}", [65, 512]) for i in range(2)]
    bcp = kb.psum("lb_bcp", [64, 512])
    gp = [kb.psum(f"lb_gp{i}", [128, 512]) for i in range(2)]
    npsh = kb.psum("lb_npsh", [128, 512])
    np2 = npsh[0:2, 0:512]
    nmp = npsh[0:128, 0:1]

    for dst, srcd in ((cosb, cos_d), (sinb, sin_d), (rrot2, rrot2_d),
                      (pad64, pad64_d), (selh2, selh2_d)):
        kb.dma("sync", dst[:, :], srcd[:, :], "lb_cld")
    for i in range(4):
        kb.dma("sync", cmask[i][:, :], cmask_d[i], "lb_cld")
    cldv = kb.sem("lb_cld").val
    for c in range(8):
        kb.dma("gpsimd", wo[c][:, :], wo_d[c * 128:(c + 1) * 128, :], "lb_wld")
    wldv = kb.sem("lb_wld").val
    kb.op("vector", lambda e_, o=onesf64: e_.memset(o[:, :], 1.0), sem="lb_init")
    initv = kb.sem("lb_init").val
    kb.wait("scalar", "lc_st")
    for j in range(8):
        kb.dmat("scalar", xn1T[j][:, :], xn1_d[:, j * 128:(j + 1) * 128], "lb_tp")
    tpv = kb.sem("lb_tp").val
    for eng in ("tensor",):
        kb.wait(eng, "lb_cld", cldv)
        kb.wait(eng, "lb_wld", wldv)
        kb.wait(eng, "lb_init", initv)
        kb.wait(eng, "lb_tp", tpv)
    kb.wait("vector", "lb_cld", cldv)
    kb.wait("scalar", "lb_cld", cldv)

    gp_guard = [0, 0]
    scp_guard = [0, 0]
    up_guard = [0, 0]
    qt_rot = 0
    qt_sq = 0

    for qk in range(2):
        w_d = wq_d if qk == 0 else wk_d
        dstt = qro if qk == 0 else kro
        mxt = qmx if qk == 0 else kmx
        for j in range(8):
            wb = wst[j % 2]
            kb.wait("gpsimd", "lb_gmm")
            for c in range(8):
                kb.dma("gpsimd", wb[:, c * 128:(c + 1) * 128],
                       w_d[c * 128:(c + 1) * 128, j * 128:(j + 1) * 128], "lb_wql")
            wqlv = kb.sem("lb_wql").val
            kb.wait("tensor", "lb_wql", wqlv)
            for nt in range(4):
                bnk = nt % 2
                kb.wait("tensor", "lb_pe", gp_guard[bnk])
                prs = [(wb[:, c * 128:(c + 1) * 128],
                        xn1T[c][:, nt * 512:(nt + 1) * 512]) for c in range(8)]
                mm_accum(kb, gp[bnk][:, :], prs, sem="lb_gmm")
                gv = kb.sem("lb_gmm").val
                kb.wait("vector", "lb_gmm", gv)
                kb.wait("vector", "lb_rot", qt_rot)
                kb.wait("vector", "lb_sq", qt_sq)
                kb.op("vector", lambda e_, o=qtmp, i=gp[bnk]:
                      e_.tensor_copy(o[:, :], i[:, :]), sem="lb_pe")
                gp_guard[bnk] = kb.sem("lb_pe").val
                qev = gp_guard[bnk]
                kb.wait("tensor", "lb_pe", qev)
                kb.op("tensor", lambda e_, ps=scp[0], lh=rrot2, rh=qtmp:
                      e_.matmul(ps[:, :], lh[:, :], rh[:, :], start=True, stop=True),
                      sem="lb_rot")
                rv = kb.sem("lb_rot").val
                qt_rot = rv
                kb.wait("scalar", "lb_pe", qev)
                kb.op("scalar", lambda e_, o=sqt, i=qtmp:
                      e_.activation(o[:, :], i[:, :], AF.Square), sem="lb_sq")
                sqv = kb.sem("lb_sq").val
                qt_sq = sqv
                kb.wait("tensor", "lb_sq", sqv)
                kb.op("tensor", lambda e_, ps=np2, lh=pad64, rh=sqt:
                      e_.matmul(ps, lh[:, :], rh[:, :], start=True, stop=True),
                      sem="lb_nmm")
                nv = kb.sem("lb_nmm").val
                kb.wait("vector", "lb_nmm", nv)
                kb.op("vector", lambda e_, o=t2c, i=np2:
                      e_.tensor_reduce(o[0:2, 0:1], i, axis=mybir.AxisListType.X,
                                       op=AL.max))
                if nt == 0:
                    kb.op("vector", lambda e_, o=mxt[j], i=t2c:
                          e_.tensor_copy(o[:, :], i[0:2, 0:1]), sem="lb_nred")
                else:
                    kb.op("vector", lambda e_, o=mxt[j], i=t2c:
                          e_.tensor_tensor(o[:, :], o[:, :], i[0:2, 0:1], op=AL.max),
                          sem="lb_nred")
                kb.wait("scalar", "lb_nred")
                kb.wait("vector", "lb_rot", rv)
                kb.op("vector", lambda e_, o=r1, i0=qtmp, i1=cosb, nt=nt:
                      e_.tensor_tensor(o[:, :], i0[:, :],
                                       i1[:, nt * 512:(nt + 1) * 512], op=AL.mult))
                kb.op("vector", lambda e_, o=r2, i0=scp[0], i1=sinb, nt=nt:
                      e_.tensor_tensor(o[:, :], i0[:, :],
                                       i1[:, nt * 512:(nt + 1) * 512], op=AL.mult))
                kb.op("vector", lambda e_, o=dstt[j], i0=r1, i1=r2, nt=nt:
                      e_.tensor_tensor(o[:, nt * 512:(nt + 1) * 512], i0[:, :],
                                       i1[:, :], op=AL.add), sem="lb_rope")
                kb.wait("tensor", "lb_rope")
    ropev = kb.sem("lb_rope").val
    kb.wait("tensor", "lb_rope", ropev)

    for half in range(2):
        kb.wait("gpsimd", "lb_vmm")
        for c in range(8):
            kb.dma("gpsimd", wvs[c][:, :],
                   wv_d[c * 128:(c + 1) * 128, half * 512:(half + 1) * 512],
                   "lb_wvl")
        wvv = kb.sem("lb_wvl").val
        kb.wait("tensor", "lb_wvl", wvv)
        for mt in range(16):
            bnk = mt % 2
            kb.wait("tensor", "lb_vev", scp_guard[bnk])
            prs = [(xn1T[c][:, mt * 128:(mt + 1) * 128], wvs[c][:, :])
                   for c in range(8)]
            mm_accum(kb, scp[bnk][:, :], prs, sem="lb_vmm")
            vv = kb.sem("lb_vmm").val
            kb.wait("vector", "lb_vmm", vv)
            dst = vaug[mt].rearrange("p (hh cc) -> p hh cc", cc=65)
            kb.op("vector", lambda e_, d=dst, i=scp[bnk], half=half:
                  e_.tensor_copy(
                      d[:, half * 8:(half + 1) * 8, 0:64],
                      i[:, :].rearrange("p (hh cc) -> p hh cc", cc=64)),
                  sem="lb_vev")
            scp_guard[bnk] = kb.sem("lb_vev").val
    for mt in range(16):
        kb.op("vector", lambda e_, o=vaug[mt]:
              e_.memset(o[:, 64:1040:65], 1.0), sem="lb_vev")
    vev_final = kb.sem("lb_vev").val
    kb.wait("tensor", "lb_vev", vev_final)
    scp_guard = [0, 0]

    kb.wait("vector", "lb_vev")
    for j in range(8):
        kb.op("vector", lambda e_, o=t2c, a=kmx[j], b=qmx[j]:
              e_.tensor_tensor(o[0:2, 1:2], a[:, :], b[:, :], op=AL.mult),
              sem="lb_nm1")
        v1 = kb.sem("lb_nm1").val
        kb.wait("scalar", "lb_nm1", v1)
        kb.op("scalar", lambda e_, o=t2c:
              e_.activation(o[0:2, 0:1], o[0:2, 1:2], AF.Sqrt), sem="lb_nm2")
        v2 = kb.sem("lb_nm2").val
        kb.wait("vector", "lb_nm2", v2)
        kb.op("vector", lambda e_, o=negm2[j], i=t2c:
              e_.tensor_scalar(o[:, :], i[0:2, 0:1], -SC, None, op0=AL.mult),
              sem="lb_nm3")
        kb.wait("scalar", "lb_nm3")

    kb.wait("tensor", "lb_rope", ropev)
    kb.wait("tensor", "lb_nm3")
    for nt in range(4):
        kb.wait("vector", "lb_pj")
        for h in range(16):
            j = h // 2
            h2 = h % 2
            rb = 64 * h2
            kb.op("tensor", lambda e_, ps=nmp, lh=selh2, rh=negm2[j], h2=h2:
                  e_.matmul(ps, lh[:, h2 * 128:(h2 + 1) * 128], rh[:, :],
                            start=True, stop=True), sem="lb_nbc")
            nbv = kb.sem("lb_nbc").val
            kb.wait("vector", "lb_nbc", nbv)
            kb.op("vector", lambda e_, o=nmax[h % 2], i=nmp:
                  e_.tensor_copy(o[:, :], i), sem="lb_nbe")
            nbe = kb.sem("lb_nbe").val
            kb.wait("scalar", "lb_nbe", nbe)
            kb.wait("tensor", "lb_nbe", nbe)
            ub = h % 2
            kb.wait("tensor", "lb_att", up_guard[ub])
            mts = list(range(4 * nt + 4))
            exp_slots = {}
            for mi, mt in enumerate(mts):
                bnk = mt % 2
                kb.wait("tensor", "lb_exp", scp_guard[bnk])
                kb.op("tensor", lambda e_, ps=scp[bnk], j=j, rb=rb, mt=mt, nt=nt:
                      e_.matmul(ps[:, :],
                                kro[j][rb:rb + 64, mt * 128:(mt + 1) * 128],
                                qro[j][rb:rb + 64, nt * 512:(nt + 1) * 512],
                                start=True, stop=True,
                                tile_position=(rb, 0)), sem="lb_sc")
                scv = kb.sem("lb_sc").val
                slot = mt % 2
                kb.wait("scalar", "lb_sc", scv)
                if slot in exp_slots:
                    kb.wait("scalar", "lb_u", exp_slots[slot][0])
                    if exp_slots[slot][1] is not None:
                        kb.wait("scalar", "lb_mk", exp_slots[slot][1])
                kb.op("scalar", lambda e_, o=expb[slot], i=scp[bnk],
                      bi=nmax[h % 2]:
                      e_.activation(o[:, :], i[:, :], AF.Exp,
                                    bias=bi[:, :], scale=SC), sem="lb_exp")
                scp_guard[bnk] = kb.sem("lb_exp").val
                ev = scp_guard[bnk]
                diag = mt >= 4 * nt
                mkv = None
                if diag:
                    kb.wait("vector", "lb_exp", ev)
                    kb.op("vector", lambda e_, o=expb[slot], m=cmask[mt - 4 * nt]:
                          e_.tensor_tensor(o[:, :], o[:, :], m[:, :], op=AL.mult),
                          sem="lb_mk")
                    mkv = kb.sem("lb_mk").val
                    kb.wait("tensor", "lb_mk", mkv)
                else:
                    kb.wait("tensor", "lb_exp", ev)
                kb.op("tensor", lambda e_, ps=up[ub], mt=mt, h=h,
                      st=(mi == 0), sp=(mi == len(mts) - 1):
                      e_.matmul(ps[:, :], vaug[mt][:, 65 * h:65 * h + 65],
                                expb[mt % 2][:, :], start=st, stop=sp),
                      sem="lb_u")
                exp_slots[slot] = (kb.sem("lb_u").val, mkv)
            uv = kb.sem("lb_u").val
            kb.wait("vector", "lb_u", uv)
            kb.op("vector", lambda e_, o=sig, i=up[ub]:
                  e_.reciprocal(o[:, :], i[64:65, :]), sem="lb_rc")
            rcv = kb.sem("lb_rc").val
            kb.wait("tensor", "lb_rc", rcv)
            kb.op("tensor", lambda e_, ps=bcp, lh=onesf64, rh=sig:
                  e_.matmul(ps[:, :], lh[:, :], rh[:, :], start=True, stop=True),
                  sem="lb_bc")
            bcv = kb.sem("lb_bc").val
            kb.wait("vector", "lb_bc", bcv)
            kb.op("vector", lambda e_, o=bcs, i=bcp:
                  e_.tensor_copy(o[:, :], i[:, :]))
            kb.op("vector", lambda e_, o=attnT, u=up[ub], b_=bcs, h=h:
                  e_.scalar_tensor_tensor(
                      o[(h % 2) * 64:(h % 2) * 64 + 64,
                        512 * (h // 2):512 * (h // 2) + 512],
                      u[0:64, :], 1.0, b_[:, :],
                      op0=AL.mult, op1=AL.mult), sem="lb_att")
            up_guard[ub] = kb.sem("lb_att").val
        attv = kb.sem("lb_att").val
        kb.wait("tensor", "lb_att", attv)
        for dc in range(8):
            bnk = dc % 2
            kb.wait("tensor", "lb_pe", gp_guard[bnk])
            prs = [(wo[c][:, dc * 128:(dc + 1) * 128],
                    attnT[:, 512 * c:512 * c + 512]) for c in range(8)]
            mm_accum(kb, gp[bnk][:, :], prs, sem="lb_pj")
            pjv = kb.sem("lb_pj").val
            kb.wait("vector", "lb_pj", pjv)
            kb.wait("vector", "lb_sst")
            kb.op("vector", lambda e_, o=sh, i=gp[bnk]:
                  e_.tensor_copy(o[:, :], i[:, :]), sem="lb_pe")
            hv = kb.sem("lb_pe").val
            kb.wait("scalar", "lb_pe", hv)
            kb.op("scalar", lambda e_, o=shf, i=sh:
                  e_.copy(o[:, :], i[:, :]), sem="lb_shf")
            hfv = kb.sem("lb_shf").val
            kb.wait("vector", "lb_shf", hfv)
            kb.op("vector", lambda e_, o=sl, i0=gp[bnk], i1=shf:
                  e_.tensor_tensor(o[:, :], i0[:, :], i1[:, :], op=AL.subtract),
                  sem="lb_pe")
            gp_guard[bnk] = kb.sem("lb_pe").val
            lv = gp_guard[bnk]
            kb.wait("sync", "lb_pe", lv)
            kb.dma("sync", s1hi_d[dc * 128:(dc + 1) * 128,
                                  nt * 512:(nt + 1) * 512], sh[:, :], "lb_sst")
            kb.dma("sync", s1lo_d[dc * 128:(dc + 1) * 128,
                                  nt * 512:(nt + 1) * 512], sl[:, :], "lb_sst")
    kb.phase_end()


def l1_ffn(kb):
    xn1_d = kb.d["xn1"].ap()
    f1hi_d = kb.d["f1hi"].ap()
    f1lo_d = kb.d["f1lo"].ap()
    wg_d = kb.d["l1_wgT"].ap()
    wu_d = kb.d["l1_wuT"].ap()
    wd_d = kb.d["l1_wdT"].ap()
    KDQ = 1.0 / (127.0 * 127.0)

    kb.fence("lb_sst")
    kb.phase_begin()
    xn1T = [kb.sbuf(f"lf_xt{j}", [128, 2048], BF16) for j in range(8)]
    wg = [kb.sbuf(f"lf_wg{c}", [128, 1024], BF16) for c in range(8)]
    wu = [kb.sbuf(f"lf_wu{c}", [128, 1024], BF16) for c in range(8)]
    wd = [kb.sbuf(f"lf_wd{c}", [128, 1024], BF16) for c in range(8)]
    h8 = [kb.sbuf(f"lf_h{i}", [128, 512], BF16) for i in range(8)]
    gs = kb.sbuf("lf_gs", [128, 512], BF16)
    fsb = [kb.sbuf(f"lf_f{dc}", [128, 2048], F32) for dc in range(8)]
    sh = kb.sbuf("lf_sh", [128, 2048], BF16)
    shf = kb.sbuf("lf_shf", [128, 512], F32)
    sl = kb.sbuf("lf_sl", [128, 2048], BF16)
    gp = [kb.psum(f"lf_gp{i}", [128, 512]) for i in range(2)]
    yp = [kb.psum(f"lf_yp{i}", [128, 512]) for i in range(2)]

    kb.wait("scalar", "lc_st")
    for j in range(8):
        kb.dmat("scalar", xn1T[j][:, :], xn1_d[:, j * 128:(j + 1) * 128], "lf_tp")
    tpv = kb.sem("lf_tp").val
    kb.wait("tensor", "lf_tp", tpv)
    for dc in range(8):
        kb.op("vector", lambda e_, o=fsb[dc]: e_.memset(o[:, :], 0.0), sem="lf_fz")
    fzv = kb.sem("lf_fz").val
    kb.wait("vector", "lf_fz", fzv)

    gp_guard = [0, 0]
    yph_guard = [0, 0]
    ypf_guard = [0, 0]
    h8_guard = 0
    for hb in range(3):
        kb.wait("gpsimd", "lf_gmm")
        kb.wait("gpsimd", "lf_ymm")
        for c in range(8):
            kb.dma("gpsimd", wg[c][:, :],
                   wg_d[c * 128:(c + 1) * 128, hb * 1024:(hb + 1) * 1024], "lf_wl")
            kb.dma("gpsimd", wu[c][:, :],
                   wu_d[c * 128:(c + 1) * 128, hb * 1024:(hb + 1) * 1024], "lf_wl")
            kb.dma("gpsimd", wd[c][:, :],
                   wd_d[hb * 1024 + c * 128: hb * 1024 + (c + 1) * 128, :], "lf_wl")
        wlv = kb.sem("lf_wl").val
        kb.wait("tensor", "lf_wl", wlv)
        for nt in range(4):
            for hc in range(8):
                bnk = hc % 2
                kb.wait("tensor", "lf_hev", gp_guard[bnk])
                prs = [(wg[c][:, hc * 128:(hc + 1) * 128],
                        xn1T[c][:, nt * 512:(nt + 1) * 512]) for c in range(8)]
                mm_accum(kb, gp[bnk][:, :], prs, sem="lf_gmm")
                kb.wait("tensor", "lf_hev", yph_guard[bnk])
                kb.wait("tensor", "lf_fadd", ypf_guard[bnk])
                prs = [(wu[c][:, hc * 128:(hc + 1) * 128],
                        xn1T[c][:, nt * 512:(nt + 1) * 512]) for c in range(8)]
                mm_accum(kb, yp[bnk][:, :], prs, sem="lf_gmm")
                gv = kb.sem("lf_gmm").val
                kb.wait("scalar", "lf_gmm", gv)
                kb.op("scalar", lambda e_, o=gs, i=gp[bnk]:
                      e_.activation(o[:, :], i[:, :], AF.Silu, scale=1.0 / 127.0),
                      sem="lf_sil")
                sv = kb.sem("lf_sil").val
                kb.wait("vector", "lf_sil", sv)
                kb.wait("vector", "lf_ymm", h8_guard)
                kb.op("vector", lambda e_, o=h8[hc], i0=yp[bnk], g_=gs:
                      e_.scalar_tensor_tensor(o[:, :], i0[:, :], KDQ, g_[:, :],
                                              op0=AL.mult, op1=AL.mult),
                      sem="lf_hev")
                gp_guard[bnk] = kb.sem("lf_hev").val
                yph_guard[bnk] = gp_guard[bnk]
            hev = kb.sem("lf_hev").val
            kb.wait("tensor", "lf_hev", hev)
            for dc in range(8):
                bnk = dc % 2
                kb.wait("tensor", "lf_fadd", ypf_guard[bnk])
                prs = [(wd[hc][:, dc * 128:(dc + 1) * 128], h8[hc][:, :])
                       for hc in range(8)]
                mm_accum(kb, yp[bnk][:, :], prs, sem="lf_ymm")
                yv = kb.sem("lf_ymm").val
                kb.wait("vector", "lf_ymm", yv)
                kb.op("vector", lambda e_, o=fsb[dc], i=yp[bnk], nt=nt:
                      e_.tensor_tensor(o[:, nt * 512:(nt + 1) * 512], i[:, :],
                                       o[:, nt * 512:(nt + 1) * 512], op=AL.add),
                      sem="lf_fadd")
                ypf_guard[bnk] = kb.sem("lf_fadd").val
            h8_guard = kb.sem("lf_ymm").val
    fav = kb.sem("lf_fadd").val
    kb.wait("vector", "lf_fadd", fav)
    kb.wait("scalar", "lf_fadd", fav)
    st_rec = {}
    for dc in range(8):
        kb.op("vector", lambda e_, o=sh, i=fsb[dc]:
              e_.tensor_copy(o[:, :], i[:, :]), sem="lf_hi")
        hv = kb.sem("lf_hi").val
        kb.wait("scalar", "lf_hi", hv)
        for n in range(4):
            kb.op("scalar", lambda e_, o=shf, i=sh, n=n:
                  e_.copy(o[:, :], i[:, n * 512:(n + 1) * 512]), sem="lf_hf")
            hfv = kb.sem("lf_hf").val
            kb.wait("vector", "lf_hf", hfv)
            kb.op("vector", lambda e_, o=sl, i=fsb[dc], hf=shf, n=n:
                  e_.tensor_tensor(o[:, n * 512:(n + 1) * 512],
                                   i[:, n * 512:(n + 1) * 512], hf[:, :],
                                   op=AL.subtract), sem="lf_lo")
            lv = kb.sem("lf_lo").val
            kb.wait("scalar", "lf_lo", lv)
        lov = kb.sem("lf_lo").val
        kb.wait("sync", "lf_hi", hv)
        kb.dma("sync", f1hi_d[dc * 128:(dc + 1) * 128, :], sh[:, :], "lf_st")
        kb.wait("sync", "lf_lo", lov)
        kb.dma("sync", f1lo_d[dc * 128:(dc + 1) * 128, :], sl[:, :], "lf_st")
        stv = kb.sem("lf_st").val
        kb.wait("vector", "lf_st", stv)
    kb.phase_end()


def final_out(kb):
    x1_d = kb.d["x1"].ap()
    s1hi_d = kb.d["s1hi"].ap()
    s1lo_d = kb.d["s1lo"].ap()
    f1hi_d = kb.d["f1hi"].ap()
    f1lo_d = kb.d["f1lo"].ap()
    out_d = kb.d["out"].ap()

    kb.fence("lf_st")
    kb.phase_begin()
    NB = 2
    xt = [kb.sbuf(f"fo_x{i}", [128, 1024], F32) for i in range(NB)]
    th = [kb.sbuf(f"fo_th{i}", [128, 1024], BF16) for i in range(NB)]
    tl = [kb.sbuf(f"fo_tl{i}", [128, 1024], BF16) for i in range(NB)]
    fh = [kb.sbuf(f"fo_fh{i}", [128, 1024], BF16) for i in range(NB)]
    fl = [kb.sbuf(f"fo_fl{i}", [128, 1024], BF16) for i in range(NB)]
    ot = [kb.sbuf(f"fo_o{i}", [128, 1024], F32) for i in range(NB)]
    kb.wait("sync", "lc_st")
    kb.wait("scalar", "lb_sst")
    kb.wait("scalar", "lf_st")
    st_after = []
    for t in range(16):
        b = t % NB
        if t >= NB:
            for eng in ("sync", "scalar", "vector"):
                kb.wait(eng, "fo_st", st_after[t - NB])
        ldv0 = kb.dma("sync", xt[b][:, :], x1_d[t * 128:(t + 1) * 128, :], "fo_ld")
        kb.dmat("scalar", th[b][:, :], s1hi_d[:, t * 128:(t + 1) * 128], "fo_tp")
        kb.dmat("scalar", tl[b][:, :], s1lo_d[:, t * 128:(t + 1) * 128], "fo_tp")
        kb.dmat("scalar", fh[b][:, :], f1hi_d[:, t * 128:(t + 1) * 128], "fo_tp")
        ldv = kb.dmat("scalar", fl[b][:, :], f1lo_d[:, t * 128:(t + 1) * 128],
                      "fo_tp")
        kb.wait("vector", "fo_ld", ldv0)
        kb.wait("vector", "fo_tp", ldv)
        kb.op("vector", lambda e_, o=ot[b], i0=th[b], i1=tl[b]:
              e_.tensor_tensor(o[:, :], i0[:, :], i1[:, :], op=AL.add))
        kb.op("vector", lambda e_, o=ot[b], i=xt[b]:
              e_.tensor_tensor(o[:, :], o[:, :], i[:, :], op=AL.add))
        kb.op("vector", lambda e_, o=ot[b], i0=fh[b], i1=fl[b], x_=xt[b]:
              e_.tensor_tensor(x_[:, :], i0[:, :], i1[:, :], op=AL.add))
        kb.op("vector", lambda e_, o=ot[b], x_=xt[b]:
              e_.tensor_tensor(o[:, :], o[:, :], x_[:, :], op=AL.add), sem="fo_s")
        sv = kb.sem("fo_s").val
        kb.wait("sync", "fo_s", sv)
        kb.dma("sync", out_d[t * 128:(t + 1) * 128, :], ot[b][:, :], "fo_st")
        st_after.append(kb.sem("fo_st").val)
    kb.phase_end()


ALL_PHASES = [p0_prep, routing_logits, routing_select, l0_attn, moe_ffn,
              l0_combine, l1_attn, l1_ffn, final_out]


def kernel(**inputs):
    kb = build_program(ALL_PHASES)
    in_maps = [prep_host_inputs(inputs, c) for c in range(8)]
    res = run_bass_kernel_spmd(kb.nc, in_maps, core_ids=list(range(8)))
    out = np.stack([np.asarray(res.results[c]["out"]) for c in range(8)], axis=0)
    return out.astype(np.float32)
